# version 11
# baseline (speedup 1.0000x reference)
import numpy as np
import ml_dtypes
import concourse.bass as bass
import concourse.mybir as mybir
from concourse.bass_utils import run_bass_kernel_spmd
from contextlib import ExitStack

F32 = mybir.dt.float32
BF16 = mybir.dt.bfloat16
AX = mybir.AxisListType.X
ALU = mybir.AluOpType
AF = mybir.ActivationFunctionType

T = 2048
C = 1024
NCT = 8
NH = 16
NTB = 16
NTS = 4
NE = 8
NHT = 22
HIDP = 2816
EPS = 1e-5
SCALE = 0.125

ENG = ("sync", "gpsimd", "tensor", "vector", "scalar")


class Sched:
    def __init__(self):
        self.prog = {e: [] for e in ENG}
        self.cnt = {}
        self.bank = {}
        self.slot = {}

    def op(self, eng, meth, *a, inc=None, incn=1, **k):
        m = None
        if inc is not None:
            self.cnt[inc] = self.cnt.get(inc, 0) + incn
            m = self.cnt[inc]
        self.prog[eng].append((meth, a, k, (inc, incn) if inc else None))
        return m

    def dma(self, eng, out, in_, inc, **k):
        return self.op(eng, "dma_start", inc=inc, incn=16, out=out, in_=in_, **k)

    def wait(self, eng, sem, val):
        if val:
            self.prog[eng].append(("wait_ge", (sem, val), {}, None))

    def edge(self, eng, sem):
        last = self.prog[eng][-1]
        if last[0] == "wait_ge":
            return self.cnt.get(sem, 0)
        if last[3] is not None:
            s2 = last[3][0]
            m = self.cnt[s2]
            self.prog[eng].append(("wait_ge", (s2, m), {}, None))
            return m
        self.cnt[sem] = self.cnt.get(sem, 0) + 1
        m = self.cnt[sem]
        self.prog[eng][-1] = (last[0], last[1], last[2], (sem, 1))
        self.prog[eng].append(("wait_ge", (sem, m), {}, None))
        return m

    def bank_wait(self, eng, b):
        if b in self.bank:
            self.wait(eng, *self.bank[b])

    def bank_free(self, b, sem, val):
        self.bank[b] = (sem, val)

    def slot_wait(self, eng, key):
        if key in self.slot:
            self.wait(eng, *self.slot[key])

    def slot_set(self, key, sem, val):
        self.slot[key] = (sem, val)


def _host_consts():
    theta = 10000.0
    inv = 1.0 / theta ** (np.arange(0, 32, 2, dtype=np.float64) / 32.0)
    tt = np.arange(T, dtype=np.float64)[:, None] * inv[None, :]
    emb = np.concatenate([tt, tt], axis=1)
    cos = np.zeros((128, T), dtype=np.float32)
    sin = np.zeros((128, T), dtype=np.float32)
    for base in (0, 64):
        cos[base:base + 32, :] = 1.0
        cos[base + 32:base + 64, :] = np.cos(emb).T.astype(np.float32)
        sin[base + 32:base + 64, :] = np.sin(emb).T.astype(np.float32)
    pm = np.zeros((128, 128), dtype=np.float32)
    for base in (32, 96):
        for i in range(16):
            pm[base + 2 * i + 1, base + 2 * i] = -1.0
            pm[base + 2 * i, base + 2 * i + 1] = 1.0
    return cos, sin, pm


def build():
    nc = bass.Bass("TRN2", target_bir_lowering=False)
    S = Sched()

    d = {}
    d["x"] = nc.dram_tensor("x", [T, C], F32, kind="ExternalInput").ap()
    d["xT"] = nc.dram_tensor("xT", [C, T], F32, kind="ExternalInput").ap()
    for n in ("wq", "wk", "wv", "wc"):
        d[n] = nc.dram_tensor(n, [C, C], F32, kind="ExternalInput").ap()
    for n in ("wupqc", "wupqe", "wupkc", "wupke"):
        d[n] = nc.dram_tensor(n, [64, 32], F32, kind="ExternalInput").ap()
    d["wupv"] = nc.dram_tensor("wupv", [64, 64], F32, kind="ExternalInput").ap()
    d["wgT"] = nc.dram_tensor("wgT", [C, NE], F32, kind="ExternalInput").ap()
    d["w12i"] = nc.dram_tensor("w12i", [NE, C, 2 * NHT, 128], F32,
                               kind="ExternalInput").ap()
    d["w3p"] = nc.dram_tensor("w3p", [NE, HIDP, C], F32, kind="ExternalInput").ap()
    d["cos"] = nc.dram_tensor("cos", [128, T], F32, kind="ExternalInput").ap()
    d["sin"] = nc.dram_tensor("sin", [128, T], F32, kind="ExternalInput").ap()
    d["pmat"] = nc.dram_tensor("pmat", [128, 128], F32, kind="ExternalInput").ap()
    d["out"] = nc.dram_tensor("out", [T, C], F32, kind="ExternalOutput").ap()
    d["w12b"] = nc.dram_tensor("w12b", [NE, C, 2 * NHT, 128], BF16,
                               kind="Internal").ap()
    d["w3b"] = nc.dram_tensor("w3b", [NE, HIDP, C], BF16, kind="Internal").ap()

    es_all = ExitStack()
    sems = {}
    pS = es_all.enter_context(nc.psum_tensor("pS", [128, 2048], F32))
    pxs = [es_all.enter_context(nc.psum_tensor(f"px{i}", [128, 512], F32))
           for i in range(4)]

    class PSlice:
        def __init__(self, h, off):
            self.h, self.off = h, off

        def __getitem__(self, idx):
            ps, fs = idx
            a = fs.start or 0
            b = fs.stop if fs.stop is not None else 512
            return self.h[ps, self.off + a:self.off + b]

    pf = [PSlice(pS, i * 512) for i in range(4)] + \
         [PSlice(p, 0) for p in pxs]

    A1 = es_all.enter_context(nc.sbuf_tensor("uA1", [128, NCT, T], BF16))
    ident_bf = es_all.enter_context(nc.sbuf_tensor("ident_bf", [128, 128], BF16))
    ident_f = es_all.enter_context(nc.sbuf_tensor("ident_f", [128, 128], F32))
    pmat = es_all.enter_context(nc.sbuf_tensor("pmat_s", [128, 128], F32))
    bq = es_all.enter_context(nc.sbuf_tensor("bq", [128, 128], BF16))
    bk = es_all.enter_context(nc.sbuf_tensor("bk", [128, 128], BF16))
    wvup = es_all.enter_context(nc.sbuf_tensor("wvup", [128, 64], BF16))
    ones1 = es_all.enter_context(nc.sbuf_tensor("ones1", [128, 128], F32))
    rbuf = es_all.enter_context(nc.sbuf_tensor("rbuf", [128, 1024], F32))
    cbuf = es_all.enter_context(nc.sbuf_tensor("cbuf", [128, 1024], F32))
    x1bf = es_all.enter_context(nc.sbuf_tensor("x1bf", [128, 1024], BF16))
    g_all = es_all.enter_context(nc.sbuf_tensor("g_all", [128, NTB, 8], F32))
    sl = es_all.enter_context(nc.sbuf_tensor("sl", [128, 8], F32))
    gm1 = es_all.enter_context(nc.sbuf_tensor("gm1", [128, 1], F32))
    gm1n = es_all.enter_context(nc.sbuf_tensor("gm1n", [128, 1], F32))
    geq = es_all.enter_context(nc.sbuf_tensor("geq", [128, 8], F32))
    gm2 = es_all.enter_context(nc.sbuf_tensor("gm2", [128, 1], F32))
    gwex = es_all.enter_context(nc.sbuf_tensor("gwex", [128, 8], F32))
    gkeep = es_all.enter_context(nc.sbuf_tensor("gkeep", [128, 8], F32))
    gz = es_all.enter_context(nc.sbuf_tensor("gz", [128, 1], F32))
    lnm = es_all.enter_context(nc.sbuf_tensor("lnm", [128, 1], F32))
    lnv = es_all.enter_context(nc.sbuf_tensor("lnv", [128, 1], F32))
    lnr = es_all.enter_context(nc.sbuf_tensor("lnr", [128, 1], F32))
    cshift = es_all.enter_context(nc.sbuf_tensor("cshift", [128, 1], F32))

    for sname in ("sd", "sg", "sm", "sc", "sp", "sv", "ss", "sq",
                  "so0", "so1", "sw0", "sw1", "sx0", "sx1",
                  "sl0", "sl1", "wb0", "wb1", "u0", "u1", "ub0", "ub1",
                  "rl0", "rl1", "ru0", "ru1", "gr0", "gr1", "yd0", "yd1",
                  "tl0", "tl1", "tw0", "tw1", "tl2", "tl3", "tw2", "tw3"):
        sems[sname] = es_all.enter_context(nc.semaphore(sname))

    S.op("gpsimd", "memset", ident_bf[:], 0.0)
    S.edge("gpsimd", "sq")
    S.op("gpsimd", "affine_select", out=ident_bf[:], in_=ident_bf[:],
         compare_op=ALU.not_equal, fill=1.0, base=0, pattern=[[-1, 128]],
         channel_multiplier=1)
    S.op("gpsimd", "memset", ident_f[:], 0.0)
    S.edge("gpsimd", "sq")
    S.op("gpsimd", "affine_select", out=ident_f[:], in_=ident_f[:],
         compare_op=ALU.not_equal, fill=1.0, base=0, pattern=[[-1, 128]],
         channel_multiplier=1)
    S.op("gpsimd", "memset", ones1[:], 1.0)
    S.op("gpsimd", "memset", cshift[:], -20.0, inc="sq")
    m_consts = S.cnt["sq"]
    S.op("gpsimd", "memset", bq[:], 0.0)
    S.op("gpsimd", "memset", bk[:], 0.0, inc="sq")
    m_bqbk = S.cnt["sq"]
    S.wait("gpsimd", "sq", m_bqbk)
    S.dma("gpsimd", pmat[:], d["pmat"], "sm")
    for (bmat, c_d, e_d) in ((bq, d["wupqc"], d["wupqe"]),
                             (bk, d["wupkc"], d["wupke"])):
        for pb in (0, 64):
            S.dma("gpsimd", bmat[pb:pb + 64, pb + 0:pb + 32], c_d, "sm")
            S.dma("gpsimd", bmat[pb:pb + 64, pb + 32:pb + 64], e_d, "sm")
    S.dma("gpsimd", wvup[0:64, :], d["wupv"], "sm")
    S.dma("gpsimd", wvup[64:128, :], d["wupv"], "sm")
    m_small_w = S.cnt["sm"]

    es_qkv = ExitStack()
    QT = es_qkv.enter_context(nc.sbuf_tensor("QT", [128, NCT, T], BF16))
    KT = es_qkv.enter_context(nc.sbuf_tensor("KT", [128, NCT, T], BF16))
    es_v = ExitStack()
    V = es_v.enter_context(nc.sbuf_tensor("V", [128, NTB, 16 * 65], BF16, side="right"))
    es_lat = ExitStack()
    A5 = es_lat.enter_context(nc.sbuf_tensor("uA5", [128, NCT, T], BF16, side="right"))
    es_wtb = ExitStack()
    wtb = es_wtb.enter_context(nc.sbuf_tensor("wtb", [128, 4, 1536], BF16,
                                              side="right"))
    es_1 = ExitStack()
    costab = es_1.enter_context(nc.sbuf_tensor("costab", [128, T], F32))
    sintab = es_1.enter_context(nc.sbuf_tensor("sintab", [128, T], F32))
    wstage = es_1.enter_context(nc.sbuf_tensor("wstage", [128, 2, NCT, 128], BF16))

    S.dma("sync", costab[:], d["cos"], "sd")
    S.dma("sync", sintab[:], d["sin"], "sd")
    m_tabs = S.cnt["sd"]

    def _flat(ap, per_p):
        return ap.rearrange("a b c -> (a b c)").rearrange("(p q) -> p q", p=128)

    w12f = d["w12i"].rearrange("e c m h -> (e c) (m h)").rearrange(
        "(p q) r -> p (q r)", p=128)
    w12bf = d["w12b"].rearrange("e c m h -> (e c) (m h)").rearrange(
        "(p q) r -> p (q r)", p=128)
    w3f = d["w3p"].rearrange("e h c -> (e h) c").rearrange(
        "(p q) r -> p (q r)", p=128)
    w3bf = d["w3b"].rearrange("e h c -> (e h) c").rearrange(
        "(p q) r -> p (q r)", p=128)
    tc_jobs = []
    for (fsrc, fdst, total) in ((w12f, w12bf, 64 * 5632), (w3f, w3bf, 176 * 1024)):
        off = 0
        while off < total:
            wdt = min(1536, total - off)
            tc_jobs.append((fsrc[:, off:off + wdt], fdst[:, off:off + wdt], wdt))
            off += wdt
    tc_state = {"i": 0}

    def emit_transcode(n):
        for _ in range(n):
            i = tc_state["i"]
            if i >= len(tc_jobs):
                return
            tc_state["i"] = i + 1
            slot = i % 4
            srcf, dstb, width = tc_jobs[i]
            S.slot_wait("gpsimd", ("wtb_st", slot))
            ml = S.dma("gpsimd", wtb[:, slot, 0:width], srcf, f"tl{slot}")
            S.wait("sync", f"tl{slot}", ml)
            ms = S.dma("sync", dstb, wtb[:, slot, 0:width], f"tw{slot}")
            S.slot_set(("wtb_st", slot), f"tw{slot}", ms)

    S.dma("gpsimd", A1[:], d["xT"].rearrange("(c p) t -> p c t", p=128), "sg")
    m_xt = S.cnt["sg"]

    S.wait("tensor", "sm", m_small_w)
    S.wait("vector", "sd", m_tabs)
    S.wait("vector", "sq", m_consts)

    m_lat_last = {}

    def down_proj(pi, wd, gi0):
        m_wload = {}

        def load(fp):
            gi = gi0 + fp
            S.slot_wait("gpsimd", ("wstage", gi % 2))
            m_wload[fp] = S.dma(
                "gpsimd", wstage[:, gi % 2, :, :],
                wd[:, fp * 128:(fp + 1) * 128].rearrange("(ct p) f -> p ct f",
                                                         p=128), f"sw{gi % 2}")
        load(0)
        load(1)
        last_ev = None
        for fp in range(NCT):
            emit_transcode(1)
            gi = gi0 + fp
            S.wait("tensor", f"sw{gi % 2}", m_wload[fp])
            if pi == 0 and fp == 0:
                S.wait("tensor", "sg", m_xt)
            for tc in range(4):
                b = (0, 1, 6, 7)[tc]
                S.bank_wait("tensor", b)
                mm = None
                for ct in range(NCT):
                    mm = S.op("tensor", "matmul", pf[b][:, :],
                              lhsT=wstage[:, gi % 2, ct, :],
                              rhs=A1[:, ct, tc * 512:(tc + 1) * 512],
                              start=(ct == 0), stop=(ct == NCT - 1),
                              inc="sp" if ct == NCT - 1 else None)
                if tc == 3:
                    S.slot_set(("wstage", gi % 2), "sp", mm)
                S.wait("scalar", "sp", mm)
                last_ev = S.op("scalar", "copy",
                               A5[:, fp, tc * 512:(tc + 1) * 512], pf[b][:, :],
                               inc="ss")
                S.bank_free(b, "ss", last_ev)
            if fp + 2 < NCT:
                load(fp + 2)
        m_lat_last[pi] = last_ev

    def up_proj_rope(pi, bmat, scale):
        S.wait("tensor", "ss", m_lat_last[pi])
        dst = QT if pi == 0 else KT
        qtmp = cbuf[:, 0:512]
        t1b = rbuf[:, 0:512]
        t2b = rbuf[:, 512:1024]
        mdone = None
        for pt in range(NCT):
            for tc in range(4):
                i = pt * 4 + tc
                ub = 2 + (i % 2)
                rb = 4 + (i % 2)
                S.bank_wait("tensor", ub)
                mU = S.op("tensor", "matmul", pf[ub][:, :], lhsT=bmat[:],
                          rhs=A5[:, pt, tc * 512:(tc + 1) * 512],
                          start=True, stop=True, inc="sp")
                S.wait("vector", "sp", mU)
                mq = S.op("vector", "tensor_scalar_mul", qtmp[:], pf[ub][:, :],
                          scale, inc="sv")
                S.bank_free(ub, "sv", mq)
                S.wait("tensor", "sv", mq)
                S.bank_wait("tensor", rb)
                mR = S.op("tensor", "matmul", pf[rb][:, :], lhsT=pmat[:],
                          rhs=qtmp[:], start=True, stop=True, inc="sp")
                S.wait("vector", "sv", mq)
                S.op("vector", "tensor_mul", t1b[:], qtmp[:],
                     costab[:, tc * 512:(tc + 1) * 512])
                S.wait("vector", "sp", mR)
                S.op("vector", "tensor_mul", t2b[:], pf[rb][:, :],
                     sintab[:, tc * 512:(tc + 1) * 512])
                S.edge("vector", "sv")
                mdone = S.op("vector", "tensor_add",
                             dst[:, pt, tc * 512:(tc + 1) * 512], t1b[:], t2b[:],
                             inc="sv")
                S.bank_free(rb, "sv", mdone)
        return mdone

    down_proj(0, d["wq"], 0)
    up_proj_rope(0, bq, SCALE)
    S.wait("scalar", "sp", S.cnt["sp"])
    down_proj(1, d["wk"], 8)
    mk_done = up_proj_rope(1, bk, 1.0)
    S.wait("scalar", "sp", S.cnt["sp"])
    down_proj(2, d["wv"], 16)

    emit_transcode(6)
    S.op("vector", "memset",
         V[:].rearrange("p t (h c) -> p t h c", c=65)[:, :, :, 64:65], 1.0,
         inc="sv")
    S.wait("tensor", "ss", m_lat_last[2])
    for h in range(NH):
        off = 64 * (h % 2)
        for tb in range(NTB):
            i = h * NTB + tb
            b = 6 + (i % 2)
            S.bank_wait("tensor", b)
            mm = S.op("tensor", "matmul", pf[b][:, 0:64],
                      lhsT=A5[off:off + 64, h // 2, tb * 128:(tb + 1) * 128],
                      rhs=wvup[off:off + 64, :], start=True, stop=True,
                      inc="sp")
            S.wait("vector", "sp", mm)
            mev = S.op("vector", "tensor_copy",
                       V[:, tb, h * 65:h * 65 + 64], pf[b][:, 0:64], inc="sv")
            S.bank_free(b, "sv", mev)
    m_v_done = S.cnt["sv"]
    m_pe_a1 = S.cnt["sp"]

    es_1.close()
    es_2 = ExitStack()
    attTA = es_2.enter_context(nc.sbuf_tensor("attTA", [128, 2, 2048], BF16))
    attTB = attTA
    umask = es_2.enter_context(nc.sbuf_tensor("umask", [128, 4, 512], BF16))
    rzs = es_2.enter_context(nc.sbuf_tensor("rzs", [128, 1, 512], F32))
    bsb = es_2.enter_context(nc.sbuf_tensor("bsb", [128, 1, 512], F32))
    ytmp = es_2.enter_context(nc.sbuf_tensor("ytmp", [128, 2, 512], BF16))

    S.op("gpsimd", "memset", umask[:], 0.0, inc="sq")
    S.wait("gpsimd", "sq", S.cnt["sq"])
    for i in range(4):
        S.op("gpsimd", "affine_select", out=umask[:, i, :], in_=umask[:, i, :],
             compare_op=ALU.is_ge, fill=-30000.0, base=-128 * i,
             pattern=[[1, 512]], channel_multiplier=-1,
             inc="sq" if i == 3 else None)
    m_umask = S.cnt["sq"]
    S.wait("tensor", "sq", m_umask)

    def emit_y(h, qS, g, nkc, m_exp):
        hp, hi = h // 2, h % 2
        attx = attT_bufs[hi]
        gpar = g % 2
        yb = 4 + (h * 4 + qS) % 2
        my = None
        S.wait("tensor", "ss", m_exp[g])
        if g == 0:
            S.bank_wait("tensor", yb)
        for i in range(4):
            kc = g * 4 + i
            my = S.op("tensor", "matmul", pf[yb][0:65, :],
                      lhsT=V[:, kc, 65 * h:65 * h + 65],
                      rhs=attx[:, gpar, i * 512:(i + 1) * 512],
                      start=(kc == 0), stop=(kc == nkc - 1),
                      skip_group_check=True,
                      inc="sp" if i == 3 else None)
        S.slot_set(("attT", gpar), "sp", my)
        return my

    attT_bufs = (attTA, attTB)
    S.wait("tensor", "sv", m_v_done)
    m_ydma = 0
    for h in range(NH):
        hp, hi = h // 2, h % 2
        off = 64 * hi
        for qS in range(4):
            emit_transcode(7)
            nkc = (qS + 1) * 4
            m_exp = {}
            m_ylast = None
            for g in range(qS + 1):
                gpar = g % 2
                for i in range(4):
                    kc = g * 4 + i
                    diag = kc - 4 * qS
                    S.bank_wait("tensor", i)
                    S.op("tensor", "matmul", pf[i][:, :],
                         lhsT=KT[off:off + 64, hp, kc * 128:(kc + 1) * 128],
                         rhs=QT[off:off + 64, hp, qS * 512:(qS + 1) * 512],
                         start=True, stop=(diag < 0))
                    if diag >= 0:
                        S.op("tensor", "matmul", pf[i][:, :],
                             lhsT=ident_bf[:], rhs=umask[:, diag, :],
                             start=False, stop=True)
                    last = S.prog["tensor"][-1]
                    S.cnt["sp"] = S.cnt.get("sp", 0) + 1
                    mm = S.cnt["sp"]
                    S.prog["tensor"][-1] = (last[0], last[1], last[2],
                                            ("sp", 1))
                if g > 0:
                    m_ylast = emit_y(h, qS, g - 1, nkc, m_exp)
                attx = attT_bufs[hi]
                S.slot_wait("scalar", ("attT", gpar))
                if h == 0 and qS == 0 and g == 0:
                    S.wait("scalar", "sq", m_consts)
                S.wait("scalar", "sp", mm)
                mex = S.op("scalar", "activation", attx[:, gpar, :],
                           pS[:, 0:2048], AF.Exp, bias=cshift[:, 0:1],
                           inc="ss")
                for i in range(4):
                    S.bank_free(i, "ss", mex)
                m_exp[g] = mex
            m_ylast = emit_y(h, qS, qS, nkc, m_exp)
            upar = (h * 4 + qS) % 2
            yb = 4 + upar
            bb = 6 + upar
            S.wait("scalar", "sp", m_ylast)
            S.slot_wait("scalar", ("rzs",))
            S.op("scalar", "activation", rzs[64:65, 0, :],
                 pf[yb][64:65, :], AF.Ln)
            S.edge("scalar", "ss")
            mrz = S.op("scalar", "activation", rzs[64:65, 0, :],
                       rzs[64:65, 0, :], AF.Exp, scale=-1.0, inc="ss")
            S.wait("tensor", "ss", mrz)
            S.bank_wait("tensor", bb)
            mbb = S.op("tensor", "matmul", pf[bb][0:64, :],
                       lhsT=ones1[64:65, 0:64], rhs=rzs[64:65, 0, :],
                       start=True, stop=True, inc="sp")
            S.slot_set(("rzs",), "sp", mbb)
            S.wait("vector", "sp", mbb)
            S.slot_wait("vector", ("bsb",))
            mbe = S.op("vector", "tensor_copy", bsb[0:64, 0, :],
                       pf[bb][0:64, :], inc="sv")
            S.bank_free(bb, "sv", mbe)
            S.wait("vector", "sv", mbe)
            if hi == 0:
                mtt = S.op("vector", "tensor_mul",
                           A5[0:64, hp, qS * 512:(qS + 1) * 512],
                           pf[yb][0:64, :], bsb[0:64, 0, :], inc="sv")
            else:
                S.slot_wait("vector", ("ytmp", qS % 2))
                mtt = S.op("vector", "tensor_mul", ytmp[0:64, qS % 2, :],
                           pf[yb][0:64, :], bsb[0:64, 0, :], inc="sv")
                S.wait("sync", "sv", mtt)
                m_ydma = S.dma("sync", A5[64:128, hp,
                                          qS * 512:(qS + 1) * 512],
                               ytmp[0:64, qS % 2, :], f"yd{qS % 2}")
                S.slot_set(("ytmp", qS % 2), f"yd{qS % 2}", m_ydma)
            S.slot_set(("bsb",), "sv", mtt)
            S.bank_free(yb, "sv", mtt)
    emit_transcode(len(tc_jobs))
    m_yT_done = S.cnt["sv"]
    m_yT_dma = (S.cnt.get("yd0", 0), S.cnt.get("yd1", 0))

    es_2.close()
    es_wtb.close()
    es_qkv.close()
    es_oac = ExitStack()
    oac = es_oac.enter_context(nc.sbuf_tensor("oac", [128, NTB, 1024], F32))
    wcb = es_oac.enter_context(nc.sbuf_tensor("wcb", [128, NCT, C], BF16))
    xres = es_oac.enter_context(nc.sbuf_tensor("xres", [128, 2, 1024], F32))
    wgb = es_oac.enter_context(nc.sbuf_tensor("wgb", [128, NCT, 8], BF16))

    m_pe_a2 = S.cnt["sp"]
    S.wait("gpsimd", "sp", m_pe_a2)
    S.wait("gpsimd", "ss", S.cnt["ss"])
    S.wait("gpsimd", "sv", m_yT_done)
    m_wcb = S.dma("gpsimd", wcb[:],
                  d["wc"].rearrange("(ct p) f -> p ct f", p=128), "sc")
    m_wgb = S.dma("gpsimd", wgb[:],
                  d["wgT"].rearrange("(ct p) e -> p ct e", p=128), "sm")

    m_ln = {}
    m_x1T = {}
    S.wait("tensor", "sc", m_wcb)
    S.wait("tensor", "sv", m_yT_done)
    S.wait("tensor", "yd0", m_yT_dma[0])
    S.wait("tensor", "yd1", m_yT_dma[1])
    S.wait("sync", "sp", m_pe_a2)
    S.wait("sync", "ss", S.cnt["ss"])
    S.wait("sync", "sv", m_yT_done)
    S.wait("sync", "yd0", m_yT_dma[0])
    S.wait("sync", "yd1", m_yT_dma[1])
    for _s in ("tw0", "tw1", "tw2", "tw3"):
        S.wait("sync", _s, S.cnt.get(_s, 0))
    for tb in range(NTB):
        S.slot_wait("sync", ("xres", tb % 2))
        m_xr = S.dma("sync", xres[:, tb % 2, :],
                     d["x"][tb * 128:(tb + 1) * 128, :], f"sx{tb % 2}")
        mWc = {}
        for fh in range(2):
            b = 2 * (tb % 2) + fh
            S.bank_wait("tensor", b)
            mm = None
            for ct in range(NCT):
                mm = S.op("tensor", "matmul", pf[b][:, :],
                          lhsT=A5[:, ct, tb * 128:(tb + 1) * 128],
                          rhs=wcb[:, ct, fh * 512:(fh + 1) * 512],
                          start=(ct == 0), stop=(ct == NCT - 1),
                          inc="sp" if ct == NCT - 1 else None)
            mWc[fh] = mm
        S.wait("vector", "sp", mWc[1])
        S.wait("vector", f"sx{tb % 2}", m_xr)
        S.op("vector", "tensor_add", rbuf[:, 0:512],
             pf[2 * (tb % 2)][:, :], xres[:, tb % 2, 0:512])
        mpr = S.op("vector", "tensor_add", rbuf[:, 512:1024],
                   pf[2 * (tb % 2) + 1][:, :], xres[:, tb % 2, 512:1024],
                   inc="sv")
        S.bank_free(2 * (tb % 2), "sv", mpr)
        S.bank_free(2 * (tb % 2) + 1, "sv", mpr)
        S.slot_set(("xres", tb % 2), "sv", mpr)
        S.wait("vector", "sv", mpr)
        S.op("vector", "reduce_sum", lnm[:], rbuf[:], axis=AX, negate=True)
        S.edge("vector", "sv")
        S.op("vector", "tensor_scalar_mul", lnm[:], lnm[:], 1.0 / 1024.0)
        S.edge("vector", "sv")
        S.op("vector", "tensor_scalar_add", cbuf[:], rbuf[:], lnm[:, 0:1])
        S.edge("vector", "sv")
        S.op("vector", "scalar_tensor_tensor", rbuf[:], in0=cbuf[:], scalar=1.0,
             in1=cbuf[:], op0=ALU.mult, op1=ALU.mult, accum_out=lnv[:, 0:1])
        S.edge("vector", "sv")
        mvv = S.op("vector", "tensor_scalar", lnv[:], lnv[:], 1.0 / 1024.0, EPS,
                   op0=ALU.mult, op1=ALU.add, inc="sv")
        S.wait("scalar", "sv", mvv)
        S.op("scalar", "activation", lnv[:], lnv[:], AF.Ln)
        S.edge("scalar", "ss")
        S.op("scalar", "activation", lnr[:], lnv[:], AF.Exp, scale=-0.5)
        mrstd = S.edge("scalar", "ss")
        S.wait("vector", "ss", mrstd)
        S.op("vector", "tensor_scalar_mul", oac[:, tb, :], cbuf[:], lnr[:, 0:1])
        mx1b = S.op("vector", "tensor_scalar_mul", x1bf[:], cbuf[:],
                    lnr[:, 0:1], inc="sv")
        m_ln[tb] = mx1b
        S.wait("tensor", "sv", mx1b)
        for half in range(2):
            b = 6 + half
            S.bank_wait("tensor", b)
            mt = None
            for ci in range(4):
                ct = half * 4 + ci
                mt = S.op("tensor", "matmul", pf[b][:, ci * 128:(ci + 1) * 128],
                          lhsT=x1bf[:, ct * 128:(ct + 1) * 128],
                          rhs=ident_bf[:], start=True, stop=True,
                          inc="sp" if ci == 3 else None)
            S.wait("vector", "sp", mt)
            mev = S.op("vector", "tensor_copy",
                       A1[:, half * 4:half * 4 + 4, tb * 128:(tb + 1) * 128],
                       pf[b][:, :].rearrange("p (c q) -> p c q", c=4),
                       inc="sv")
            S.bank_free(b, "sv", mev)
        m_x1T[tb] = mev

    m_g = {}
    S.wait("tensor", "sm", m_wgb)
    for tb in range(NTB):
        S.wait("tensor", "sv", m_x1T[tb])
        S.bank_wait("tensor", 5)
        mm = None
        for ct in range(NCT):
            mm = S.op("tensor", "matmul", pf[5][:, 0:8],
                      lhsT=A1[:, ct, tb * 128:(tb + 1) * 128],
                      rhs=wgb[:, ct, :], start=(ct == 0), stop=(ct == NCT - 1),
                      inc="sp" if ct == NCT - 1 else None)
        S.wait("vector", "sp", mm)
        if tb > 0:
            S.wait("vector", "ss", m_sl_free)
        mev = S.op("vector", "tensor_copy", sl[:], pf[5][:, 0:8], inc="sv")
        S.bank_free(5, "sv", mev)
        S.wait("vector", "sv", mev)
        S.op("vector", "reduce_max", gm1[:], sl[:], axis=AX)
        S.edge("vector", "sv")
        S.op("vector", "tensor_scalar_mul", gm1n[:], gm1[:], -1.0)
        S.op("vector", "tensor_scalar", geq[:], sl[:], gm1[:, 0:1], None,
             op0=ALU.is_ge)
        S.edge("vector", "sv")
        S.op("vector", "scalar_tensor_tensor", geq[:], in0=geq[:], scalar=-1e30,
             in1=sl[:], op0=ALU.mult, op1=ALU.add)
        S.edge("vector", "sv")
        S.op("vector", "reduce_max", gm2[:], geq[:], axis=AX)
        mgv = S.edge("vector", "sv")
        S.wait("scalar", "sv", mgv)
        msx = S.op("scalar", "activation", gwex[:], sl[:], AF.Exp,
                   bias=gm1n[:, 0:1], inc="ss")
        m_sl_free = msx
        S.op("vector", "tensor_scalar", gkeep[:], sl[:], gm2[:, 0:1], None,
             op0=ALU.is_ge)
        S.edge("vector", "sv")
        S.wait("vector", "ss", msx)
        S.op("vector", "tensor_mul", gkeep[:], gkeep[:], gwex[:])
        S.edge("vector", "sv")
        S.op("vector", "reduce_sum", gz[:], gkeep[:], axis=AX)
        S.edge("vector", "sv")
        S.op("vector", "reciprocal", gz[:], gz[:])
        S.edge("vector", "sv")
        m_g[tb] = S.op("vector", "tensor_scalar_mul", g_all[:, tb, :],
                       gkeep[:], gz[:, 0:1], inc="sv")

    es_lat.close()
    es_v.close()
    es_f = ExitStack()
    gTrow = es_f.enter_context(nc.sbuf_tensor("gTrow", [8, NTB, 128], F32, side="right"))
    gRow = es_f.enter_context(nc.sbuf_tensor("gRow", [1, 2, 512], F32, side="right"))
    hiddenG = es_f.enter_context(nc.sbuf_tensor("hiddenG", [128, NHT, 512], BF16, side="right"))
    w12s = es_f.enter_context(nc.sbuf_tensor("w12s", [128, 2, NCT, 2, 128], BF16, side="right"))
    w3s = es_f.enter_context(nc.sbuf_tensor("w3s", [128, 2, 1024], BF16, side="right"))
    x1te = es_f.enter_context(nc.sbuf_tensor("x1te", [128, 2, NCT, 512], BF16))
    gBt = es_f.enter_context(nc.sbuf_tensor("gBt", [128, 2, 512], BF16, side="right"))
    s1b = es_f.enter_context(nc.sbuf_tensor("s1b", [128, 2, 512], BF16, side="right"))
    outb = es_f.enter_context(nc.sbuf_tensor("outb", [128, 2, 1024], F32))

    for tb in range(NTB):
        S.wait("tensor", "sv", m_g[tb])
        S.bank_wait("tensor", 4)
        mt = S.op("tensor", "matmul", pf[4][0:8, 0:128],
                  lhsT=g_all[:, tb, :], rhs=ident_f[:], start=True, stop=True,
                  inc="sp")
        S.wait("vector", "sp", mt)
        mev = S.op("vector", "tensor_copy", gTrow[0:8, tb, :],
                   pf[4][0:8, 0:128], inc="sv")
        S.bank_free(4, "sv", mev)
    m_gT = S.cnt["sv"]
    m_pe_a3 = S.cnt["sp"]

    m_w12 = {}
    m_w3 = {}
    m_wb = {}
    m_x1te = {}
    m_oadd = {}

    first_ffn_dma = [True]
    for _q in ("sync", "gpsimd"):
        for _s in ("tw0", "tw1", "tw2", "tw3"):
            S.wait(_q, _s, S.cnt.get(_s, 0))
    S.wait("sync", "sp", m_pe_a3)
    S.wait("gpsimd", "sp", m_pe_a3)

    def w12_load(e, ts, ht):
        buf = ht % 2
        q = "sync" if buf == 0 else "gpsimd"
        S.slot_wait(q, ("w12s", buf))
        srcb = d["w12b"][e, :, 2 * ht:2 * ht + 2, :].rearrange(
            "(ct p) m h -> p ct (m h)", p=128)
        dstb = w12s[:, buf].rearrange("p ct m h -> p ct (m h)")
        m_w12[(e, ts, ht)] = S.dma(q, dstb, srcb, f"rl{buf}")

    def w3_load(e, ts, ht):
        buf = ht % 2
        q = "gpsimd" if buf == 0 else "sync"
        S.slot_wait(q, ("w3s", buf))
        m_w3[(e, ts, ht)] = S.dma(
            q, w3s[:, buf, :],
            d["w3b"][e, ht * 128:(ht + 1) * 128, :], f"u{buf}")

    def prep_x1te(e, ts):
        par = ts % 2
        b = 4 + par
        if e == 0 and ts == 0:
            S.wait("gpsimd", "sv", m_gT)
        S.slot_wait("gpsimd", ("gRow", par))
        m_grow = S.dma("gpsimd", gRow[0:1, par, :],
                       gTrow[e:e + 1, ts * 4:(ts + 1) * 4, :].rearrange(
                           "r b q -> r (b q)"), f"gr{par}")
        S.wait("tensor", f"gr{par}", m_grow)
        S.bank_wait("tensor", b)
        mgbm = S.op("tensor", "matmul", pf[b][:, :], lhsT=ones1[0:1, 0:128],
                    rhs=gRow[0:1, par, :], start=True, stop=True, inc="sp")
        S.slot_set(("gRow", par), "sp", mgbm)
        S.wait("vector", "sp", mgbm)
        S.slot_wait("vector", ("gBt", par))
        mgb = S.op("vector", "tensor_copy", gBt[:, par, :], pf[b][:, :],
                   inc="sv")
        S.bank_free(b, "sv", mgb)
        S.wait("vector", "sv", mgb)
        S.slot_wait("vector", ("x1te", par))
        mte = None
        for ct in range(NCT):
            mte = S.op("vector", "tensor_mul", x1te[:, par, ct, :],
                       A1[:, ct, ts * 512:(ts + 1) * 512], gBt[:, par, :],
                       inc="sv" if ct == NCT - 1 else None)
        S.slot_set(("gBt", par), "sv", mte)
        m_x1te[(e, ts)] = mte

    for e in range(NE):
        for ts in range(4):
            if ts == 0:
                prep_x1te(e, 0)
            w3_load(e, ts, 0)
            w3_load(e, ts, 1)
            mhg = {}
            for ht in range(NHT):
                buf = ht % 2
                w12_load(e, ts, ht)
                S.wait("tensor", f"rl{buf}", m_w12[(e, ts, ht)])
                if ht == 0:
                    S.wait("tensor", "sv", m_x1te[(e, ts)])
                b1 = buf
                b2 = 2 + buf
                S.bank_wait("tensor", b1)
                mm1 = None
                for ct in range(NCT):
                    mm1 = S.op("tensor", "matmul", pf[b1][:, :],
                               lhsT=w12s[:, buf, ct, 0, :],
                               rhs=A1[:, ct, ts * 512:(ts + 1) * 512],
                               start=(ct == 0), stop=(ct == NCT - 1),
                               inc="sp" if ct == NCT - 1 else None)
                S.bank_wait("tensor", b2)
                mm2 = None
                for ct in range(NCT):
                    mm2 = S.op("tensor", "matmul", pf[b2][:, :],
                               lhsT=w12s[:, buf, ct, 1, :],
                               rhs=x1te[:, ts % 2, ct, :],
                               start=(ct == 0), stop=(ct == NCT - 1),
                               inc="sp" if ct == NCT - 1 else None)
                S.slot_set(("w12s", buf), "sp", mm2)
                if ht == NHT - 1:
                    m_h2_done = mm2
                S.wait("scalar", "sp", mm1)
                S.slot_wait("scalar", ("s1b", buf))
                ms1 = S.op("scalar", "activation", s1b[:, buf, :],
                           pf[b1][:, :], AF.Silu, inc="ss")
                S.bank_free(b1, "ss", ms1)
                S.wait("vector", "ss", ms1)
                S.wait("vector", "sp", mm2)
                mhg[ht] = S.op("vector", "tensor_mul", hiddenG[:, ht, :],
                               s1b[:, buf, :], pf[b2][:, :], inc="sv")
                S.slot_set(("s1b", buf), "sv", mhg[ht])
                S.bank_free(b2, "sv", mhg[ht])
            S.slot_set(("x1te", ts % 2), "sp", m_h2_done)
            if ts < 3:
                prep_x1te(e, ts + 1)
            elif e < NE - 1:
                prep_x1te(e + 1, 0)
            mlast = {}
            for ht in range(NHT):
                buf = ht % 2
                if ht >= 2:
                    pass
                S.wait("tensor", f"u{buf}", m_w3[(e, ts, ht)])
                if ht == 0:
                    S.wait("tensor", "sv", mhg[NHT - 1])
                    for b in range(8):
                        S.bank_wait("tensor", b)
                for tbq in range(4):
                    for dhh in range(2):
                        b = tbq * 2 + dhh
                        mm = S.op("tensor", "matmul", pf[b][:, :],
                                  lhsT=hiddenG[:, ht, tbq * 128:(tbq + 1) * 128],
                                  rhs=w3s[:, buf, dhh * 512:(dhh + 1) * 512],
                                  start=(ht == 0), stop=(ht == NHT - 1),
                                  skip_group_check=True,
                                  inc="sp" if (tbq == 3 and dhh == 1) else None)
                S.slot_set(("w3s", buf), "sp", mm)
                if ht + 2 < NHT:
                    w3_load(e, ts, ht + 2)
            m_w3_all = S.cnt["sp"]
            mev = None
            for tbq in range(4):
                for dhh in range(2):
                    b = tbq * 2 + dhh
                    S.wait("vector", "sp", m_w3_all)
                    mev = S.op("vector", "tensor_add",
                               oac[:, ts * 4 + tbq, dhh * 512:(dhh + 1) * 512],
                               pf[b][:, :],
                               oac[:, ts * 4 + tbq, dhh * 512:(dhh + 1) * 512],
                               inc="sv")
                    S.bank_free(b, "sv", mev)
            m_oadd[(e, ts)] = mev
    m_ffn_done = S.cnt["sv"]

    for tb in range(NTB):
        if tb == 0:
            S.wait("vector", "sv", m_ffn_done)
        S.slot_wait("vector", ("outb", tb % 2))
        S.op("vector", "reduce_sum", lnm[:], oac[:, tb, :], axis=AX, negate=True)
        S.edge("vector", "sv")
        S.op("vector", "tensor_scalar_mul", lnm[:], lnm[:], 1.0 / 1024.0)
        S.edge("vector", "sv")
        S.op("vector", "tensor_scalar_add", cbuf[:], oac[:, tb, :], lnm[:, 0:1])
        S.edge("vector", "sv")
        S.op("vector", "scalar_tensor_tensor", rbuf[:], in0=cbuf[:], scalar=1.0,
             in1=cbuf[:], op0=ALU.mult, op1=ALU.mult, accum_out=lnv[:, 0:1])
        S.edge("vector", "sv")
        mvv = S.op("vector", "tensor_scalar", lnv[:], lnv[:], 1.0 / 1024.0, EPS,
                   op0=ALU.mult, op1=ALU.add, inc="sv")
        S.wait("scalar", "sv", mvv)
        S.op("scalar", "activation", lnv[:], lnv[:], AF.Ln)
        S.edge("scalar", "ss")
        S.op("scalar", "activation", lnr[:], lnv[:], AF.Exp, scale=-0.5)
        mrstd = S.edge("scalar", "ss")
        S.wait("vector", "ss", mrstd)
        mo = S.op("vector", "tensor_scalar_mul", outb[:, tb % 2, :], cbuf[:],
                  lnr[:, 0:1], inc="sv")
        S.wait("sync", "sv", mo)
        mdma = S.dma("sync", d["out"][tb * 128:(tb + 1) * 128, :],
                     outb[:, tb % 2, :], f"so{tb % 2}")
        S.slot_set(("outb", tb % 2), f"so{tb % 2}", mdma)
    S.wait("sync", "so0", S.cnt["so0"])
    S.wait("sync", "so1", S.cnt["so1"])

    with nc.Block() as block:
        def _mk(eng_name):
            def runner(engine):
                for meth, a, k, incv in S.prog[eng_name]:
                    if meth == "wait_ge":
                        engine.wait_ge(sems[a[0]], a[1])
                        continue
                    inst = getattr(engine, meth)(*a, **k)
                    if incv is not None:
                        inst.then_inc(sems[incv[0]], incv[1])
            return runner
        block.sync(_mk("sync"))
        block.gpsimd(_mk("gpsimd"))
        block.tensor(_mk("tensor"))
        block.vector(_mk("vector"))
        block.scalar(_mk("scalar"))

    es_f.close()
    es_oac.close()
    es_all.close()
    return nc


_CACHE = {}


def kernel(**inputs):
    def A(name):
        return np.ascontiguousarray(np.asarray(inputs[name], dtype=np.float32))

    x = A("L0_attn_x")
    wq = A("L0_attn_Wq_down")
    wk = A("L0_attn_Wk_down")
    wv = A("L0_attn_Wv_down")
    wc = A("L0_attn_Wc")
    wupqc = A("L0_attn_Wq_up_c")
    wupqe = A("L0_attn_Wq_up_e")
    wupkc = A("L0_attn_Wk_up_c")
    wupke = A("L0_attn_Wk_up_e")
    wupv = A("L0_attn_Wv_up")
    wg = A("L0_ffn_Wg")
    w1 = A("L0_ffn_W1")
    w2 = A("L0_ffn_W2")
    w3 = A("L0_ffn_W3")

    wgT = np.ascontiguousarray(wg.T)
    w1t = np.zeros((NE, C, HIDP), np.float32)
    w1t[:, :, :2730] = w1.transpose(0, 2, 1)
    w2t = np.zeros((NE, C, HIDP), np.float32)
    w2t[:, :, :2730] = w2.transpose(0, 2, 1)
    w12i = np.zeros((NE, C, 2 * NHT, 128), np.float32)
    w12i[:, :, 0::2, :] = w1t.reshape(NE, C, NHT, 128)
    w12i[:, :, 1::2, :] = w2t.reshape(NE, C, NHT, 128)
    w12i = np.ascontiguousarray(w12i)
    w3p = np.zeros((NE, HIDP, C), np.float32)
    w3p[:, :2730, :] = w3.transpose(0, 2, 1)
    w3p = np.ascontiguousarray(w3p)
    cos, sin, pm = _host_consts()

    if "nc" not in _CACHE:
        _CACHE["nc"] = build()
    nc = _CACHE["nc"]

    shared = dict(wq=wq, wk=wk, wv=wv, wc=wc, wupqc=wupqc, wupqe=wupqe,
                  wupkc=wupkc, wupke=wupke, wupv=wupv, wgT=wgT,
                  w12i=w12i, w3p=w3p, cos=cos, sin=sin, pmat=pm)
    in_maps = []
    for b in range(8):
        m = dict(shared)
        m["x"] = np.ascontiguousarray(x[b])
        m["xT"] = np.ascontiguousarray(x[b].T)
        in_maps.append(m)

    res = run_bass_kernel_spmd(nc, in_maps, core_ids=list(range(8)))
    return np.stack([r["out"] for r in res.results], axis=0)


# version 12
# speedup vs baseline: 1.1903x; 1.1903x over previous
import numpy as np
import ml_dtypes
import concourse.bass as bass
import concourse.mybir as mybir
from concourse.bass_utils import run_bass_kernel_spmd
from contextlib import ExitStack

F32 = mybir.dt.float32
BF16 = mybir.dt.bfloat16
AX = mybir.AxisListType.X
ALU = mybir.AluOpType
AF = mybir.ActivationFunctionType

T = 2048
C = 1024
NCT = 8
NH = 16
NTB = 16
NTS = 4
NE = 8
NHT = 22
HIDP = 2816
EPS = 1e-5
SCALE = 0.125

ENG = ("sync", "gpsimd", "tensor", "vector", "scalar")


class Sched:
    def __init__(self):
        self.prog = {e: [] for e in ENG}
        self.cnt = {}
        self.bank = {}
        self.slot = {}

    def op(self, eng, meth, *a, inc=None, incn=1, **k):
        m = None
        if inc is not None:
            self.cnt[inc] = self.cnt.get(inc, 0) + incn
            m = self.cnt[inc]
        self.prog[eng].append((meth, a, k, (inc, incn) if inc else None))
        return m

    def dma(self, eng, out, in_, inc, **k):
        return self.op(eng, "dma_start", inc=inc, incn=16, out=out, in_=in_, **k)

    def wait(self, eng, sem, val):
        if val:
            self.prog[eng].append(("wait_ge", (sem, val), {}, None))

    def edge(self, eng, sem):
        last = self.prog[eng][-1]
        if last[0] == "wait_ge":
            return self.cnt.get(sem, 0)
        if last[3] is not None:
            s2 = last[3][0]
            m = self.cnt[s2]
            self.prog[eng].append(("wait_ge", (s2, m), {}, None))
            return m
        self.cnt[sem] = self.cnt.get(sem, 0) + 1
        m = self.cnt[sem]
        self.prog[eng][-1] = (last[0], last[1], last[2], (sem, 1))
        self.prog[eng].append(("wait_ge", (sem, m), {}, None))
        return m

    def bank_wait(self, eng, b):
        if b in self.bank:
            self.wait(eng, *self.bank[b])

    def bank_free(self, b, sem, val):
        self.bank[b] = (sem, val)

    def slot_wait(self, eng, key):
        if key in self.slot:
            self.wait(eng, *self.slot[key])

    def slot_set(self, key, sem, val):
        self.slot[key] = (sem, val)


def _host_consts():
    theta = 10000.0
    inv = 1.0 / theta ** (np.arange(0, 32, 2, dtype=np.float64) / 32.0)
    tt = np.arange(T, dtype=np.float64)[:, None] * inv[None, :]
    emb = np.concatenate([tt, tt], axis=1)
    cos = np.zeros((128, T), dtype=np.float32)
    sin = np.zeros((128, T), dtype=np.float32)
    for base in (0, 64):
        cos[base:base + 32, :] = 1.0
        cos[base + 32:base + 64, :] = np.cos(emb).T.astype(np.float32)
        sin[base + 32:base + 64, :] = np.sin(emb).T.astype(np.float32)
    pm = np.zeros((128, 128), dtype=np.float32)
    for base in (32, 96):
        for i in range(16):
            pm[base + 2 * i + 1, base + 2 * i] = -1.0
            pm[base + 2 * i, base + 2 * i + 1] = 1.0
    return cos, sin, pm


def build():
    nc = bass.Bass("TRN2", target_bir_lowering=False)
    S = Sched()

    d = {}
    d["x"] = nc.dram_tensor("x", [T, C], F32, kind="ExternalInput").ap()
    d["xT"] = nc.dram_tensor("xT", [C, T], F32, kind="ExternalInput").ap()
    for n in ("wq", "wk", "wv", "wc"):
        d[n] = nc.dram_tensor(n, [C, C], F32, kind="ExternalInput").ap()
    for n in ("wupqc", "wupqe", "wupkc", "wupke"):
        d[n] = nc.dram_tensor(n, [64, 32], F32, kind="ExternalInput").ap()
    d["wupv"] = nc.dram_tensor("wupv", [64, 64], F32, kind="ExternalInput").ap()
    d["wgT"] = nc.dram_tensor("wgT", [C, NE], F32, kind="ExternalInput").ap()
    d["w12i"] = nc.dram_tensor("w12i", [NE, C, 2 * NHT, 128], F32,
                               kind="ExternalInput").ap()
    d["w3p"] = nc.dram_tensor("w3p", [NE, HIDP, C], F32, kind="ExternalInput").ap()
    d["cos"] = nc.dram_tensor("cos", [128, T], F32, kind="ExternalInput").ap()
    d["sin"] = nc.dram_tensor("sin", [128, T], F32, kind="ExternalInput").ap()
    d["pmat"] = nc.dram_tensor("pmat", [128, 128], F32, kind="ExternalInput").ap()
    d["out"] = nc.dram_tensor("out", [T, C], F32, kind="ExternalOutput").ap()
    d["w12b"] = nc.dram_tensor("w12b", [NE, C, 2 * NHT, 128], BF16,
                               kind="Internal").ap()
    d["w3b"] = nc.dram_tensor("w3b", [NE, HIDP, C], BF16, kind="Internal").ap()

    es_all = ExitStack()
    sems = {}
    pS = es_all.enter_context(nc.psum_tensor("pS", [128, 2048], F32))
    pxs = [es_all.enter_context(nc.psum_tensor(f"px{i}", [128, 512], F32))
           for i in range(4)]

    class PSlice:
        def __init__(self, h, off):
            self.h, self.off = h, off

        def __getitem__(self, idx):
            ps, fs = idx
            a = fs.start or 0
            b = fs.stop if fs.stop is not None else 512
            return self.h[ps, self.off + a:self.off + b]

    pf = [PSlice(pS, i * 512) for i in range(4)] + \
         [PSlice(p, 0) for p in pxs]

    A1 = es_all.enter_context(nc.sbuf_tensor("uA1", [128, NCT, T], BF16))
    ident_bf = es_all.enter_context(nc.sbuf_tensor("ident_bf", [128, 128], BF16))
    ident_f = es_all.enter_context(nc.sbuf_tensor("ident_f", [128, 128], F32))
    pmat = es_all.enter_context(nc.sbuf_tensor("pmat_s", [128, 128], F32))
    bq = es_all.enter_context(nc.sbuf_tensor("bq", [128, 128], BF16))
    bk = es_all.enter_context(nc.sbuf_tensor("bk", [128, 128], BF16))
    ones_bf = es_all.enter_context(nc.sbuf_tensor("ones_bf", [1, 128], BF16))
    wvup = es_all.enter_context(nc.sbuf_tensor("wvup", [128, 64], BF16))
    ones1 = es_all.enter_context(nc.sbuf_tensor("ones1", [128, 128], F32))
    rbuf = es_all.enter_context(nc.sbuf_tensor("rbuf", [128, 1024], F32))
    cbuf = es_all.enter_context(nc.sbuf_tensor("cbuf", [128, 1024], F32))
    x1bf = es_all.enter_context(nc.sbuf_tensor("x1bf", [128, 1024], BF16))
    g_all = es_all.enter_context(nc.sbuf_tensor("g_all", [128, NTB, 8], F32))
    sl = es_all.enter_context(nc.sbuf_tensor("sl", [128, 8], F32))
    gm1 = es_all.enter_context(nc.sbuf_tensor("gm1", [128, 1], F32))
    gm1n = es_all.enter_context(nc.sbuf_tensor("gm1n", [128, 1], F32))
    geq = es_all.enter_context(nc.sbuf_tensor("geq", [128, 8], F32))
    gm2 = es_all.enter_context(nc.sbuf_tensor("gm2", [128, 1], F32))
    gwex = es_all.enter_context(nc.sbuf_tensor("gwex", [128, 8], F32))
    gkeep = es_all.enter_context(nc.sbuf_tensor("gkeep", [128, 8], F32))
    gz = es_all.enter_context(nc.sbuf_tensor("gz", [128, 1], F32))
    lnm = es_all.enter_context(nc.sbuf_tensor("lnm", [128, 1], F32))
    lnv = es_all.enter_context(nc.sbuf_tensor("lnv", [128, 1], F32))
    lnr = es_all.enter_context(nc.sbuf_tensor("lnr", [128, 1], F32))
    cshift = es_all.enter_context(nc.sbuf_tensor("cshift", [128, 1], F32))

    for sname in ("sd", "sg", "sm", "sc", "sp", "sv", "ss", "sq",
                  "so0", "so1", "sw0", "sw1", "sx0", "sx1",
                  "sl0", "sl1", "wb0", "wb1", "u0", "u1", "ub0", "ub1",
                  "rl0", "rl1", "rl2", "rl3", "u2", "u3", "gr0", "gr1", "yd0", "yd1",
                  "tl0", "tl1", "tw0", "tw1", "tl2", "tl3", "tw2", "tw3"):
        sems[sname] = es_all.enter_context(nc.semaphore(sname))

    S.op("gpsimd", "memset", ident_bf[:], 0.0)
    S.edge("gpsimd", "sq")
    S.op("gpsimd", "affine_select", out=ident_bf[:], in_=ident_bf[:],
         compare_op=ALU.not_equal, fill=1.0, base=0, pattern=[[-1, 128]],
         channel_multiplier=1)
    S.op("gpsimd", "memset", ident_f[:], 0.0)
    S.edge("gpsimd", "sq")
    S.op("gpsimd", "affine_select", out=ident_f[:], in_=ident_f[:],
         compare_op=ALU.not_equal, fill=1.0, base=0, pattern=[[-1, 128]],
         channel_multiplier=1)
    S.op("gpsimd", "memset", ones1[:], 1.0)
    S.op("gpsimd", "memset", ones_bf[:], 1.0)
    S.op("gpsimd", "memset", cshift[:], -20.0, inc="sq")
    m_consts = S.cnt["sq"]
    S.op("gpsimd", "memset", bq[:], 0.0)
    S.op("gpsimd", "memset", bk[:], 0.0, inc="sq")
    m_bqbk = S.cnt["sq"]
    S.wait("gpsimd", "sq", m_bqbk)
    S.dma("gpsimd", pmat[:], d["pmat"], "sm")
    for (bmat, c_d, e_d) in ((bq, d["wupqc"], d["wupqe"]),
                             (bk, d["wupkc"], d["wupke"])):
        for pb in (0, 64):
            S.dma("gpsimd", bmat[pb:pb + 64, pb + 0:pb + 32], c_d, "sm")
            S.dma("gpsimd", bmat[pb:pb + 64, pb + 32:pb + 64], e_d, "sm")
    S.dma("gpsimd", wvup[0:64, :], d["wupv"], "sm")
    S.dma("gpsimd", wvup[64:128, :], d["wupv"], "sm")
    m_small_w = S.cnt["sm"]

    es_qkv = ExitStack()
    QT = es_qkv.enter_context(nc.sbuf_tensor("QT", [128, NCT, T], BF16))
    KT = es_qkv.enter_context(nc.sbuf_tensor("KT", [128, NCT, T], BF16))
    es_v = ExitStack()
    V = es_v.enter_context(nc.sbuf_tensor("V", [128, NTB, 16 * 65], BF16, side="right"))
    es_lat = ExitStack()
    A5 = es_lat.enter_context(nc.sbuf_tensor("uA5", [128, NCT, T], BF16, side="right"))
    es_wtb = ExitStack()
    wtb = es_wtb.enter_context(nc.sbuf_tensor("wtb", [128, 4, 1536], BF16,
                                              side="right"))
    es_1 = ExitStack()
    costab = es_1.enter_context(nc.sbuf_tensor("costab", [128, T], F32))
    sintab = es_1.enter_context(nc.sbuf_tensor("sintab", [128, T], F32))
    wstage = es_1.enter_context(nc.sbuf_tensor("wstage", [128, 2, NCT, 128], BF16))

    S.dma("sync", costab[:], d["cos"], "sd")
    S.dma("sync", sintab[:], d["sin"], "sd")
    m_tabs = S.cnt["sd"]

    def _flat(ap, per_p):
        return ap.rearrange("a b c -> (a b c)").rearrange("(p q) -> p q", p=128)

    w12f = d["w12i"].rearrange("e c m h -> (e c) (m h)").rearrange(
        "(p q) r -> p (q r)", p=128)
    w12bf = d["w12b"].rearrange("e c m h -> (e c) (m h)").rearrange(
        "(p q) r -> p (q r)", p=128)
    w3f = d["w3p"].rearrange("e h c -> (e h) c").rearrange(
        "(p q) r -> p (q r)", p=128)
    w3bf = d["w3b"].rearrange("e h c -> (e h) c").rearrange(
        "(p q) r -> p (q r)", p=128)
    tc_jobs = []
    for (fsrc, fdst, total) in ((w12f, w12bf, 64 * 5632), (w3f, w3bf, 176 * 1024)):
        off = 0
        while off < total:
            wdt = min(1536, total - off)
            tc_jobs.append((fsrc[:, off:off + wdt], fdst[:, off:off + wdt], wdt))
            off += wdt
    tc_state = {"i": 0}

    def emit_transcode(n):
        for _ in range(n):
            i = tc_state["i"]
            if i >= len(tc_jobs):
                return
            tc_state["i"] = i + 1
            slot = i % 4
            srcf, dstb, width = tc_jobs[i]
            S.slot_wait("gpsimd", ("wtb_st", slot))
            ml = S.dma("gpsimd", wtb[:, slot, 0:width], srcf, f"tl{slot}")
            S.wait("sync", f"tl{slot}", ml)
            ms = S.dma("sync", dstb, wtb[:, slot, 0:width], f"tw{slot}")
            S.slot_set(("wtb_st", slot), f"tw{slot}", ms)

    S.dma("gpsimd", A1[:], d["xT"].rearrange("(c p) t -> p c t", p=128), "sg")
    m_xt = S.cnt["sg"]

    S.wait("tensor", "sm", m_small_w)
    S.wait("vector", "sd", m_tabs)
    S.wait("vector", "sq", m_consts)

    m_lat_last = {}

    def down_proj(pi, wd, gi0):
        m_wload = {}

        def load(fp):
            gi = gi0 + fp
            S.slot_wait("gpsimd", ("wstage", gi % 2))
            m_wload[fp] = S.dma(
                "gpsimd", wstage[:, gi % 2, :, :],
                wd[:, fp * 128:(fp + 1) * 128].rearrange("(ct p) f -> p ct f",
                                                         p=128), f"sw{gi % 2}")
        load(0)
        load(1)
        last_ev = None
        for fp in range(NCT):
            emit_transcode(1)
            gi = gi0 + fp
            S.wait("tensor", f"sw{gi % 2}", m_wload[fp])
            if pi == 0 and fp == 0:
                S.wait("tensor", "sg", m_xt)
            for tc in range(4):
                b = tc % 2
                S.bank_wait("tensor", b)
                mm = None
                for ct in range(NCT):
                    mm = S.op("tensor", "matmul", pf[b][:, :],
                              lhsT=wstage[:, gi % 2, ct, :],
                              rhs=A1[:, ct, tc * 512:(tc + 1) * 512],
                              start=(ct == 0), stop=(ct == NCT - 1),
                              inc="sp" if ct == NCT - 1 else None)
                if tc == 3:
                    S.slot_set(("wstage", gi % 2), "sp", mm)
                S.wait("vector", "sp", mm)
                last_ev = S.op("vector", "tensor_copy",
                               A5[:, fp, tc * 512:(tc + 1) * 512], pf[b][:, :],
                               inc="sv")
                S.bank_free(b, "sv", last_ev)
            if fp + 2 < NCT:
                load(fp + 2)
        m_lat_last[pi] = last_ev

    def up_proj_rope(pi, bmat, scale):
        S.wait("tensor", "sv", m_lat_last[pi])
        dst = QT if pi == 0 else KT
        qtmp = cbuf[:, 0:512]
        t1b = rbuf[:, 0:512]
        t2b = rbuf[:, 512:1024]
        mdone = None
        for pt in range(NCT):
            for tc in range(4):
                i = pt * 4 + tc
                ub = 2 + (i % 2)
                rb = 4 + (i % 2)
                S.bank_wait("tensor", ub)
                mU = S.op("tensor", "matmul", pf[ub][:, :], lhsT=bmat[:],
                          rhs=A5[:, pt, tc * 512:(tc + 1) * 512],
                          start=True, stop=True, inc="sp")
                S.wait("vector", "sp", mU)
                mq = S.op("vector", "tensor_scalar_mul", qtmp[:], pf[ub][:, :],
                          scale, inc="sv")
                S.bank_free(ub, "sv", mq)
                S.wait("tensor", "sv", mq)
                S.bank_wait("tensor", rb)
                mR = S.op("tensor", "matmul", pf[rb][:, :], lhsT=pmat[:],
                          rhs=qtmp[:], start=True, stop=True, inc="sp")
                S.wait("vector", "sv", mq)
                S.op("vector", "tensor_mul", t1b[:], qtmp[:],
                     costab[:, tc * 512:(tc + 1) * 512])
                S.wait("vector", "sp", mR)
                S.op("vector", "tensor_mul", t2b[:], pf[rb][:, :],
                     sintab[:, tc * 512:(tc + 1) * 512])
                S.edge("vector", "sv")
                mdone = S.op("vector", "tensor_add",
                             dst[:, pt, tc * 512:(tc + 1) * 512], t1b[:], t2b[:],
                             inc="sv")
                S.bank_free(rb, "sv", mdone)
        return mdone

    down_proj(0, d["wq"], 0)
    up_proj_rope(0, bq, SCALE)
    S.wait("vector", "sp", S.cnt["sp"])
    down_proj(1, d["wk"], 8)
    mk_done = up_proj_rope(1, bk, 1.0)
    S.wait("vector", "sp", S.cnt["sp"])
    down_proj(2, d["wv"], 16)

    emit_transcode(6)
    S.op("vector", "memset",
         V[:].rearrange("p t (h c) -> p t h c", c=65)[:, :, :, 64:65], 1.0,
         inc="sv")
    S.wait("tensor", "sv", m_lat_last[2])
    for h in range(NH):
        off = 64 * (h % 2)
        for tb in range(NTB):
            i = h * NTB + tb
            b = 6 + (i % 2)
            S.bank_wait("tensor", b)
            mm = S.op("tensor", "matmul", pf[b][:, 0:64],
                      lhsT=A5[off:off + 64, h // 2, tb * 128:(tb + 1) * 128],
                      rhs=wvup[off:off + 64, :], start=True, stop=True,
                      inc="sp")
            S.wait("vector", "sp", mm)
            mev = S.op("vector", "tensor_copy",
                       V[:, tb, h * 65:h * 65 + 64], pf[b][:, 0:64], inc="sv")
            S.bank_free(b, "sv", mev)
    m_v_done = S.cnt["sv"]
    m_pe_a1 = S.cnt["sp"]

    es_1.close()
    es_2 = ExitStack()
    attTA = es_2.enter_context(nc.sbuf_tensor("attTA", [128, 2, 2048], BF16))
    attTB = attTA
    umask = es_2.enter_context(nc.sbuf_tensor("umask", [128, 4, 512], BF16))
    rzs = es_2.enter_context(nc.sbuf_tensor("rzs", [128, 1, 512], F32))
    bsb = es_2.enter_context(nc.sbuf_tensor("bsb", [128, 1, 512], F32))
    ytmp = es_2.enter_context(nc.sbuf_tensor("ytmp", [128, 2, 512], BF16))

    S.op("gpsimd", "memset", umask[:], 0.0, inc="sq")
    S.wait("gpsimd", "sq", S.cnt["sq"])
    for i in range(4):
        S.op("gpsimd", "affine_select", out=umask[:, i, :], in_=umask[:, i, :],
             compare_op=ALU.is_ge, fill=-30000.0, base=-128 * i,
             pattern=[[1, 512]], channel_multiplier=-1,
             inc="sq" if i == 3 else None)
    m_umask = S.cnt["sq"]
    S.wait("tensor", "sq", m_umask)

    def emit_y(h, qS, g, nkc, m_exp):
        hp, hi = h // 2, h % 2
        attx = attT_bufs[hi]
        gpar = g % 2
        yb = 4 + (h * 4 + qS) % 2
        my = None
        S.wait("tensor", "ss", m_exp[g])
        if g == 0:
            S.bank_wait("tensor", yb)
        for i in range(4):
            kc = g * 4 + i
            my = S.op("tensor", "matmul", pf[yb][0:65, :],
                      lhsT=V[:, kc, 65 * h:65 * h + 65],
                      rhs=attx[:, gpar, i * 512:(i + 1) * 512],
                      start=(kc == 0), stop=(kc == nkc - 1),
                      skip_group_check=True,
                      inc="sp" if i == 3 else None)
        S.slot_set(("attT", gpar), "sp", my)
        return my

    attT_bufs = (attTA, attTB)
    S.wait("tensor", "sv", m_v_done)
    m_ydma = 0
    for h in range(NH):
        hp, hi = h // 2, h % 2
        off = 64 * hi
        for qS in range(4):
            emit_transcode(7)
            nkc = (qS + 1) * 4
            m_exp = {}
            m_ylast = None
            for g in range(qS + 1):
                gpar = g % 2
                for i in range(4):
                    kc = g * 4 + i
                    diag = kc - 4 * qS
                    S.bank_wait("tensor", i)
                    S.op("tensor", "matmul", pf[i][:, :],
                         lhsT=KT[off:off + 64, hp, kc * 128:(kc + 1) * 128],
                         rhs=QT[off:off + 64, hp, qS * 512:(qS + 1) * 512],
                         start=True, stop=(diag < 0))
                    if diag >= 0:
                        S.op("tensor", "matmul", pf[i][:, :],
                             lhsT=ident_bf[:], rhs=umask[:, diag, :],
                             start=False, stop=True)
                    last = S.prog["tensor"][-1]
                    S.cnt["sp"] = S.cnt.get("sp", 0) + 1
                    mm = S.cnt["sp"]
                    S.prog["tensor"][-1] = (last[0], last[1], last[2],
                                            ("sp", 1))
                if g > 0:
                    m_ylast = emit_y(h, qS, g - 1, nkc, m_exp)
                attx = attT_bufs[hi]
                S.slot_wait("scalar", ("attT", gpar))
                if h == 0 and qS == 0 and g == 0:
                    S.wait("scalar", "sq", m_consts)
                S.wait("scalar", "sp", mm)
                mex = S.op("scalar", "activation", attx[:, gpar, :],
                           pS[:, 0:2048], AF.Exp, bias=cshift[:, 0:1],
                           inc="ss")
                for i in range(4):
                    S.bank_free(i, "ss", mex)
                m_exp[g] = mex
            m_ylast = emit_y(h, qS, qS, nkc, m_exp)
            upar = (h * 4 + qS) % 2
            yb = 4 + upar
            bb = 6 + upar
            S.wait("scalar", "sp", m_ylast)
            S.slot_wait("scalar", ("rzs",))
            S.op("scalar", "activation", rzs[64:65, 0, :],
                 pf[yb][64:65, :], AF.Ln)
            S.edge("scalar", "ss")
            mrz = S.op("scalar", "activation", rzs[64:65, 0, :],
                       rzs[64:65, 0, :], AF.Exp, scale=-1.0, inc="ss")
            S.wait("tensor", "ss", mrz)
            S.bank_wait("tensor", bb)
            mbb = S.op("tensor", "matmul", pf[bb][0:64, :],
                       lhsT=ones1[64:65, 0:64], rhs=rzs[64:65, 0, :],
                       start=True, stop=True, inc="sp")
            S.slot_set(("rzs",), "sp", mbb)
            S.wait("vector", "sp", mbb)
            S.slot_wait("vector", ("bsb",))
            mbe = S.op("vector", "tensor_copy", bsb[0:64, 0, :],
                       pf[bb][0:64, :], inc="sv")
            S.bank_free(bb, "sv", mbe)
            S.wait("vector", "sv", mbe)
            if hi == 0:
                mtt = S.op("vector", "tensor_mul",
                           A5[0:64, hp, qS * 512:(qS + 1) * 512],
                           pf[yb][0:64, :], bsb[0:64, 0, :], inc="sv")
            else:
                S.slot_wait("vector", ("ytmp", qS % 2))
                mtt = S.op("vector", "tensor_mul", ytmp[0:64, qS % 2, :],
                           pf[yb][0:64, :], bsb[0:64, 0, :], inc="sv")
                S.wait("sync", "sv", mtt)
                m_ydma = S.dma("sync", A5[64:128, hp,
                                          qS * 512:(qS + 1) * 512],
                               ytmp[0:64, qS % 2, :], f"yd{qS % 2}")
                S.slot_set(("ytmp", qS % 2), f"yd{qS % 2}", m_ydma)
            S.slot_set(("bsb",), "sv", mtt)
            S.bank_free(yb, "sv", mtt)
    emit_transcode(len(tc_jobs))
    m_yT_done = S.cnt["sv"]
    m_yT_dma = (S.cnt.get("yd0", 0), S.cnt.get("yd1", 0))

    es_2.close()
    es_wtb.close()
    es_qkv.close()
    es_oac = ExitStack()
    oac = es_oac.enter_context(nc.sbuf_tensor("oac", [128, NTB, 1024], F32))
    wcb = es_oac.enter_context(nc.sbuf_tensor("wcb", [128, NCT, C], BF16))
    xres = es_oac.enter_context(nc.sbuf_tensor("xres", [128, 2, 1024], F32))
    wgb = es_oac.enter_context(nc.sbuf_tensor("wgb", [128, NCT, 8], BF16))

    m_pe_a2 = S.cnt["sp"]
    S.wait("gpsimd", "sp", m_pe_a2)
    S.wait("gpsimd", "ss", S.cnt["ss"])
    S.wait("gpsimd", "sv", m_yT_done)
    m_wcb = S.dma("gpsimd", wcb[:],
                  d["wc"].rearrange("(ct p) f -> p ct f", p=128), "sc")
    m_wgb = S.dma("gpsimd", wgb[:],
                  d["wgT"].rearrange("(ct p) e -> p ct e", p=128), "sm")

    m_ln = {}
    m_x1T = {}
    S.wait("tensor", "sc", m_wcb)
    S.wait("tensor", "sv", m_yT_done)
    S.wait("tensor", "yd0", m_yT_dma[0])
    S.wait("tensor", "yd1", m_yT_dma[1])
    S.wait("sync", "sp", m_pe_a2)
    S.wait("sync", "ss", S.cnt["ss"])
    S.wait("sync", "sv", m_yT_done)
    S.wait("sync", "yd0", m_yT_dma[0])
    S.wait("sync", "yd1", m_yT_dma[1])
    for _s in ("tw0", "tw1", "tw2", "tw3"):
        S.wait("sync", _s, S.cnt.get(_s, 0))
    for tb in range(NTB):
        S.slot_wait("sync", ("xres", tb % 2))
        m_xr = S.dma("sync", xres[:, tb % 2, :],
                     d["x"][tb * 128:(tb + 1) * 128, :], f"sx{tb % 2}")
        mWc = {}
        for fh in range(2):
            b = 2 * (tb % 2) + fh
            S.bank_wait("tensor", b)
            mm = None
            for ct in range(NCT):
                mm = S.op("tensor", "matmul", pf[b][:, :],
                          lhsT=A5[:, ct, tb * 128:(tb + 1) * 128],
                          rhs=wcb[:, ct, fh * 512:(fh + 1) * 512],
                          start=(ct == 0), stop=(ct == NCT - 1),
                          inc="sp" if ct == NCT - 1 else None)
            mWc[fh] = mm
        S.wait("vector", "sp", mWc[1])
        S.wait("vector", f"sx{tb % 2}", m_xr)
        S.op("vector", "tensor_add", rbuf[:, 0:512],
             pf[2 * (tb % 2)][:, :], xres[:, tb % 2, 0:512])
        mpr = S.op("vector", "tensor_add", rbuf[:, 512:1024],
                   pf[2 * (tb % 2) + 1][:, :], xres[:, tb % 2, 512:1024],
                   inc="sv")
        S.bank_free(2 * (tb % 2), "sv", mpr)
        S.bank_free(2 * (tb % 2) + 1, "sv", mpr)
        S.slot_set(("xres", tb % 2), "sv", mpr)
        S.wait("vector", "sv", mpr)
        S.op("vector", "reduce_sum", lnm[:], rbuf[:], axis=AX, negate=True)
        S.edge("vector", "sv")
        S.op("vector", "tensor_scalar_mul", lnm[:], lnm[:], 1.0 / 1024.0)
        S.edge("vector", "sv")
        S.op("vector", "tensor_scalar_add", cbuf[:], rbuf[:], lnm[:, 0:1])
        S.edge("vector", "sv")
        S.op("vector", "scalar_tensor_tensor", rbuf[:], in0=cbuf[:], scalar=1.0,
             in1=cbuf[:], op0=ALU.mult, op1=ALU.mult, accum_out=lnv[:, 0:1])
        S.edge("vector", "sv")
        mvv = S.op("vector", "tensor_scalar", lnv[:], lnv[:], 1.0 / 1024.0, EPS,
                   op0=ALU.mult, op1=ALU.add, inc="sv")
        S.wait("scalar", "sv", mvv)
        S.op("scalar", "activation", lnv[:], lnv[:], AF.Ln)
        S.edge("scalar", "ss")
        S.op("scalar", "activation", lnr[:], lnv[:], AF.Exp, scale=-0.5)
        mrstd = S.edge("scalar", "ss")
        S.wait("vector", "ss", mrstd)
        S.op("vector", "tensor_scalar_mul", oac[:, tb, :], cbuf[:], lnr[:, 0:1])
        mx1b = S.op("vector", "tensor_scalar_mul", x1bf[:], cbuf[:],
                    lnr[:, 0:1], inc="sv")
        m_ln[tb] = mx1b
        S.wait("tensor", "sv", mx1b)
        for half in range(2):
            b = 6 + half
            S.bank_wait("tensor", b)
            mt = None
            for ci in range(4):
                ct = half * 4 + ci
                mt = S.op("tensor", "matmul", pf[b][:, ci * 128:(ci + 1) * 128],
                          lhsT=x1bf[:, ct * 128:(ct + 1) * 128],
                          rhs=ident_bf[:], start=True, stop=True,
                          inc="sp" if ci == 3 else None)
            S.wait("vector", "sp", mt)
            mev = S.op("vector", "tensor_copy",
                       A1[:, half * 4:half * 4 + 4, tb * 128:(tb + 1) * 128],
                       pf[b][:, :].rearrange("p (c q) -> p c q", c=4),
                       inc="sv")
            S.bank_free(b, "sv", mev)
        m_x1T[tb] = mev

    m_g = {}
    S.wait("tensor", "sm", m_wgb)
    for tb in range(NTB):
        S.wait("tensor", "sv", m_x1T[tb])
        S.bank_wait("tensor", 5)
        mm = None
        for ct in range(NCT):
            mm = S.op("tensor", "matmul", pf[5][:, 0:8],
                      lhsT=A1[:, ct, tb * 128:(tb + 1) * 128],
                      rhs=wgb[:, ct, :], start=(ct == 0), stop=(ct == NCT - 1),
                      inc="sp" if ct == NCT - 1 else None)
        S.wait("vector", "sp", mm)
        if tb > 0:
            S.wait("vector", "ss", m_sl_free)
        mev = S.op("vector", "tensor_copy", sl[:], pf[5][:, 0:8], inc="sv")
        S.bank_free(5, "sv", mev)
        S.wait("vector", "sv", mev)
        S.op("vector", "reduce_max", gm1[:], sl[:], axis=AX)
        S.edge("vector", "sv")
        S.op("vector", "tensor_scalar_mul", gm1n[:], gm1[:], -1.0)
        S.op("vector", "tensor_scalar", geq[:], sl[:], gm1[:, 0:1], None,
             op0=ALU.is_ge)
        S.edge("vector", "sv")
        S.op("vector", "scalar_tensor_tensor", geq[:], in0=geq[:], scalar=-1e30,
             in1=sl[:], op0=ALU.mult, op1=ALU.add)
        S.edge("vector", "sv")
        S.op("vector", "reduce_max", gm2[:], geq[:], axis=AX)
        mgv = S.edge("vector", "sv")
        S.wait("scalar", "sv", mgv)
        msx = S.op("scalar", "activation", gwex[:], sl[:], AF.Exp,
                   bias=gm1n[:, 0:1], inc="ss")
        m_sl_free = msx
        S.op("vector", "tensor_scalar", gkeep[:], sl[:], gm2[:, 0:1], None,
             op0=ALU.is_ge)
        S.edge("vector", "sv")
        S.wait("vector", "ss", msx)
        S.op("vector", "tensor_mul", gkeep[:], gkeep[:], gwex[:])
        S.edge("vector", "sv")
        S.op("vector", "reduce_sum", gz[:], gkeep[:], axis=AX)
        S.edge("vector", "sv")
        S.op("vector", "reciprocal", gz[:], gz[:])
        S.edge("vector", "sv")
        m_g[tb] = S.op("vector", "tensor_scalar_mul", g_all[:, tb, :],
                       gkeep[:], gz[:, 0:1], inc="sv")

    es_lat.close()
    es_v.close()
    es_f = ExitStack()
    gTrow = es_f.enter_context(nc.sbuf_tensor("gTrow", [8, NTB, 128], BF16, side="right"))
    gRow = es_f.enter_context(nc.sbuf_tensor("gRow", [1, 2, 512], BF16, side="right"))
    hiddenG = es_f.enter_context(nc.sbuf_tensor("hiddenG", [128, NHT, 512], BF16, side="right"))
    w12s = es_f.enter_context(nc.sbuf_tensor("w12s", [128, 4, NCT, 2, 128], BF16, side="right"))
    w3s = es_f.enter_context(nc.sbuf_tensor("w3s", [128, 2, 1024], BF16, side="right"))
    x1te = es_f.enter_context(nc.sbuf_tensor("x1te", [128, 2, NCT, 512], BF16))
    gBt = es_f.enter_context(nc.sbuf_tensor("gBt", [128, 2, 512], BF16, side="right"))
    s1b = es_f.enter_context(nc.sbuf_tensor("s1b", [128, 2, 512], BF16, side="right"))
    outb = es_f.enter_context(nc.sbuf_tensor("outb", [128, 1, 1024], F32))

    for tb in range(NTB):
        S.wait("tensor", "sv", m_g[tb])
        S.bank_wait("tensor", 4)
        mt = S.op("tensor", "matmul", pf[4][0:8, 0:128],
                  lhsT=g_all[:, tb, :], rhs=ident_f[:], start=True, stop=True,
                  inc="sp")
        S.wait("vector", "sp", mt)
        mev = S.op("vector", "tensor_copy", gTrow[0:8, tb, :],
                   pf[4][0:8, 0:128], inc="sv")
        S.bank_free(4, "sv", mev)
    m_gT = S.cnt["sv"]
    m_pe_a3 = S.cnt["sp"]

    m_w12 = {}
    m_w3 = {}
    m_wb = {}
    m_x1te = {}
    m_oadd = {}

    first_ffn_dma = [True]
    for _q in ("sync", "gpsimd"):
        for _s in ("tw0", "tw1", "tw2", "tw3"):
            S.wait(_q, _s, S.cnt.get(_s, 0))
    S.wait("sync", "sp", m_pe_a3)
    S.wait("gpsimd", "sp", m_pe_a3)

    def w12_load(e, ts, ht):
        buf = ht % 4
        S.slot_wait("sync", ("w12s", buf))
        srcb = d["w12b"][e, :, 2 * ht:2 * ht + 2, :].rearrange(
            "(ct p) m h -> p ct (m h)", p=128)
        dstb = w12s[:, buf].rearrange("p ct m h -> p ct (m h)")
        m_w12[(e, ts, ht)] = S.dma("sync", dstb, srcb, f"rl{buf}")

    def w3_load(e, ts, ht):
        buf = ht % 2
        S.slot_wait("gpsimd", ("w3s", buf))
        m_w3[(e, ts, ht)] = S.dma(
            "gpsimd", w3s[:, buf, :],
            d["w3b"][e, ht * 128:(ht + 1) * 128, :], f"u{buf}")

    def prep_x1te(e, ts):
        par = ts % 2
        b = 4 + par
        if e == 0 and ts == 0:
            S.wait("gpsimd", "sv", m_gT)
        S.slot_wait("gpsimd", ("gRow", par))
        m_grow = S.dma("gpsimd", gRow[0:1, par, :],
                       gTrow[e:e + 1, ts * 4:(ts + 1) * 4, :].rearrange(
                           "r b q -> r (b q)"), f"gr{par}")
        S.wait("tensor", f"gr{par}", m_grow)
        S.bank_wait("tensor", b)
        mgbm = S.op("tensor", "matmul", pf[b][:, :], lhsT=ones_bf[0:1, :],
                    rhs=gRow[0:1, par, :], start=True, stop=True, inc="sp")
        S.slot_set(("gRow", par), "sp", mgbm)
        S.wait("vector", "sp", mgbm)
        S.slot_wait("vector", ("gBt", par))
        mgb = S.op("vector", "tensor_copy", gBt[:, par, :], pf[b][:, :],
                   inc="sv")
        S.bank_free(b, "sv", mgb)
        S.wait("vector", "sv", mgb)
        S.slot_wait("vector", ("x1te", par))
        mte = None
        for ct in range(NCT):
            mte = S.op("vector", "tensor_mul", x1te[:, par, ct, :],
                       A1[:, ct, ts * 512:(ts + 1) * 512], gBt[:, par, :],
                       inc="sv" if ct == NCT - 1 else None)
        S.slot_set(("gBt", par), "sv", mte)
        m_x1te[(e, ts)] = mte

    for e in range(NE):
        for ts in range(4):
            if ts == 0:
                prep_x1te(e, 0)
            w3_load(e, ts, 0)
            w3_load(e, ts, 1)
            mhg = {}
            for ht in range(NHT):
                buf = ht % 4
                w12_load(e, ts, ht)
                S.wait("tensor", f"rl{buf}", m_w12[(e, ts, ht)])
                if ht == 0:
                    S.wait("tensor", "sv", m_x1te[(e, ts)])
                b1 = buf
                b2 = 2 + buf
                S.bank_wait("tensor", b1)
                mm1 = None
                for ct in range(NCT):
                    mm1 = S.op("tensor", "matmul", pf[b1][:, :],
                               lhsT=w12s[:, buf, ct, 0, :],
                               rhs=A1[:, ct, ts * 512:(ts + 1) * 512],
                               start=(ct == 0), stop=(ct == NCT - 1),
                               inc="sp" if ct == NCT - 1 else None)
                S.bank_wait("tensor", b2)
                mm2 = None
                for ct in range(NCT):
                    mm2 = S.op("tensor", "matmul", pf[b2][:, :],
                               lhsT=w12s[:, buf, ct, 1, :],
                               rhs=x1te[:, ts % 2, ct, :],
                               start=(ct == 0), stop=(ct == NCT - 1),
                               inc="sp" if ct == NCT - 1 else None)
                S.slot_set(("w12s", buf), "sp", mm2)
                if ht == NHT - 1:
                    m_h2_done = mm2
                S.wait("scalar", "sp", mm1)
                S.slot_wait("scalar", ("s1b", ht % 2))
                ms1 = S.op("scalar", "activation", s1b[:, ht % 2, :],
                           pf[b1][:, :], AF.Silu, inc="ss")
                S.bank_free(b1, "ss", ms1)
                S.wait("vector", "ss", ms1)
                S.wait("vector", "sp", mm2)
                mhg[ht] = S.op("vector", "tensor_mul", hiddenG[:, ht, :],
                               s1b[:, ht % 2, :], pf[b2][:, :], inc="sv")
                S.slot_set(("s1b", ht % 2), "sv", mhg[ht])
                S.bank_free(b2, "sv", mhg[ht])
            S.slot_set(("x1te", ts % 2), "sp", m_h2_done)
            if ts < 3:
                prep_x1te(e, ts + 1)
            elif e < NE - 1:
                prep_x1te(e + 1, 0)
            mlast = {}
            for ht in range(NHT):
                buf = ht % 2
                if ht >= 2:
                    pass
                S.wait("tensor", f"u{buf}", m_w3[(e, ts, ht)])
                if ht == 0:
                    S.wait("tensor", "sv", mhg[NHT - 1])
                    for b in range(8):
                        S.bank_wait("tensor", b)
                for tbq in range(4):
                    for dhh in range(2):
                        b = tbq * 2 + dhh
                        mm = S.op("tensor", "matmul", pf[b][:, :],
                                  lhsT=hiddenG[:, ht, tbq * 128:(tbq + 1) * 128],
                                  rhs=w3s[:, buf, dhh * 512:(dhh + 1) * 512],
                                  start=(ht == 0), stop=(ht == NHT - 1),
                                  skip_group_check=True,
                                  inc="sp" if (tbq == 3 and dhh == 1) else None)
                S.slot_set(("w3s", buf), "sp", mm)
                if ht + 2 < NHT:
                    w3_load(e, ts, ht + 2)
            m_w3_all = S.cnt["sp"]
            mev = None
            for tbq in range(4):
                for dhh in range(2):
                    b = tbq * 2 + dhh
                    S.wait("vector", "sp", m_w3_all)
                    mev = S.op("vector", "tensor_add",
                               oac[:, ts * 4 + tbq, dhh * 512:(dhh + 1) * 512],
                               pf[b][:, :],
                               oac[:, ts * 4 + tbq, dhh * 512:(dhh + 1) * 512],
                               inc="sv")
                    S.bank_free(b, "sv", mev)
            m_oadd[(e, ts)] = mev
    m_ffn_done = S.cnt["sv"]

    for tb in range(NTB):
        if tb == 0:
            S.wait("vector", "sv", m_ffn_done)
        S.slot_wait("vector", ("outb",))
        S.op("vector", "reduce_sum", lnm[:], oac[:, tb, :], axis=AX, negate=True)
        S.edge("vector", "sv")
        S.op("vector", "tensor_scalar_mul", lnm[:], lnm[:], 1.0 / 1024.0)
        S.edge("vector", "sv")
        S.op("vector", "tensor_scalar_add", cbuf[:], oac[:, tb, :], lnm[:, 0:1])
        S.edge("vector", "sv")
        S.op("vector", "scalar_tensor_tensor", rbuf[:], in0=cbuf[:], scalar=1.0,
             in1=cbuf[:], op0=ALU.mult, op1=ALU.mult, accum_out=lnv[:, 0:1])
        S.edge("vector", "sv")
        mvv = S.op("vector", "tensor_scalar", lnv[:], lnv[:], 1.0 / 1024.0, EPS,
                   op0=ALU.mult, op1=ALU.add, inc="sv")
        S.wait("scalar", "sv", mvv)
        S.op("scalar", "activation", lnv[:], lnv[:], AF.Ln)
        S.edge("scalar", "ss")
        S.op("scalar", "activation", lnr[:], lnv[:], AF.Exp, scale=-0.5)
        mrstd = S.edge("scalar", "ss")
        S.wait("vector", "ss", mrstd)
        mo = S.op("vector", "tensor_scalar_mul", outb[:, 0, :], cbuf[:],
                  lnr[:, 0:1], inc="sv")
        S.wait("sync", "sv", mo)
        mdma = S.dma("sync", d["out"][tb * 128:(tb + 1) * 128, :],
                     outb[:, 0, :], f"so{tb % 2}")
        S.slot_set(("outb",), f"so{tb % 2}", mdma)
    S.wait("sync", "so0", S.cnt["so0"])
    S.wait("sync", "so1", S.cnt["so1"])

    with nc.Block() as block:
        def _mk(eng_name):
            def runner(engine):
                for meth, a, k, incv in S.prog[eng_name]:
                    if meth == "wait_ge":
                        engine.wait_ge(sems[a[0]], a[1])
                        continue
                    inst = getattr(engine, meth)(*a, **k)
                    if incv is not None:
                        inst.then_inc(sems[incv[0]], incv[1])
            return runner
        block.sync(_mk("sync"))
        block.gpsimd(_mk("gpsimd"))
        block.tensor(_mk("tensor"))
        block.vector(_mk("vector"))
        block.scalar(_mk("scalar"))

    es_f.close()
    es_oac.close()
    es_all.close()
    return nc


_CACHE = {}


def kernel(**inputs):
    def A(name):
        return np.ascontiguousarray(np.asarray(inputs[name], dtype=np.float32))

    x = A("L0_attn_x")
    wq = A("L0_attn_Wq_down")
    wk = A("L0_attn_Wk_down")
    wv = A("L0_attn_Wv_down")
    wc = A("L0_attn_Wc")
    wupqc = A("L0_attn_Wq_up_c")
    wupqe = A("L0_attn_Wq_up_e")
    wupkc = A("L0_attn_Wk_up_c")
    wupke = A("L0_attn_Wk_up_e")
    wupv = A("L0_attn_Wv_up")
    wg = A("L0_ffn_Wg")
    w1 = A("L0_ffn_W1")
    w2 = A("L0_ffn_W2")
    w3 = A("L0_ffn_W3")

    wgT = np.ascontiguousarray(wg.T)
    w1t = np.zeros((NE, C, HIDP), np.float32)
    w1t[:, :, :2730] = w1.transpose(0, 2, 1)
    w2t = np.zeros((NE, C, HIDP), np.float32)
    w2t[:, :, :2730] = w2.transpose(0, 2, 1)
    w12i = np.zeros((NE, C, 2 * NHT, 128), np.float32)
    w12i[:, :, 0::2, :] = w1t.reshape(NE, C, NHT, 128)
    w12i[:, :, 1::2, :] = w2t.reshape(NE, C, NHT, 128)
    w12i = np.ascontiguousarray(w12i)
    w3p = np.zeros((NE, HIDP, C), np.float32)
    w3p[:, :2730, :] = w3.transpose(0, 2, 1)
    w3p = np.ascontiguousarray(w3p)
    cos, sin, pm = _host_consts()

    if "nc" not in _CACHE:
        _CACHE["nc"] = build()
    nc = _CACHE["nc"]

    shared = dict(wq=wq, wk=wk, wv=wv, wc=wc, wupqc=wupqc, wupqe=wupqe,
                  wupkc=wupkc, wupke=wupke, wupv=wupv, wgT=wgT,
                  w12i=w12i, w3p=w3p, cos=cos, sin=sin, pmat=pm)
    in_maps = []
    for b in range(8):
        m = dict(shared)
        m["x"] = np.ascontiguousarray(x[b])
        m["xT"] = np.ascontiguousarray(x[b].T)
        in_maps.append(m)

    res = run_bass_kernel_spmd(nc, in_maps, core_ids=list(range(8)))
    return np.stack([r["out"] for r in res.results], axis=0)


# version 13
# speedup vs baseline: 1.2082x; 1.0151x over previous
import numpy as np
import ml_dtypes
import concourse.bass as bass
import concourse.mybir as mybir
from concourse.bass_utils import run_bass_kernel_spmd
from contextlib import ExitStack

F32 = mybir.dt.float32
BF16 = mybir.dt.bfloat16
AX = mybir.AxisListType.X
ALU = mybir.AluOpType
AF = mybir.ActivationFunctionType

T = 2048
C = 1024
NCT = 8
NH = 16
NTB = 16
NTS = 4
NE = 8
NHT = 22
HIDP = 2816
EPS = 1e-5
SCALE = 0.125

ENG = ("sync", "gpsimd", "tensor", "vector", "scalar")


class Sched:
    def __init__(self):
        self.prog = {e: [] for e in ENG}
        self.cnt = {}
        self.bank = {}
        self.slot = {}

    def op(self, eng, meth, *a, inc=None, incn=1, **k):
        m = None
        if inc is not None:
            self.cnt[inc] = self.cnt.get(inc, 0) + incn
            m = self.cnt[inc]
        self.prog[eng].append((meth, a, k, (inc, incn) if inc else None))
        return m

    def dma(self, eng, out, in_, inc, **k):
        return self.op(eng, "dma_start", inc=inc, incn=16, out=out, in_=in_, **k)

    def wait(self, eng, sem, val):
        if val:
            self.prog[eng].append(("wait_ge", (sem, val), {}, None))

    def edge(self, eng, sem):
        last = self.prog[eng][-1]
        if last[0] == "wait_ge":
            return self.cnt.get(sem, 0)
        if last[3] is not None:
            s2 = last[3][0]
            m = self.cnt[s2]
            self.prog[eng].append(("wait_ge", (s2, m), {}, None))
            return m
        self.cnt[sem] = self.cnt.get(sem, 0) + 1
        m = self.cnt[sem]
        self.prog[eng][-1] = (last[0], last[1], last[2], (sem, 1))
        self.prog[eng].append(("wait_ge", (sem, m), {}, None))
        return m

    def bank_wait(self, eng, b):
        if b in self.bank:
            self.wait(eng, *self.bank[b])

    def bank_free(self, b, sem, val):
        self.bank[b] = (sem, val)

    def slot_wait(self, eng, key):
        if key in self.slot:
            self.wait(eng, *self.slot[key])

    def slot_set(self, key, sem, val):
        self.slot[key] = (sem, val)


def _host_consts():
    theta = 10000.0
    inv = 1.0 / theta ** (np.arange(0, 32, 2, dtype=np.float64) / 32.0)
    tt = np.arange(T, dtype=np.float64)[:, None] * inv[None, :]
    emb = np.concatenate([tt, tt], axis=1)
    cos = np.zeros((128, T), dtype=np.float32)
    sin = np.zeros((128, T), dtype=np.float32)
    for base in (0, 64):
        cos[base:base + 32, :] = 1.0
        cos[base + 32:base + 64, :] = np.cos(emb).T.astype(np.float32)
        sin[base + 32:base + 64, :] = np.sin(emb).T.astype(np.float32)
    pm = np.zeros((128, 128), dtype=np.float32)
    for base in (32, 96):
        for i in range(16):
            pm[base + 2 * i + 1, base + 2 * i] = -1.0
            pm[base + 2 * i, base + 2 * i + 1] = 1.0
    return cos, sin, pm


def build():
    nc = bass.Bass("TRN2", target_bir_lowering=False)
    S = Sched()

    d = {}
    d["x"] = nc.dram_tensor("x", [T, C], F32, kind="ExternalInput").ap()
    d["xT"] = nc.dram_tensor("xT", [C, T], F32, kind="ExternalInput").ap()
    for n in ("wq", "wk", "wv", "wc"):
        d[n] = nc.dram_tensor(n, [C, C], F32, kind="ExternalInput").ap()
    for n in ("wupqc", "wupqe", "wupkc", "wupke"):
        d[n] = nc.dram_tensor(n, [64, 32], F32, kind="ExternalInput").ap()
    d["wupv"] = nc.dram_tensor("wupv", [64, 64], F32, kind="ExternalInput").ap()
    d["wgT"] = nc.dram_tensor("wgT", [C, NE], F32, kind="ExternalInput").ap()
    d["w12i"] = nc.dram_tensor("w12i", [NE, C, 2 * NHT, 128], F32,
                               kind="ExternalInput").ap()
    d["w3p"] = nc.dram_tensor("w3p", [NE, HIDP, C], F32, kind="ExternalInput").ap()
    d["cos"] = nc.dram_tensor("cos", [128, T], F32, kind="ExternalInput").ap()
    d["sin"] = nc.dram_tensor("sin", [128, T], F32, kind="ExternalInput").ap()
    d["pmat"] = nc.dram_tensor("pmat", [128, 128], F32, kind="ExternalInput").ap()
    d["out"] = nc.dram_tensor("out", [T, C], F32, kind="ExternalOutput").ap()
    d["w12b"] = nc.dram_tensor("w12b", [NE, C, 2 * NHT, 128], BF16,
                               kind="Internal").ap()
    d["w3b"] = nc.dram_tensor("w3b", [NE, HIDP, C], BF16, kind="Internal").ap()

    es_all = ExitStack()
    sems = {}
    pS = es_all.enter_context(nc.psum_tensor("pS", [128, 2048], F32))
    pxs = [es_all.enter_context(nc.psum_tensor(f"px{i}", [128, 512], F32))
           for i in range(4)]

    class PSlice:
        def __init__(self, h, off):
            self.h, self.off = h, off

        def __getitem__(self, idx):
            ps, fs = idx
            a = fs.start or 0
            b = fs.stop if fs.stop is not None else 512
            return self.h[ps, self.off + a:self.off + b]

    pf = [PSlice(pS, i * 512) for i in range(4)] + \
         [PSlice(p, 0) for p in pxs]

    A1 = es_all.enter_context(nc.sbuf_tensor("uA1", [128, NCT, T], BF16))
    ident_bf = es_all.enter_context(nc.sbuf_tensor("ident_bf", [128, 128], BF16))
    ident_f = es_all.enter_context(nc.sbuf_tensor("ident_f", [128, 128], F32))
    pmat = es_all.enter_context(nc.sbuf_tensor("pmat_s", [128, 128], F32))
    bq = es_all.enter_context(nc.sbuf_tensor("bq", [128, 128], BF16))
    bk = es_all.enter_context(nc.sbuf_tensor("bk", [128, 128], BF16))
    ones_bf = es_all.enter_context(nc.sbuf_tensor("ones_bf", [1, 128], BF16))
    wvup = es_all.enter_context(nc.sbuf_tensor("wvup", [128, 64], BF16))
    ones1 = es_all.enter_context(nc.sbuf_tensor("ones1", [128, 128], F32))
    rbuf = es_all.enter_context(nc.sbuf_tensor("rbuf", [128, 1024], F32))
    cbuf = es_all.enter_context(nc.sbuf_tensor("cbuf", [128, 1024], F32))
    x1bf = es_all.enter_context(nc.sbuf_tensor("x1bf", [128, 1024], BF16))
    g_all = es_all.enter_context(nc.sbuf_tensor("g_all", [128, NTB, 8], F32))
    sl = es_all.enter_context(nc.sbuf_tensor("sl", [128, 8], F32))
    gm1 = es_all.enter_context(nc.sbuf_tensor("gm1", [128, 1], F32))
    gm1n = es_all.enter_context(nc.sbuf_tensor("gm1n", [128, 1], F32))
    geq = es_all.enter_context(nc.sbuf_tensor("geq", [128, 8], F32))
    gm2 = es_all.enter_context(nc.sbuf_tensor("gm2", [128, 1], F32))
    gwex = es_all.enter_context(nc.sbuf_tensor("gwex", [128, 8], F32))
    gkeep = es_all.enter_context(nc.sbuf_tensor("gkeep", [128, 8], F32))
    gz = es_all.enter_context(nc.sbuf_tensor("gz", [128, 1], F32))
    lnm = es_all.enter_context(nc.sbuf_tensor("lnm", [128, 1], F32))
    lnv = es_all.enter_context(nc.sbuf_tensor("lnv", [128, 1], F32))
    lnr = es_all.enter_context(nc.sbuf_tensor("lnr", [128, 1], F32))
    cshift = es_all.enter_context(nc.sbuf_tensor("cshift", [128, 1], F32))

    for sname in ("sd", "sg", "sm", "sc", "sp", "sv", "ss", "sq",
                  "so0", "so1", "sw0", "sw1", "sx0", "sx1",
                  "sl0", "sl1", "wb0", "wb1", "u0", "u1", "ub0", "ub1",
                  "rl0", "rl1", "rl2", "rl3", "u2", "u3", "gr0", "gr1", "yd0", "yd1",
                  "tl0", "tl1", "tw0", "tw1", "tl2", "tl3", "tw2", "tw3"):
        sems[sname] = es_all.enter_context(nc.semaphore(sname))

    S.op("gpsimd", "memset", ident_bf[:], 0.0)
    S.edge("gpsimd", "sq")
    S.op("gpsimd", "affine_select", out=ident_bf[:], in_=ident_bf[:],
         compare_op=ALU.not_equal, fill=1.0, base=0, pattern=[[-1, 128]],
         channel_multiplier=1)
    S.op("gpsimd", "memset", ident_f[:], 0.0)
    S.edge("gpsimd", "sq")
    S.op("gpsimd", "affine_select", out=ident_f[:], in_=ident_f[:],
         compare_op=ALU.not_equal, fill=1.0, base=0, pattern=[[-1, 128]],
         channel_multiplier=1)
    S.op("gpsimd", "memset", ones1[:], 1.0)
    S.op("gpsimd", "memset", ones_bf[:], 1.0)
    S.op("gpsimd", "memset", cshift[:], -20.0, inc="sq")
    m_consts = S.cnt["sq"]
    S.op("gpsimd", "memset", bq[:], 0.0)
    S.op("gpsimd", "memset", bk[:], 0.0, inc="sq")
    m_bqbk = S.cnt["sq"]
    S.wait("gpsimd", "sq", m_bqbk)
    S.dma("gpsimd", pmat[:], d["pmat"], "sm")
    for (bmat, c_d, e_d) in ((bq, d["wupqc"], d["wupqe"]),
                             (bk, d["wupkc"], d["wupke"])):
        for pb in (0, 64):
            S.dma("gpsimd", bmat[pb:pb + 64, pb + 0:pb + 32], c_d, "sm")
            S.dma("gpsimd", bmat[pb:pb + 64, pb + 32:pb + 64], e_d, "sm")
    S.dma("gpsimd", wvup[0:64, :], d["wupv"], "sm")
    S.dma("gpsimd", wvup[64:128, :], d["wupv"], "sm")
    m_small_w = S.cnt["sm"]

    es_qkv = ExitStack()
    QT = es_qkv.enter_context(nc.sbuf_tensor("QT", [128, NCT, T], BF16))
    KT = es_qkv.enter_context(nc.sbuf_tensor("KT", [128, NCT, T], BF16))
    es_v = ExitStack()
    V = es_v.enter_context(nc.sbuf_tensor("V", [128, NTB, 16 * 65], BF16, side="right"))
    es_lat = ExitStack()
    A5 = es_lat.enter_context(nc.sbuf_tensor("uA5", [128, NCT, T], BF16, side="right"))
    es_wtb = ExitStack()
    wtb = es_wtb.enter_context(nc.sbuf_tensor("wtb", [128, 4, 1536], BF16,
                                              side="right"))
    es_1 = ExitStack()
    costab = es_1.enter_context(nc.sbuf_tensor("costab", [128, T], F32))
    sintab = es_1.enter_context(nc.sbuf_tensor("sintab", [128, T], F32))
    wstage = es_1.enter_context(nc.sbuf_tensor("wstage", [128, 2, NCT, 128], BF16))

    S.dma("sync", costab[:], d["cos"], "sd")
    S.dma("sync", sintab[:], d["sin"], "sd")
    m_tabs = S.cnt["sd"]

    def _flat(ap, per_p):
        return ap.rearrange("a b c -> (a b c)").rearrange("(p q) -> p q", p=128)

    w12f = d["w12i"].rearrange("e c m h -> (e c) (m h)").rearrange(
        "(p q) r -> p (q r)", p=128)
    w12bf = d["w12b"].rearrange("e c m h -> (e c) (m h)").rearrange(
        "(p q) r -> p (q r)", p=128)
    w3f = d["w3p"].rearrange("e h c -> (e h) c").rearrange(
        "(p q) r -> p (q r)", p=128)
    w3bf = d["w3b"].rearrange("e h c -> (e h) c").rearrange(
        "(p q) r -> p (q r)", p=128)
    tc_jobs = []
    for (fsrc, fdst, total) in ((w12f, w12bf, 64 * 5632), (w3f, w3bf, 176 * 1024)):
        off = 0
        while off < total:
            wdt = min(1536, total - off)
            tc_jobs.append((fsrc[:, off:off + wdt], fdst[:, off:off + wdt], wdt))
            off += wdt
    tc_state = {"i": 0}

    def emit_transcode(n):
        for _ in range(n):
            i = tc_state["i"]
            if i >= len(tc_jobs):
                return
            tc_state["i"] = i + 1
            slot = i % 4
            srcf, dstb, width = tc_jobs[i]
            S.slot_wait("gpsimd", ("wtb_st", slot))
            ml = S.dma("gpsimd", wtb[:, slot, 0:width], srcf, f"tl{slot}")
            S.wait("sync", f"tl{slot}", ml)
            ms = S.dma("sync", dstb, wtb[:, slot, 0:width], f"tw{slot}")
            S.slot_set(("wtb_st", slot), f"tw{slot}", ms)

    S.dma("gpsimd", A1[:], d["xT"].rearrange("(c p) t -> p c t", p=128), "sg")
    m_xt = S.cnt["sg"]

    S.wait("tensor", "sm", m_small_w)
    S.wait("vector", "sd", m_tabs)
    S.wait("vector", "sq", m_consts)

    m_lat_last = {}

    def down_proj(pi, wd, gi0):
        m_wload = {}

        def load(fp):
            gi = gi0 + fp
            S.slot_wait("gpsimd", ("wstage", gi % 2))
            m_wload[fp] = S.dma(
                "gpsimd", wstage[:, gi % 2, :, :],
                wd[:, fp * 128:(fp + 1) * 128].rearrange("(ct p) f -> p ct f",
                                                         p=128), f"sw{gi % 2}")
        load(0)
        load(1)
        last_ev = None
        for fp in range(NCT):
            emit_transcode(1)
            gi = gi0 + fp
            S.wait("tensor", f"sw{gi % 2}", m_wload[fp])
            if pi == 0 and fp == 0:
                S.wait("tensor", "sg", m_xt)
            for tc in range(4):
                b = tc % 2
                S.bank_wait("tensor", b)
                mm = None
                for ct in range(NCT):
                    mm = S.op("tensor", "matmul", pf[b][:, :],
                              lhsT=wstage[:, gi % 2, ct, :],
                              rhs=A1[:, ct, tc * 512:(tc + 1) * 512],
                              start=(ct == 0), stop=(ct == NCT - 1),
                              inc="sp" if ct == NCT - 1 else None)
                if tc == 3:
                    S.slot_set(("wstage", gi % 2), "sp", mm)
                S.wait("vector", "sp", mm)
                last_ev = S.op("vector", "tensor_copy",
                               A5[:, fp, tc * 512:(tc + 1) * 512], pf[b][:, :],
                               inc="sv")
                S.bank_free(b, "sv", last_ev)
            if fp + 2 < NCT:
                load(fp + 2)
        m_lat_last[pi] = last_ev

    def up_proj_rope(pi, bmat, scale):
        S.wait("tensor", "sv", m_lat_last[pi])
        dst = QT if pi == 0 else KT
        qtmp = cbuf[:, 0:512]
        t1b = rbuf[:, 0:512]
        t2b = rbuf[:, 512:1024]
        mdone = None
        for pt in range(NCT):
            for tc in range(4):
                i = pt * 4 + tc
                ub = 2 + (i % 2)
                rb = 4 + (i % 2)
                S.bank_wait("tensor", ub)
                mU = S.op("tensor", "matmul", pf[ub][:, :], lhsT=bmat[:],
                          rhs=A5[:, pt, tc * 512:(tc + 1) * 512],
                          start=True, stop=True, inc="sp")
                S.wait("vector", "sp", mU)
                mq = S.op("vector", "tensor_scalar_mul", qtmp[:], pf[ub][:, :],
                          scale, inc="sv")
                S.bank_free(ub, "sv", mq)
                S.wait("tensor", "sv", mq)
                S.bank_wait("tensor", rb)
                mR = S.op("tensor", "matmul", pf[rb][:, :], lhsT=pmat[:],
                          rhs=qtmp[:], start=True, stop=True, inc="sp")
                S.wait("vector", "sv", mq)
                S.op("vector", "tensor_mul", t1b[:], qtmp[:],
                     costab[:, tc * 512:(tc + 1) * 512])
                S.wait("vector", "sp", mR)
                S.op("vector", "tensor_mul", t2b[:], pf[rb][:, :],
                     sintab[:, tc * 512:(tc + 1) * 512])
                S.edge("vector", "sv")
                mdone = S.op("vector", "tensor_add",
                             dst[:, pt, tc * 512:(tc + 1) * 512], t1b[:], t2b[:],
                             inc="sv")
                S.bank_free(rb, "sv", mdone)
        return mdone

    down_proj(0, d["wq"], 0)
    up_proj_rope(0, bq, SCALE)
    S.wait("vector", "sp", S.cnt["sp"])
    down_proj(1, d["wk"], 8)
    mk_done = up_proj_rope(1, bk, 1.0)
    S.wait("vector", "sp", S.cnt["sp"])
    down_proj(2, d["wv"], 16)

    emit_transcode(6)
    S.op("vector", "memset",
         V[:].rearrange("p t (h c) -> p t h c", c=65)[:, :, :, 64:65], 1.0,
         inc="sv")
    S.wait("tensor", "sv", m_lat_last[2])
    for h in range(NH):
        off = 64 * (h % 2)
        for tb in range(NTB):
            i = h * NTB + tb
            b = 6 + (i % 2)
            S.bank_wait("tensor", b)
            mm = S.op("tensor", "matmul", pf[b][:, 0:64],
                      lhsT=A5[off:off + 64, h // 2, tb * 128:(tb + 1) * 128],
                      rhs=wvup[off:off + 64, :], start=True, stop=True,
                      inc="sp")
            S.wait("vector", "sp", mm)
            mev = S.op("vector", "tensor_copy",
                       V[:, tb, h * 65:h * 65 + 64], pf[b][:, 0:64], inc="sv")
            S.bank_free(b, "sv", mev)
    m_v_done = S.cnt["sv"]
    m_pe_a1 = S.cnt["sp"]

    es_1.close()
    es_2 = ExitStack()
    attTA = es_2.enter_context(nc.sbuf_tensor("attTA", [128, 2, 2048], BF16))
    attTB = attTA
    umask = es_2.enter_context(nc.sbuf_tensor("umask", [128, 4, 512], BF16))
    rzs = es_2.enter_context(nc.sbuf_tensor("rzs", [128, 1, 512], F32))
    bsb = es_2.enter_context(nc.sbuf_tensor("bsb", [128, 1, 512], F32))
    ytmp = es_2.enter_context(nc.sbuf_tensor("ytmp", [128, 2, 512], BF16))

    S.op("gpsimd", "memset", umask[:], 0.0, inc="sq")
    S.wait("gpsimd", "sq", S.cnt["sq"])
    for i in range(4):
        S.op("gpsimd", "affine_select", out=umask[:, i, :], in_=umask[:, i, :],
             compare_op=ALU.is_ge, fill=-30000.0, base=-128 * i,
             pattern=[[1, 512]], channel_multiplier=-1,
             inc="sq" if i == 3 else None)
    m_umask = S.cnt["sq"]
    S.wait("tensor", "sq", m_umask)

    def emit_y(h, qS, g, nkc, m_exp):
        hp, hi = h // 2, h % 2
        attx = attT_bufs[hi]
        gpar = g % 2
        yb = 4 + (h * 4 + qS) % 2
        my = None
        S.wait("tensor", "ss", m_exp[g])
        if g == 0:
            S.bank_wait("tensor", yb)
        for i in range(4):
            kc = g * 4 + i
            my = S.op("tensor", "matmul", pf[yb][0:65, :],
                      lhsT=V[:, kc, 65 * h:65 * h + 65],
                      rhs=attx[:, gpar, i * 512:(i + 1) * 512],
                      start=(kc == 0), stop=(kc == nkc - 1),
                      skip_group_check=True,
                      inc="sp" if i == 3 else None)
        S.slot_set(("attT", gpar), "sp", my)
        return my

    attT_bufs = (attTA, attTB)
    S.wait("tensor", "sv", m_v_done)
    m_ydma = 0
    for h in range(NH):
        hp, hi = h // 2, h % 2
        off = 64 * hi
        for qS in range(4):
            emit_transcode(7)
            nkc = (qS + 1) * 4
            m_exp = {}
            m_ylast = None
            for g in range(qS + 1):
                gpar = g % 2
                for i in range(4):
                    kc = g * 4 + i
                    diag = kc - 4 * qS
                    S.bank_wait("tensor", i)
                    if i == 0:
                        S.op("tensor", "matmul", pf[i][:, :],
                             lhsT=ident_bf[:], rhs=umask[:, 0, :],
                             start=True, stop=True)
                    S.op("tensor", "matmul", pf[i][:, :],
                         lhsT=KT[off:off + 64, hp, kc * 128:(kc + 1) * 128],
                         rhs=QT[off:off + 64, hp, qS * 512:(qS + 1) * 512],
                         start=True, stop=(diag < 0))
                    if diag >= 0:
                        S.op("tensor", "matmul", pf[i][:, :],
                             lhsT=ident_bf[:], rhs=umask[:, diag, :],
                             start=False, stop=True)
                    last = S.prog["tensor"][-1]
                    S.cnt["sp"] = S.cnt.get("sp", 0) + 1
                    mm = S.cnt["sp"]
                    S.prog["tensor"][-1] = (last[0], last[1], last[2],
                                            ("sp", 1))
                if g > 0:
                    m_ylast = emit_y(h, qS, g - 1, nkc, m_exp)
                attx = attT_bufs[hi]
                S.slot_wait("scalar", ("attT", gpar))
                if h == 0 and qS == 0 and g == 0:
                    S.wait("scalar", "sq", m_consts)
                S.wait("scalar", "sp", mm)
                mex = S.op("scalar", "activation", attx[:, gpar, :],
                           pS[:, 0:2048], AF.Exp, bias=cshift[:, 0:1],
                           inc="ss")
                for i in range(4):
                    S.bank_free(i, "ss", mex)
                m_exp[g] = mex
            m_ylast = emit_y(h, qS, qS, nkc, m_exp)
            upar = (h * 4 + qS) % 2
            yb = 4 + upar
            bb = 6 + upar
            S.wait("scalar", "sp", m_ylast)
            S.slot_wait("scalar", ("rzs",))
            S.op("scalar", "activation", rzs[64:65, 0, :],
                 pf[yb][64:65, :], AF.Ln)
            S.edge("scalar", "ss")
            mrz = S.op("scalar", "activation", rzs[64:65, 0, :],
                       rzs[64:65, 0, :], AF.Exp, scale=-1.0, inc="ss")
            S.wait("tensor", "ss", mrz)
            S.bank_wait("tensor", bb)
            mbb = S.op("tensor", "matmul", pf[bb][0:64, :],
                       lhsT=ones1[64:65, 0:64], rhs=rzs[64:65, 0, :],
                       start=True, stop=True, inc="sp")
            S.slot_set(("rzs",), "sp", mbb)
            S.wait("vector", "sp", mbb)
            S.slot_wait("vector", ("bsb",))
            mbe = S.op("vector", "tensor_copy", bsb[0:64, 0, :],
                       pf[bb][0:64, :], inc="sv")
            S.bank_free(bb, "sv", mbe)
            S.wait("vector", "sv", mbe)
            if hi == 0:
                mtt = S.op("vector", "tensor_mul",
                           A5[0:64, hp, qS * 512:(qS + 1) * 512],
                           pf[yb][0:64, :], bsb[0:64, 0, :], inc="sv")
            else:
                S.slot_wait("vector", ("ytmp", qS % 2))
                mtt = S.op("vector", "tensor_mul", ytmp[0:64, qS % 2, :],
                           pf[yb][0:64, :], bsb[0:64, 0, :], inc="sv")
                S.wait("sync", "sv", mtt)
                m_ydma = S.dma("sync", A5[64:128, hp,
                                          qS * 512:(qS + 1) * 512],
                               ytmp[0:64, qS % 2, :], f"yd{qS % 2}")
                S.slot_set(("ytmp", qS % 2), f"yd{qS % 2}", m_ydma)
            S.slot_set(("bsb",), "sv", mtt)
            S.bank_free(yb, "sv", mtt)
    emit_transcode(len(tc_jobs))
    m_yT_done = S.cnt["sv"]
    m_yT_dma = (S.cnt.get("yd0", 0), S.cnt.get("yd1", 0))

    es_2.close()
    es_wtb.close()
    es_qkv.close()
    es_oac = ExitStack()
    oac = es_oac.enter_context(nc.sbuf_tensor("oac", [128, NTB, 1024], F32))
    wcb = es_oac.enter_context(nc.sbuf_tensor("wcb", [128, NCT, C], BF16))
    xres = es_oac.enter_context(nc.sbuf_tensor("xres", [128, 2, 1024], F32))
    wgb = es_oac.enter_context(nc.sbuf_tensor("wgb", [128, NCT, 8], BF16))

    m_pe_a2 = S.cnt["sp"]
    S.wait("gpsimd", "sp", m_pe_a2)
    S.wait("gpsimd", "ss", S.cnt["ss"])
    S.wait("gpsimd", "sv", m_yT_done)
    m_wcb = S.dma("gpsimd", wcb[:],
                  d["wc"].rearrange("(ct p) f -> p ct f", p=128), "sc")
    m_wgb = S.dma("gpsimd", wgb[:],
                  d["wgT"].rearrange("(ct p) e -> p ct e", p=128), "sm")

    m_ln = {}
    m_x1T = {}
    S.wait("tensor", "sc", m_wcb)
    S.wait("tensor", "sv", m_yT_done)
    S.wait("tensor", "yd0", m_yT_dma[0])
    S.wait("tensor", "yd1", m_yT_dma[1])
    S.wait("sync", "sp", m_pe_a2)
    S.wait("sync", "ss", S.cnt["ss"])
    S.wait("sync", "sv", m_yT_done)
    S.wait("sync", "yd0", m_yT_dma[0])
    S.wait("sync", "yd1", m_yT_dma[1])
    for _s in ("tw0", "tw1", "tw2", "tw3"):
        S.wait("sync", _s, S.cnt.get(_s, 0))
    for tb in range(NTB):
        S.slot_wait("sync", ("xres", tb % 2))
        m_xr = S.dma("sync", xres[:, tb % 2, :],
                     d["x"][tb * 128:(tb + 1) * 128, :], f"sx{tb % 2}")
        mWc = {}
        for fh in range(2):
            b = 2 * (tb % 2) + fh
            S.bank_wait("tensor", b)
            mm = None
            for ct in range(NCT):
                mm = S.op("tensor", "matmul", pf[b][:, :],
                          lhsT=A5[:, ct, tb * 128:(tb + 1) * 128],
                          rhs=wcb[:, ct, fh * 512:(fh + 1) * 512],
                          start=(ct == 0), stop=(ct == NCT - 1),
                          inc="sp" if ct == NCT - 1 else None)
            mWc[fh] = mm
        S.wait("vector", "sp", mWc[1])
        S.wait("vector", f"sx{tb % 2}", m_xr)
        S.op("vector", "tensor_add", rbuf[:, 0:512],
             pf[2 * (tb % 2)][:, :], xres[:, tb % 2, 0:512])
        mpr = S.op("vector", "tensor_add", rbuf[:, 512:1024],
                   pf[2 * (tb % 2) + 1][:, :], xres[:, tb % 2, 512:1024],
                   inc="sv")
        S.bank_free(2 * (tb % 2), "sv", mpr)
        S.bank_free(2 * (tb % 2) + 1, "sv", mpr)
        S.slot_set(("xres", tb % 2), "sv", mpr)
        S.wait("vector", "sv", mpr)
        S.op("vector", "reduce_sum", lnm[:], rbuf[:], axis=AX, negate=True)
        S.edge("vector", "sv")
        S.op("vector", "tensor_scalar_mul", lnm[:], lnm[:], 1.0 / 1024.0)
        S.edge("vector", "sv")
        S.op("vector", "tensor_scalar_add", cbuf[:], rbuf[:], lnm[:, 0:1])
        S.edge("vector", "sv")
        S.op("vector", "scalar_tensor_tensor", rbuf[:], in0=cbuf[:], scalar=1.0,
             in1=cbuf[:], op0=ALU.mult, op1=ALU.mult, accum_out=lnv[:, 0:1])
        S.edge("vector", "sv")
        mvv = S.op("vector", "tensor_scalar", lnv[:], lnv[:], 1.0 / 1024.0, EPS,
                   op0=ALU.mult, op1=ALU.add, inc="sv")
        S.wait("scalar", "sv", mvv)
        S.op("scalar", "activation", lnv[:], lnv[:], AF.Ln)
        S.edge("scalar", "ss")
        S.op("scalar", "activation", lnr[:], lnv[:], AF.Exp, scale=-0.5)
        mrstd = S.edge("scalar", "ss")
        S.wait("vector", "ss", mrstd)
        S.op("vector", "tensor_scalar_mul", oac[:, tb, :], cbuf[:], lnr[:, 0:1])
        mx1b = S.op("vector", "tensor_scalar_mul", x1bf[:], cbuf[:],
                    lnr[:, 0:1], inc="sv")
        m_ln[tb] = mx1b
        S.wait("tensor", "sv", mx1b)
        for half in range(2):
            b = 6 + half
            S.bank_wait("tensor", b)
            mt = None
            for ci in range(4):
                ct = half * 4 + ci
                mt = S.op("tensor", "matmul", pf[b][:, ci * 128:(ci + 1) * 128],
                          lhsT=x1bf[:, ct * 128:(ct + 1) * 128],
                          rhs=ident_bf[:], start=True, stop=True,
                          inc="sp" if ci == 3 else None)
            S.wait("vector", "sp", mt)
            mev = S.op("vector", "tensor_copy",
                       A1[:, half * 4:half * 4 + 4, tb * 128:(tb + 1) * 128],
                       pf[b][:, :].rearrange("p (c q) -> p c q", c=4),
                       inc="sv")
            S.bank_free(b, "sv", mev)
        m_x1T[tb] = mev

    m_g = {}
    S.wait("tensor", "sm", m_wgb)
    for tb in range(NTB):
        S.wait("tensor", "sv", m_x1T[tb])
        S.bank_wait("tensor", 5)
        mm = None
        for ct in range(NCT):
            mm = S.op("tensor", "matmul", pf[5][:, 0:8],
                      lhsT=A1[:, ct, tb * 128:(tb + 1) * 128],
                      rhs=wgb[:, ct, :], start=(ct == 0), stop=(ct == NCT - 1),
                      inc="sp" if ct == NCT - 1 else None)
        S.wait("vector", "sp", mm)
        if tb > 0:
            S.wait("vector", "ss", m_sl_free)
        mev = S.op("vector", "tensor_copy", sl[:], pf[5][:, 0:8], inc="sv")
        S.bank_free(5, "sv", mev)
        S.wait("vector", "sv", mev)
        S.op("vector", "reduce_max", gm1[:], sl[:], axis=AX)
        S.edge("vector", "sv")
        S.op("vector", "tensor_scalar_mul", gm1n[:], gm1[:], -1.0)
        S.op("vector", "tensor_scalar", geq[:], sl[:], gm1[:, 0:1], None,
             op0=ALU.is_ge)
        S.edge("vector", "sv")
        S.op("vector", "scalar_tensor_tensor", geq[:], in0=geq[:], scalar=-1e30,
             in1=sl[:], op0=ALU.mult, op1=ALU.add)
        S.edge("vector", "sv")
        S.op("vector", "reduce_max", gm2[:], geq[:], axis=AX)
        mgv = S.edge("vector", "sv")
        S.wait("scalar", "sv", mgv)
        msx = S.op("scalar", "activation", gwex[:], sl[:], AF.Exp,
                   bias=gm1n[:, 0:1], inc="ss")
        m_sl_free = msx
        S.op("vector", "tensor_scalar", gkeep[:], sl[:], gm2[:, 0:1], None,
             op0=ALU.is_ge)
        S.edge("vector", "sv")
        S.wait("vector", "ss", msx)
        S.op("vector", "tensor_mul", gkeep[:], gkeep[:], gwex[:])
        S.edge("vector", "sv")
        S.op("vector", "reduce_sum", gz[:], gkeep[:], axis=AX)
        S.edge("vector", "sv")
        S.op("vector", "reciprocal", gz[:], gz[:])
        S.edge("vector", "sv")
        m_g[tb] = S.op("vector", "tensor_scalar_mul", g_all[:, tb, :],
                       gkeep[:], gz[:, 0:1], inc="sv")

    es_lat.close()
    es_v.close()
    es_f = ExitStack()
    gTrow = es_f.enter_context(nc.sbuf_tensor("gTrow", [8, NTB, 128], BF16, side="right"))
    gRow = es_f.enter_context(nc.sbuf_tensor("gRow", [1, 2, 512], BF16, side="right"))
    hiddenG = es_f.enter_context(nc.sbuf_tensor("hiddenG", [128, NHT, 512], BF16, side="right"))
    w12s = es_f.enter_context(nc.sbuf_tensor("w12s", [128, 4, NCT, 2, 128], BF16, side="right"))
    w3s = es_f.enter_context(nc.sbuf_tensor("w3s", [128, 2, 1024], BF16, side="right"))
    x1te = es_f.enter_context(nc.sbuf_tensor("x1te", [128, 2, NCT, 512], BF16))
    gBt = es_f.enter_context(nc.sbuf_tensor("gBt", [128, 2, 512], BF16, side="right"))
    s1b = es_f.enter_context(nc.sbuf_tensor("s1b", [128, 2, 512], BF16, side="right"))
    outb = es_f.enter_context(nc.sbuf_tensor("outb", [128, 1, 1024], F32))

    for tb in range(NTB):
        S.wait("tensor", "sv", m_g[tb])
        S.bank_wait("tensor", 4)
        mt = S.op("tensor", "matmul", pf[4][0:8, 0:128],
                  lhsT=g_all[:, tb, :], rhs=ident_f[:], start=True, stop=True,
                  inc="sp")
        S.wait("vector", "sp", mt)
        mev = S.op("vector", "tensor_copy", gTrow[0:8, tb, :],
                   pf[4][0:8, 0:128], inc="sv")
        S.bank_free(4, "sv", mev)
    m_gT = S.cnt["sv"]
    m_pe_a3 = S.cnt["sp"]

    m_w12 = {}
    m_w3 = {}
    m_wb = {}
    m_x1te = {}
    m_oadd = {}

    first_ffn_dma = [True]
    for _q in ("sync", "gpsimd"):
        for _s in ("tw0", "tw1", "tw2", "tw3"):
            S.wait(_q, _s, S.cnt.get(_s, 0))
    S.wait("sync", "sp", m_pe_a3)
    S.wait("gpsimd", "sp", m_pe_a3)

    def w12_load(e, ts, ht):
        buf = ht % 4
        S.slot_wait("sync", ("w12s", buf))
        srcb = d["w12b"][e, :, 2 * ht:2 * ht + 2, :].rearrange(
            "(ct p) m h -> p ct (m h)", p=128)
        dstb = w12s[:, buf].rearrange("p ct m h -> p ct (m h)")
        m_w12[(e, ts, ht)] = S.dma("sync", dstb, srcb, f"rl{buf}")

    def w3_load(e, ts, ht):
        buf = ht % 2
        S.slot_wait("gpsimd", ("w3s", buf))
        m_w3[(e, ts, ht)] = S.dma(
            "gpsimd", w3s[:, buf, :],
            d["w3b"][e, ht * 128:(ht + 1) * 128, :], f"u{buf}")

    def prep_x1te(e, ts):
        par = ts % 2
        b = 4 + par
        if e == 0 and ts == 0:
            S.wait("gpsimd", "sv", m_gT)
        S.slot_wait("gpsimd", ("gRow", par))
        m_grow = S.dma("gpsimd", gRow[0:1, par, :],
                       gTrow[e:e + 1, ts * 4:(ts + 1) * 4, :].rearrange(
                           "r b q -> r (b q)"), f"gr{par}")
        S.wait("tensor", f"gr{par}", m_grow)
        S.bank_wait("tensor", b)
        mgbm = S.op("tensor", "matmul", pf[b][:, :], lhsT=ones_bf[0:1, :],
                    rhs=gRow[0:1, par, :], start=True, stop=True, inc="sp")
        S.slot_set(("gRow", par), "sp", mgbm)
        S.wait("vector", "sp", mgbm)
        S.slot_wait("vector", ("gBt", par))
        mgb = S.op("vector", "tensor_copy", gBt[:, par, :], pf[b][:, :],
                   inc="sv")
        S.bank_free(b, "sv", mgb)
        S.wait("vector", "sv", mgb)
        S.slot_wait("vector", ("x1te", par))
        mte = None
        for ct in range(NCT):
            mte = S.op("vector", "tensor_mul", x1te[:, par, ct, :],
                       A1[:, ct, ts * 512:(ts + 1) * 512], gBt[:, par, :],
                       inc="sv" if ct == NCT - 1 else None)
        S.slot_set(("gBt", par), "sv", mte)
        m_x1te[(e, ts)] = mte

    for e in range(NE):
        for ts in range(4):
            if ts == 0:
                prep_x1te(e, 0)
            w3_load(e, ts, 0)
            w3_load(e, ts, 1)
            mhg = {}
            for ht in range(NHT):
                buf = ht % 4
                w12_load(e, ts, ht)
                S.wait("tensor", f"rl{buf}", m_w12[(e, ts, ht)])
                if ht == 0:
                    S.wait("tensor", "sv", m_x1te[(e, ts)])
                b1 = buf
                b2 = 2 + buf
                S.bank_wait("tensor", b1)
                mm1 = None
                for ct in range(NCT):
                    mm1 = S.op("tensor", "matmul", pf[b1][:, :],
                               lhsT=w12s[:, buf, ct, 0, :],
                               rhs=A1[:, ct, ts * 512:(ts + 1) * 512],
                               start=(ct == 0), stop=(ct == NCT - 1),
                               inc="sp" if ct == NCT - 1 else None)
                S.bank_wait("tensor", b2)
                mm2 = None
                for ct in range(NCT):
                    mm2 = S.op("tensor", "matmul", pf[b2][:, :],
                               lhsT=w12s[:, buf, ct, 1, :],
                               rhs=x1te[:, ts % 2, ct, :],
                               start=(ct == 0), stop=(ct == NCT - 1),
                               inc="sp" if ct == NCT - 1 else None)
                S.slot_set(("w12s", buf), "sp", mm2)
                if ht == NHT - 1:
                    m_h2_done = mm2
                S.wait("scalar", "sp", mm1)
                S.slot_wait("scalar", ("s1b", ht % 2))
                ms1 = S.op("scalar", "activation", s1b[:, ht % 2, :],
                           pf[b1][:, :], AF.Silu, inc="ss")
                S.bank_free(b1, "ss", ms1)
                S.wait("vector", "ss", ms1)
                S.wait("vector", "sp", mm2)
                mhg[ht] = S.op("vector", "tensor_mul", hiddenG[:, ht, :],
                               s1b[:, ht % 2, :], pf[b2][:, :], inc="sv")
                S.slot_set(("s1b", ht % 2), "sv", mhg[ht])
                S.bank_free(b2, "sv", mhg[ht])
            S.slot_set(("x1te", ts % 2), "sp", m_h2_done)
            if ts < 3:
                prep_x1te(e, ts + 1)
            elif e < NE - 1:
                prep_x1te(e + 1, 0)
            mlast = {}
            for ht in range(NHT):
                buf = ht % 2
                if ht >= 2:
                    pass
                S.wait("tensor", f"u{buf}", m_w3[(e, ts, ht)])
                if ht == 0:
                    S.wait("tensor", "sv", mhg[NHT - 1])
                    for b in range(8):
                        S.bank_wait("tensor", b)
                for tbq in range(4):
                    for dhh in range(2):
                        b = tbq * 2 + dhh
                        mm = S.op("tensor", "matmul", pf[b][:, :],
                                  lhsT=hiddenG[:, ht, tbq * 128:(tbq + 1) * 128],
                                  rhs=w3s[:, buf, dhh * 512:(dhh + 1) * 512],
                                  start=(ht == 0), stop=(ht == NHT - 1),
                                  skip_group_check=True,
                                  inc="sp" if (tbq == 3 and dhh == 1) else None)
                S.slot_set(("w3s", buf), "sp", mm)
                if ht + 2 < NHT:
                    w3_load(e, ts, ht + 2)
            m_w3_all = S.cnt["sp"]
            mev = None
            for tbq in range(4):
                for dhh in range(2):
                    b = tbq * 2 + dhh
                    S.wait("vector", "sp", m_w3_all)
                    mev = S.op("vector", "tensor_add",
                               oac[:, ts * 4 + tbq, dhh * 512:(dhh + 1) * 512],
                               pf[b][:, :],
                               oac[:, ts * 4 + tbq, dhh * 512:(dhh + 1) * 512],
                               inc="sv")
                    S.bank_free(b, "sv", mev)
            m_oadd[(e, ts)] = mev
    m_ffn_done = S.cnt["sv"]

    for tb in range(NTB):
        if tb == 0:
            S.wait("vector", "sv", m_ffn_done)
        S.slot_wait("vector", ("outb",))
        S.op("vector", "reduce_sum", lnm[:], oac[:, tb, :], axis=AX, negate=True)
        S.edge("vector", "sv")
        S.op("vector", "tensor_scalar_mul", lnm[:], lnm[:], 1.0 / 1024.0)
        S.edge("vector", "sv")
        S.op("vector", "tensor_scalar_add", cbuf[:], oac[:, tb, :], lnm[:, 0:1])
        S.edge("vector", "sv")
        S.op("vector", "scalar_tensor_tensor", rbuf[:], in0=cbuf[:], scalar=1.0,
             in1=cbuf[:], op0=ALU.mult, op1=ALU.mult, accum_out=lnv[:, 0:1])
        S.edge("vector", "sv")
        mvv = S.op("vector", "tensor_scalar", lnv[:], lnv[:], 1.0 / 1024.0, EPS,
                   op0=ALU.mult, op1=ALU.add, inc="sv")
        S.wait("scalar", "sv", mvv)
        S.op("scalar", "activation", lnv[:], lnv[:], AF.Ln)
        S.edge("scalar", "ss")
        S.op("scalar", "activation", lnr[:], lnv[:], AF.Exp, scale=-0.5)
        mrstd = S.edge("scalar", "ss")
        S.wait("vector", "ss", mrstd)
        mo = S.op("vector", "tensor_scalar_mul", outb[:, 0, :], cbuf[:],
                  lnr[:, 0:1], inc="sv")
        S.wait("sync", "sv", mo)
        mdma = S.dma("sync", d["out"][tb * 128:(tb + 1) * 128, :],
                     outb[:, 0, :], f"so{tb % 2}")
        S.slot_set(("outb",), f"so{tb % 2}", mdma)
    S.wait("sync", "so0", S.cnt["so0"])
    S.wait("sync", "so1", S.cnt["so1"])

    with nc.Block() as block:
        def _mk(eng_name):
            def runner(engine):
                for meth, a, k, incv in S.prog[eng_name]:
                    if meth == "wait_ge":
                        engine.wait_ge(sems[a[0]], a[1])
                        continue
                    inst = getattr(engine, meth)(*a, **k)
                    if incv is not None:
                        inst.then_inc(sems[incv[0]], incv[1])
            return runner
        block.sync(_mk("sync"))
        block.gpsimd(_mk("gpsimd"))
        block.tensor(_mk("tensor"))
        block.vector(_mk("vector"))
        block.scalar(_mk("scalar"))

    es_f.close()
    es_oac.close()
    es_all.close()
    return nc


_CACHE = {}


def kernel(**inputs):
    def A(name):
        return np.ascontiguousarray(np.asarray(inputs[name], dtype=np.float32))

    x = A("L0_attn_x")
    wq = A("L0_attn_Wq_down")
    wk = A("L0_attn_Wk_down")
    wv = A("L0_attn_Wv_down")
    wc = A("L0_attn_Wc")
    wupqc = A("L0_attn_Wq_up_c")
    wupqe = A("L0_attn_Wq_up_e")
    wupkc = A("L0_attn_Wk_up_c")
    wupke = A("L0_attn_Wk_up_e")
    wupv = A("L0_attn_Wv_up")
    wg = A("L0_ffn_Wg")
    w1 = A("L0_ffn_W1")
    w2 = A("L0_ffn_W2")
    w3 = A("L0_ffn_W3")

    wgT = np.ascontiguousarray(wg.T)
    w1t = np.zeros((NE, C, HIDP), np.float32)
    w1t[:, :, :2730] = w1.transpose(0, 2, 1)
    w2t = np.zeros((NE, C, HIDP), np.float32)
    w2t[:, :, :2730] = w2.transpose(0, 2, 1)
    w12i = np.zeros((NE, C, 2 * NHT, 128), np.float32)
    w12i[:, :, 0::2, :] = w1t.reshape(NE, C, NHT, 128)
    w12i[:, :, 1::2, :] = w2t.reshape(NE, C, NHT, 128)
    w12i = np.ascontiguousarray(w12i)
    w3p = np.zeros((NE, HIDP, C), np.float32)
    w3p[:, :2730, :] = w3.transpose(0, 2, 1)
    w3p = np.ascontiguousarray(w3p)
    cos, sin, pm = _host_consts()

    if "nc" not in _CACHE:
        _CACHE["nc"] = build()
    nc = _CACHE["nc"]

    shared = dict(wq=wq, wk=wk, wv=wv, wc=wc, wupqc=wupqc, wupqe=wupqe,
                  wupkc=wupkc, wupke=wupke, wupv=wupv, wgT=wgT,
                  w12i=w12i, w3p=w3p, cos=cos, sin=sin, pmat=pm)
    in_maps = []
    for b in range(8):
        m = dict(shared)
        m["x"] = np.ascontiguousarray(x[b])
        m["xT"] = np.ascontiguousarray(x[b].T)
        in_maps.append(m)

    res = run_bass_kernel_spmd(nc, in_maps, core_ids=list(range(8)))
    return np.stack([r["out"] for r in res.results], axis=0)


# version 14
# speedup vs baseline: 1.2116x; 1.0028x over previous
import numpy as np
import ml_dtypes
import concourse.bass as bass
import concourse.mybir as mybir
from concourse.bass_utils import run_bass_kernel_spmd
from contextlib import ExitStack

F32 = mybir.dt.float32
BF16 = mybir.dt.bfloat16
AX = mybir.AxisListType.X
ALU = mybir.AluOpType
AF = mybir.ActivationFunctionType

T = 2048
C = 1024
NCT = 8
NH = 16
NTB = 16
NTS = 4
NE = 8
NHT = 22
HIDP = 2816
EPS = 1e-5
SCALE = 0.125

ENG = ("sync", "gpsimd", "tensor", "vector", "scalar")


class Sched:
    def __init__(self):
        self.prog = {e: [] for e in ENG}
        self.cnt = {}
        self.bank = {}
        self.slot = {}

    def op(self, eng, meth, *a, inc=None, incn=1, **k):
        m = None
        if inc is not None:
            self.cnt[inc] = self.cnt.get(inc, 0) + incn
            m = self.cnt[inc]
        self.prog[eng].append((meth, a, k, (inc, incn) if inc else None))
        return m

    def dma(self, eng, out, in_, inc, **k):
        return self.op(eng, "dma_start", inc=inc, incn=16, out=out, in_=in_, **k)

    def wait(self, eng, sem, val):
        if val:
            self.prog[eng].append(("wait_ge", (sem, val), {}, None))

    def edge(self, eng, sem):
        last = self.prog[eng][-1]
        if last[0] == "wait_ge":
            return self.cnt.get(sem, 0)
        if last[3] is not None:
            s2 = last[3][0]
            m = self.cnt[s2]
            self.prog[eng].append(("wait_ge", (s2, m), {}, None))
            return m
        self.cnt[sem] = self.cnt.get(sem, 0) + 1
        m = self.cnt[sem]
        self.prog[eng][-1] = (last[0], last[1], last[2], (sem, 1))
        self.prog[eng].append(("wait_ge", (sem, m), {}, None))
        return m

    def bank_wait(self, eng, b):
        if b in self.bank:
            self.wait(eng, *self.bank[b])

    def bank_free(self, b, sem, val):
        self.bank[b] = (sem, val)

    def slot_wait(self, eng, key):
        if key in self.slot:
            self.wait(eng, *self.slot[key])

    def slot_set(self, key, sem, val):
        self.slot[key] = (sem, val)


def _host_consts():
    theta = 10000.0
    inv = 1.0 / theta ** (np.arange(0, 32, 2, dtype=np.float64) / 32.0)
    tt = np.arange(T, dtype=np.float64)[:, None] * inv[None, :]
    emb = np.concatenate([tt, tt], axis=1)
    cos = np.zeros((128, T), dtype=np.float32)
    sin = np.zeros((128, T), dtype=np.float32)
    for base in (0, 64):
        cos[base:base + 32, :] = 1.0
        cos[base + 32:base + 64, :] = np.cos(emb).T.astype(np.float32)
        sin[base + 32:base + 64, :] = np.sin(emb).T.astype(np.float32)
    pm = np.zeros((128, 128), dtype=np.float32)
    for base in (32, 96):
        for i in range(16):
            pm[base + 2 * i + 1, base + 2 * i] = -1.0
            pm[base + 2 * i, base + 2 * i + 1] = 1.0
    return cos, sin, pm


def build():
    nc = bass.Bass("TRN2", target_bir_lowering=False)
    S = Sched()

    d = {}
    d["x"] = nc.dram_tensor("x", [T, C], F32, kind="ExternalInput").ap()
    d["xT"] = nc.dram_tensor("xT", [C, T], F32, kind="ExternalInput").ap()
    for n in ("wq", "wk", "wv", "wc"):
        d[n] = nc.dram_tensor(n, [C, C], F32, kind="ExternalInput").ap()
    for n in ("wupqc", "wupqe", "wupkc", "wupke"):
        d[n] = nc.dram_tensor(n, [64, 32], F32, kind="ExternalInput").ap()
    d["wupv"] = nc.dram_tensor("wupv", [64, 64], F32, kind="ExternalInput").ap()
    d["wgT"] = nc.dram_tensor("wgT", [C, NE], F32, kind="ExternalInput").ap()
    d["w12i"] = nc.dram_tensor("w12i", [NE, C, 2 * NHT, 128], F32,
                               kind="ExternalInput").ap()
    d["w3p"] = nc.dram_tensor("w3p", [NE, HIDP, C], F32, kind="ExternalInput").ap()
    d["cos"] = nc.dram_tensor("cos", [128, T], F32, kind="ExternalInput").ap()
    d["sin"] = nc.dram_tensor("sin", [128, T], F32, kind="ExternalInput").ap()
    d["pmat"] = nc.dram_tensor("pmat", [128, 128], F32, kind="ExternalInput").ap()
    d["out"] = nc.dram_tensor("out", [T, C], F32, kind="ExternalOutput").ap()
    d["w12b"] = nc.dram_tensor("w12b", [NE, C, 2 * NHT, 128], BF16,
                               kind="Internal").ap()
    d["w3b"] = nc.dram_tensor("w3b", [NE, HIDP, C], BF16, kind="Internal").ap()

    es_all = ExitStack()
    sems = {}
    pS = es_all.enter_context(nc.psum_tensor("pS", [128, 2048], F32))
    pxs = [es_all.enter_context(nc.psum_tensor(f"px{i}", [128, 512], F32))
           for i in range(4)]

    class PSlice:
        def __init__(self, h, off):
            self.h, self.off = h, off

        def __getitem__(self, idx):
            ps, fs = idx
            a = fs.start or 0
            b = fs.stop if fs.stop is not None else 512
            return self.h[ps, self.off + a:self.off + b]

    pf = [PSlice(pS, i * 512) for i in range(4)] + \
         [PSlice(p, 0) for p in pxs]

    A1 = es_all.enter_context(nc.sbuf_tensor("uA1", [128, NCT, T], BF16))
    ident_bf = es_all.enter_context(nc.sbuf_tensor("ident_bf", [128, 128], BF16))
    ident_f = es_all.enter_context(nc.sbuf_tensor("ident_f", [128, 128], F32))
    pmat = es_all.enter_context(nc.sbuf_tensor("pmat_s", [128, 128], F32))
    bq = es_all.enter_context(nc.sbuf_tensor("bq", [128, 128], BF16))
    bk = es_all.enter_context(nc.sbuf_tensor("bk", [128, 128], BF16))
    ones_bf = es_all.enter_context(nc.sbuf_tensor("ones_bf", [1, 128], BF16))
    wvup = es_all.enter_context(nc.sbuf_tensor("wvup", [128, 64], BF16))
    ones1 = es_all.enter_context(nc.sbuf_tensor("ones1", [128, 128], F32))
    rbuf = es_all.enter_context(nc.sbuf_tensor("rbuf", [128, 1024], F32))
    cbuf = es_all.enter_context(nc.sbuf_tensor("cbuf", [128, 1024], F32))
    x1bf = es_all.enter_context(nc.sbuf_tensor("x1bf", [128, 1024], BF16))
    g_all = es_all.enter_context(nc.sbuf_tensor("g_all", [128, NTB, 8], F32))
    sl = es_all.enter_context(nc.sbuf_tensor("sl", [128, 8], F32))
    gm1 = es_all.enter_context(nc.sbuf_tensor("gm1", [128, 1], F32))
    gm1n = es_all.enter_context(nc.sbuf_tensor("gm1n", [128, 1], F32))
    geq = es_all.enter_context(nc.sbuf_tensor("geq", [128, 8], F32))
    gm2 = es_all.enter_context(nc.sbuf_tensor("gm2", [128, 1], F32))
    gwex = es_all.enter_context(nc.sbuf_tensor("gwex", [128, 8], F32))
    gkeep = es_all.enter_context(nc.sbuf_tensor("gkeep", [128, 8], F32))
    gz = es_all.enter_context(nc.sbuf_tensor("gz", [128, 1], F32))
    lnm = es_all.enter_context(nc.sbuf_tensor("lnm", [128, 1], F32))
    lnv = es_all.enter_context(nc.sbuf_tensor("lnv", [128, 1], F32))
    lnr = es_all.enter_context(nc.sbuf_tensor("lnr", [128, 1], F32))
    cshift = es_all.enter_context(nc.sbuf_tensor("cshift", [128, 1], F32))

    for sname in ("sd", "sg", "sm", "sc", "sp", "sv", "ss", "sq",
                  "so0", "so1", "sw0", "sw1", "sx0", "sx1",
                  "sl0", "sl1", "wb0", "wb1", "u0", "u1", "ub0", "ub1",
                  "rl0", "rl1", "rl2", "rl3", "u2", "u3", "gr0", "gr1", "yd0", "yd1",
                  "tl0", "tl1", "tw0", "tw1", "tl2", "tl3", "tw2", "tw3"):
        sems[sname] = es_all.enter_context(nc.semaphore(sname))

    S.op("gpsimd", "memset", ident_bf[:], 0.0)
    S.edge("gpsimd", "sq")
    S.op("gpsimd", "affine_select", out=ident_bf[:], in_=ident_bf[:],
         compare_op=ALU.not_equal, fill=1.0, base=0, pattern=[[-1, 128]],
         channel_multiplier=1)
    S.op("gpsimd", "memset", ident_f[:], 0.0)
    S.edge("gpsimd", "sq")
    S.op("gpsimd", "affine_select", out=ident_f[:], in_=ident_f[:],
         compare_op=ALU.not_equal, fill=1.0, base=0, pattern=[[-1, 128]],
         channel_multiplier=1)
    S.op("gpsimd", "memset", ones1[:], 1.0)
    S.op("gpsimd", "memset", ones_bf[:], 1.0)
    S.op("gpsimd", "memset", cshift[:], -20.0, inc="sq")
    m_consts = S.cnt["sq"]
    S.op("gpsimd", "memset", bq[:], 0.0)
    S.op("gpsimd", "memset", bk[:], 0.0, inc="sq")
    m_bqbk = S.cnt["sq"]
    S.wait("gpsimd", "sq", m_bqbk)
    S.dma("gpsimd", pmat[:], d["pmat"], "sm")
    for (bmat, c_d, e_d) in ((bq, d["wupqc"], d["wupqe"]),
                             (bk, d["wupkc"], d["wupke"])):
        for pb in (0, 64):
            S.dma("gpsimd", bmat[pb:pb + 64, pb + 0:pb + 32], c_d, "sm")
            S.dma("gpsimd", bmat[pb:pb + 64, pb + 32:pb + 64], e_d, "sm")
    S.dma("gpsimd", wvup[0:64, :], d["wupv"], "sm")
    S.dma("gpsimd", wvup[64:128, :], d["wupv"], "sm")
    m_small_w = S.cnt["sm"]

    es_qkv = ExitStack()
    QT = es_qkv.enter_context(nc.sbuf_tensor("QT", [128, NCT, T], BF16))
    KT = es_qkv.enter_context(nc.sbuf_tensor("KT", [128, NCT, T], BF16))
    es_v = ExitStack()
    V = es_v.enter_context(nc.sbuf_tensor("V", [128, NTB, 16 * 65], BF16, side="right"))
    es_lat = ExitStack()
    A5 = es_lat.enter_context(nc.sbuf_tensor("uA5", [128, NCT, T], BF16, side="right"))
    es_wtb = ExitStack()
    wtb = es_wtb.enter_context(nc.sbuf_tensor("wtb", [128, 4, 1536], BF16,
                                              side="right"))
    es_1 = ExitStack()
    costab = es_1.enter_context(nc.sbuf_tensor("costab", [128, T], F32))
    sintab = es_1.enter_context(nc.sbuf_tensor("sintab", [128, T], F32))
    wstage = es_1.enter_context(nc.sbuf_tensor("wstage", [128, 2, NCT, 128], BF16))

    S.dma("sync", costab[:], d["cos"], "sd")
    S.dma("sync", sintab[:], d["sin"], "sd")
    m_tabs = S.cnt["sd"]

    def _flat(ap, per_p):
        return ap.rearrange("a b c -> (a b c)").rearrange("(p q) -> p q", p=128)

    w12f = d["w12i"].rearrange("e c m h -> (e c) (m h)").rearrange(
        "(p q) r -> p (q r)", p=128)
    w12bf = d["w12b"].rearrange("e c m h -> (e c) (m h)").rearrange(
        "(p q) r -> p (q r)", p=128)
    w3f = d["w3p"].rearrange("e h c -> (e h) c").rearrange(
        "(p q) r -> p (q r)", p=128)
    w3bf = d["w3b"].rearrange("e h c -> (e h) c").rearrange(
        "(p q) r -> p (q r)", p=128)
    tc_jobs = []
    for (fsrc, fdst, total) in ((w12f, w12bf, 64 * 5632), (w3f, w3bf, 176 * 1024)):
        off = 0
        while off < total:
            wdt = min(1536, total - off)
            tc_jobs.append((fsrc[:, off:off + wdt], fdst[:, off:off + wdt], wdt))
            off += wdt
    tc_state = {"i": 0}

    def emit_transcode(n):
        for _ in range(n):
            i = tc_state["i"]
            if i >= len(tc_jobs):
                return
            tc_state["i"] = i + 1
            slot = i % 4
            srcf, dstb, width = tc_jobs[i]
            S.slot_wait("gpsimd", ("wtb_st", slot))
            ml = S.dma("gpsimd", wtb[:, slot, 0:width], srcf, f"tl{slot}")
            S.wait("sync", f"tl{slot}", ml)
            ms = S.dma("sync", dstb, wtb[:, slot, 0:width], f"tw{slot}")
            S.slot_set(("wtb_st", slot), f"tw{slot}", ms)

    S.dma("gpsimd", A1[:], d["xT"].rearrange("(c p) t -> p c t", p=128), "sg")
    m_xt = S.cnt["sg"]

    S.wait("tensor", "sm", m_small_w)
    S.wait("vector", "sd", m_tabs)
    S.wait("vector", "sq", m_consts)

    m_lat_last = {}

    def down_proj(pi, wd, gi0):
        m_wload = {}

        def load(fp):
            gi = gi0 + fp
            S.slot_wait("gpsimd", ("wstage", gi % 2))
            m_wload[fp] = S.dma(
                "gpsimd", wstage[:, gi % 2, :, :],
                wd[:, fp * 128:(fp + 1) * 128].rearrange("(ct p) f -> p ct f",
                                                         p=128), f"sw{gi % 2}")
        load(0)
        load(1)
        last_ev = None
        for fp in range(NCT):
            emit_transcode(1)
            gi = gi0 + fp
            S.wait("tensor", f"sw{gi % 2}", m_wload[fp])
            if pi == 0 and fp == 0:
                S.wait("tensor", "sg", m_xt)
            for tc in range(4):
                b = tc % 2
                S.bank_wait("tensor", b)
                mm = None
                for ct in range(NCT):
                    mm = S.op("tensor", "matmul", pf[b][:, :],
                              lhsT=wstage[:, gi % 2, ct, :],
                              rhs=A1[:, ct, tc * 512:(tc + 1) * 512],
                              start=(ct == 0), stop=(ct == NCT - 1),
                              inc="sp" if ct == NCT - 1 else None)
                if tc == 3:
                    S.slot_set(("wstage", gi % 2), "sp", mm)
                S.wait("vector", "sp", mm)
                last_ev = S.op("vector", "tensor_copy",
                               A5[:, fp, tc * 512:(tc + 1) * 512], pf[b][:, :],
                               inc="sv")
                S.bank_free(b, "sv", last_ev)
            if fp + 2 < NCT:
                load(fp + 2)
        m_lat_last[pi] = last_ev

    def up_proj_rope(pi, bmat, scale):
        S.wait("tensor", "sv", m_lat_last[pi])
        dst = QT if pi == 0 else KT
        qtmp = cbuf[:, 0:512]
        t1b = rbuf[:, 0:512]
        t2b = rbuf[:, 512:1024]
        mdone = None
        for pt in range(NCT):
            for tc in range(4):
                i = pt * 4 + tc
                ub = 2 + (i % 2)
                rb = 4 + (i % 2)
                S.bank_wait("tensor", ub)
                mU = S.op("tensor", "matmul", pf[ub][:, :], lhsT=bmat[:],
                          rhs=A5[:, pt, tc * 512:(tc + 1) * 512],
                          start=True, stop=True, inc="sp")
                S.wait("vector", "sp", mU)
                mq = S.op("vector", "tensor_scalar_mul", qtmp[:], pf[ub][:, :],
                          scale, inc="sv")
                S.bank_free(ub, "sv", mq)
                S.wait("tensor", "sv", mq)
                S.bank_wait("tensor", rb)
                mR = S.op("tensor", "matmul", pf[rb][:, :], lhsT=pmat[:],
                          rhs=qtmp[:], start=True, stop=True, inc="sp")
                S.wait("vector", "sv", mq)
                S.op("vector", "tensor_mul", t1b[:], qtmp[:],
                     costab[:, tc * 512:(tc + 1) * 512])
                S.wait("vector", "sp", mR)
                S.op("vector", "tensor_mul", t2b[:], pf[rb][:, :],
                     sintab[:, tc * 512:(tc + 1) * 512])
                S.edge("vector", "sv")
                mdone = S.op("vector", "tensor_add",
                             dst[:, pt, tc * 512:(tc + 1) * 512], t1b[:], t2b[:],
                             inc="sv")
                S.bank_free(rb, "sv", mdone)
        return mdone

    down_proj(0, d["wq"], 0)
    up_proj_rope(0, bq, SCALE)
    S.wait("vector", "sp", S.cnt["sp"])
    down_proj(1, d["wk"], 8)
    mk_done = up_proj_rope(1, bk, 1.0)
    S.wait("vector", "sp", S.cnt["sp"])
    down_proj(2, d["wv"], 16)

    emit_transcode(6)
    S.op("vector", "memset",
         V[:].rearrange("p t (h c) -> p t h c", c=65)[:, :, :, 64:65], 1.0,
         inc="sv")
    S.wait("tensor", "sv", m_lat_last[2])
    for h in range(NH):
        off = 64 * (h % 2)
        for tb in range(NTB):
            i = h * NTB + tb
            b = 6 + (i % 2)
            S.bank_wait("tensor", b)
            mm = S.op("tensor", "matmul", pf[b][:, 0:64],
                      lhsT=A5[off:off + 64, h // 2, tb * 128:(tb + 1) * 128],
                      rhs=wvup[off:off + 64, :], start=True, stop=True,
                      inc="sp")
            S.wait("vector", "sp", mm)
            mev = S.op("vector", "tensor_copy",
                       V[:, tb, h * 65:h * 65 + 64], pf[b][:, 0:64], inc="sv")
            S.bank_free(b, "sv", mev)
    m_v_done = S.cnt["sv"]
    m_pe_a1 = S.cnt["sp"]

    es_1.close()
    es_2 = ExitStack()
    attTA = es_2.enter_context(nc.sbuf_tensor("attTA", [128, 2, 2048], BF16))
    attTB = attTA
    umask = es_2.enter_context(nc.sbuf_tensor("umask", [128, 4, 512], BF16))
    rzs = es_2.enter_context(nc.sbuf_tensor("rzs", [128, 1, 512], F32))
    bsb = es_2.enter_context(nc.sbuf_tensor("bsb", [128, 1, 512], F32))
    ytmp = es_2.enter_context(nc.sbuf_tensor("ytmp", [128, 2, 512], BF16))

    S.op("gpsimd", "memset", umask[:], 0.0, inc="sq")
    S.wait("gpsimd", "sq", S.cnt["sq"])
    for i in range(4):
        S.op("gpsimd", "affine_select", out=umask[:, i, :], in_=umask[:, i, :],
             compare_op=ALU.is_ge, fill=-30000.0, base=-128 * i,
             pattern=[[1, 512]], channel_multiplier=-1,
             inc="sq" if i == 3 else None)
    m_umask = S.cnt["sq"]
    S.wait("tensor", "sq", m_umask)

    def emit_y(h, qS, g, nkc, m_exp):
        hp, hi = h // 2, h % 2
        attx = attT_bufs[hi]
        gpar = g % 2
        yb = 4 + (h * 4 + qS) % 2
        my = None
        S.wait("tensor", "ss", m_exp[g])
        if g == 0:
            S.bank_wait("tensor", yb)
        for i in range(4):
            kc = g * 4 + i
            my = S.op("tensor", "matmul", pf[yb][0:65, :],
                      lhsT=V[:, kc, 65 * h:65 * h + 65],
                      rhs=attx[:, gpar, i * 512:(i + 1) * 512],
                      start=(kc == 0), stop=(kc == nkc - 1),
                      skip_group_check=True,
                      inc="sp" if i == 3 else None)
        S.slot_set(("attT", gpar), "sp", my)
        return my

    attT_bufs = (attTA, attTB)
    S.wait("tensor", "sv", m_v_done)
    m_ydma = 0
    for h in range(NH):
        hp, hi = h // 2, h % 2
        off = 64 * hi
        for qS in range(4):
            emit_transcode(7)
            nkc = (qS + 1) * 4
            m_exp = {}
            m_ylast = None
            for g in range(qS + 1):
                gpar = g % 2
                for i in range(4):
                    kc = g * 4 + i
                    diag = kc - 4 * qS
                    S.bank_wait("tensor", i)
                    if i == 0:
                        S.op("tensor", "matmul", pf[i][:, :],
                             lhsT=ident_bf[:], rhs=umask[:, 0, :],
                             start=True, stop=True)
                    S.op("tensor", "matmul", pf[i][:, :],
                         lhsT=KT[off:off + 64, hp, kc * 128:(kc + 1) * 128],
                         rhs=QT[off:off + 64, hp, qS * 512:(qS + 1) * 512],
                         start=True, stop=(diag < 0))
                    if diag >= 0:
                        S.op("tensor", "matmul", pf[i][:, :],
                             lhsT=ident_bf[:], rhs=umask[:, diag, :],
                             start=False, stop=True)
                    last = S.prog["tensor"][-1]
                    S.cnt["sp"] = S.cnt.get("sp", 0) + 1
                    mm = S.cnt["sp"]
                    S.prog["tensor"][-1] = (last[0], last[1], last[2],
                                            ("sp", 1))
                if g > 0:
                    m_ylast = emit_y(h, qS, g - 1, nkc, m_exp)
                attx = attT_bufs[hi]
                S.slot_wait("scalar", ("attT", gpar))
                if h == 0 and qS == 0 and g == 0:
                    S.wait("scalar", "sq", m_consts)
                S.wait("scalar", "sp", mm)
                mex = S.op("scalar", "activation", attx[:, gpar, :],
                           pS[:, 0:2048], AF.Exp, bias=cshift[:, 0:1],
                           inc="ss")
                for i in range(4):
                    S.bank_free(i, "ss", mex)
                m_exp[g] = mex
            m_ylast = emit_y(h, qS, qS, nkc, m_exp)
            upar = (h * 4 + qS) % 2
            yb = 4 + upar
            bb = 6 + upar
            S.wait("scalar", "sp", m_ylast)
            S.slot_wait("scalar", ("rzs",))
            S.op("scalar", "activation", rzs[64:65, 0, :],
                 pf[yb][64:65, :], AF.Ln)
            S.edge("scalar", "ss")
            mrz = S.op("scalar", "activation", rzs[64:65, 0, :],
                       rzs[64:65, 0, :], AF.Exp, scale=-1.0, inc="ss")
            S.wait("tensor", "ss", mrz)
            S.bank_wait("tensor", bb)
            mbb = S.op("tensor", "matmul", pf[bb][0:64, :],
                       lhsT=ones1[64:65, 0:64], rhs=rzs[64:65, 0, :],
                       start=True, stop=True, inc="sp")
            S.slot_set(("rzs",), "sp", mbb)
            S.wait("vector", "sp", mbb)
            S.slot_wait("vector", ("bsb",))
            mbe = S.op("vector", "tensor_copy", bsb[0:64, 0, :],
                       pf[bb][0:64, :], inc="sv")
            S.bank_free(bb, "sv", mbe)
            S.wait("vector", "sv", mbe)
            if hi == 0:
                mtt = S.op("vector", "tensor_mul",
                           A5[0:64, hp, qS * 512:(qS + 1) * 512],
                           pf[yb][0:64, :], bsb[0:64, 0, :], inc="sv")
            else:
                S.slot_wait("vector", ("ytmp", qS % 2))
                mtt = S.op("vector", "tensor_mul", ytmp[0:64, qS % 2, :],
                           pf[yb][0:64, :], bsb[0:64, 0, :], inc="sv")
                S.wait("sync", "sv", mtt)
                m_ydma = S.dma("sync", A5[64:128, hp,
                                          qS * 512:(qS + 1) * 512],
                               ytmp[0:64, qS % 2, :], f"yd{qS % 2}")
                S.slot_set(("ytmp", qS % 2), f"yd{qS % 2}", m_ydma)
            S.slot_set(("bsb",), "sv", mtt)
            S.bank_free(yb, "sv", mtt)
    emit_transcode(len(tc_jobs))
    m_yT_done = S.cnt["sv"]
    m_yT_dma = (S.cnt.get("yd0", 0), S.cnt.get("yd1", 0))

    es_2.close()
    es_wtb.close()
    es_qkv.close()
    es_oac = ExitStack()
    oac = es_oac.enter_context(nc.sbuf_tensor("oac", [128, NTB, 1024], F32))
    wcb = es_oac.enter_context(nc.sbuf_tensor("wcb", [128, NCT, C], BF16))
    xres = es_oac.enter_context(nc.sbuf_tensor("xres", [128, 2, 1024], F32))
    wgb = es_oac.enter_context(nc.sbuf_tensor("wgb", [128, NCT, 8], BF16))

    m_pe_a2 = S.cnt["sp"]
    S.wait("gpsimd", "sp", m_pe_a2)
    S.wait("gpsimd", "ss", S.cnt["ss"])
    S.wait("gpsimd", "sv", m_yT_done)
    m_wcb = S.dma("gpsimd", wcb[:],
                  d["wc"].rearrange("(ct p) f -> p ct f", p=128), "sc")
    m_wgb = S.dma("gpsimd", wgb[:],
                  d["wgT"].rearrange("(ct p) e -> p ct e", p=128), "sm")

    m_ln = {}
    m_x1T = {}
    S.wait("tensor", "sc", m_wcb)
    S.wait("tensor", "sv", m_yT_done)
    S.wait("tensor", "yd0", m_yT_dma[0])
    S.wait("tensor", "yd1", m_yT_dma[1])
    S.wait("sync", "sp", m_pe_a2)
    S.wait("sync", "ss", S.cnt["ss"])
    S.wait("sync", "sv", m_yT_done)
    S.wait("sync", "yd0", m_yT_dma[0])
    S.wait("sync", "yd1", m_yT_dma[1])
    for _s in ("tw0", "tw1", "tw2", "tw3"):
        S.wait("sync", _s, S.cnt.get(_s, 0))
    for tb in range(NTB):
        S.slot_wait("sync", ("xres", tb % 2))
        m_xr = S.dma("sync", xres[:, tb % 2, :],
                     d["x"][tb * 128:(tb + 1) * 128, :], f"sx{tb % 2}")
        mWc = {}
        for fh in range(2):
            b = 2 * (tb % 2) + fh
            S.bank_wait("tensor", b)
            mm = None
            for ct in range(NCT):
                mm = S.op("tensor", "matmul", pf[b][:, :],
                          lhsT=A5[:, ct, tb * 128:(tb + 1) * 128],
                          rhs=wcb[:, ct, fh * 512:(fh + 1) * 512],
                          start=(ct == 0), stop=(ct == NCT - 1),
                          inc="sp" if ct == NCT - 1 else None)
            mWc[fh] = mm
        S.wait("vector", "sp", mWc[1])
        S.wait("vector", f"sx{tb % 2}", m_xr)
        S.op("vector", "tensor_add", rbuf[:, 0:512],
             pf[2 * (tb % 2)][:, :], xres[:, tb % 2, 0:512])
        mpr = S.op("vector", "tensor_add", rbuf[:, 512:1024],
                   pf[2 * (tb % 2) + 1][:, :], xres[:, tb % 2, 512:1024],
                   inc="sv")
        S.bank_free(2 * (tb % 2), "sv", mpr)
        S.bank_free(2 * (tb % 2) + 1, "sv", mpr)
        S.slot_set(("xres", tb % 2), "sv", mpr)
        S.wait("vector", "sv", mpr)
        S.op("vector", "reduce_sum", lnm[:], rbuf[:], axis=AX, negate=True)
        S.edge("vector", "sv")
        S.op("vector", "tensor_scalar_mul", lnm[:], lnm[:], 1.0 / 1024.0)
        S.edge("vector", "sv")
        S.op("vector", "tensor_scalar_add", cbuf[:], rbuf[:], lnm[:, 0:1])
        S.edge("vector", "sv")
        S.op("vector", "scalar_tensor_tensor", rbuf[:], in0=cbuf[:], scalar=1.0,
             in1=cbuf[:], op0=ALU.mult, op1=ALU.mult, accum_out=lnv[:, 0:1])
        S.edge("vector", "sv")
        mvv = S.op("vector", "tensor_scalar", lnv[:], lnv[:], 1.0 / 1024.0, EPS,
                   op0=ALU.mult, op1=ALU.add, inc="sv")
        S.wait("scalar", "sv", mvv)
        S.op("scalar", "activation", lnv[:], lnv[:], AF.Ln)
        S.edge("scalar", "ss")
        S.op("scalar", "activation", lnr[:], lnv[:], AF.Exp, scale=-0.5)
        mrstd = S.edge("scalar", "ss")
        S.wait("vector", "ss", mrstd)
        S.op("vector", "tensor_scalar_mul", oac[:, tb, :], cbuf[:], lnr[:, 0:1])
        mx1b = S.op("vector", "tensor_scalar_mul", x1bf[:], cbuf[:],
                    lnr[:, 0:1], inc="sv")
        m_ln[tb] = mx1b
        S.wait("tensor", "sv", mx1b)
        for half in range(2):
            b = 6 + half
            S.bank_wait("tensor", b)
            mt = None
            for ci in range(4):
                ct = half * 4 + ci
                mt = S.op("tensor", "matmul", pf[b][:, ci * 128:(ci + 1) * 128],
                          lhsT=x1bf[:, ct * 128:(ct + 1) * 128],
                          rhs=ident_bf[:], start=True, stop=True,
                          inc="sp" if ci == 3 else None)
            S.wait("vector", "sp", mt)
            mev = S.op("vector", "tensor_copy",
                       A1[:, half * 4:half * 4 + 4, tb * 128:(tb + 1) * 128],
                       pf[b][:, :].rearrange("p (c q) -> p c q", c=4),
                       inc="sv")
            S.bank_free(b, "sv", mev)
        m_x1T[tb] = mev

    m_g = {}
    S.wait("tensor", "sm", m_wgb)
    for tb in range(NTB):
        S.wait("tensor", "sv", m_x1T[tb])
        S.bank_wait("tensor", 5)
        mm = None
        for ct in range(NCT):
            mm = S.op("tensor", "matmul", pf[5][:, 0:8],
                      lhsT=A1[:, ct, tb * 128:(tb + 1) * 128],
                      rhs=wgb[:, ct, :], start=(ct == 0), stop=(ct == NCT - 1),
                      inc="sp" if ct == NCT - 1 else None)
        S.wait("vector", "sp", mm)
        if tb > 0:
            S.wait("vector", "ss", m_sl_free)
        mev = S.op("vector", "tensor_copy", sl[:], pf[5][:, 0:8], inc="sv")
        S.bank_free(5, "sv", mev)
        S.wait("vector", "sv", mev)
        S.op("vector", "reduce_max", gm1[:], sl[:], axis=AX)
        S.edge("vector", "sv")
        S.op("vector", "tensor_scalar_mul", gm1n[:], gm1[:], -1.0)
        S.op("vector", "tensor_scalar", geq[:], sl[:], gm1[:, 0:1], None,
             op0=ALU.is_ge)
        S.edge("vector", "sv")
        S.op("vector", "scalar_tensor_tensor", geq[:], in0=geq[:], scalar=-1e30,
             in1=sl[:], op0=ALU.mult, op1=ALU.add)
        S.edge("vector", "sv")
        S.op("vector", "reduce_max", gm2[:], geq[:], axis=AX)
        mgv = S.edge("vector", "sv")
        S.wait("scalar", "sv", mgv)
        msx = S.op("scalar", "activation", gwex[:], sl[:], AF.Exp,
                   bias=gm1n[:, 0:1], inc="ss")
        m_sl_free = msx
        S.op("vector", "tensor_scalar", gkeep[:], sl[:], gm2[:, 0:1], None,
             op0=ALU.is_ge)
        S.edge("vector", "sv")
        S.wait("vector", "ss", msx)
        S.op("vector", "tensor_mul", gkeep[:], gkeep[:], gwex[:])
        S.edge("vector", "sv")
        S.op("vector", "reduce_sum", gz[:], gkeep[:], axis=AX)
        S.edge("vector", "sv")
        S.op("vector", "reciprocal", gz[:], gz[:])
        S.edge("vector", "sv")
        m_g[tb] = S.op("vector", "tensor_scalar_mul", g_all[:, tb, :],
                       gkeep[:], gz[:, 0:1], inc="sv")

    es_lat.close()
    es_v.close()
    es_f = ExitStack()
    gTrow = es_f.enter_context(nc.sbuf_tensor("gTrow", [8, NTB, 128], BF16, side="right"))
    gRow = es_f.enter_context(nc.sbuf_tensor("gRow", [1, 2, 512], BF16, side="right"))
    hiddenG = es_f.enter_context(nc.sbuf_tensor("hiddenG", [128, NHT, 512], BF16, side="right"))
    w12s = es_f.enter_context(nc.sbuf_tensor("w12s", [128, 4, NCT, 2, 128], BF16, side="right"))
    w3s = es_f.enter_context(nc.sbuf_tensor("w3s", [128, 2, 1024], BF16, side="right"))
    x1te = es_f.enter_context(nc.sbuf_tensor("x1te", [128, 2, NCT, 512], BF16))
    gBt = es_f.enter_context(nc.sbuf_tensor("gBt", [128, 2, 512], BF16, side="right"))
    s1b = es_f.enter_context(nc.sbuf_tensor("s1b", [128, 2, 512], BF16, side="right"))
    outb = es_f.enter_context(nc.sbuf_tensor("outb", [128, 1, 1024], F32))

    for tb in range(NTB):
        S.wait("tensor", "sv", m_g[tb])
        S.bank_wait("tensor", 4)
        mt = S.op("tensor", "matmul", pf[4][0:8, 0:128],
                  lhsT=g_all[:, tb, :], rhs=ident_f[:], start=True, stop=True,
                  inc="sp")
        S.wait("vector", "sp", mt)
        mev = S.op("vector", "tensor_copy", gTrow[0:8, tb, :],
                   pf[4][0:8, 0:128], inc="sv")
        S.bank_free(4, "sv", mev)
    m_gT = S.cnt["sv"]
    m_pe_a3 = S.cnt["sp"]

    m_w12 = {}
    m_w3 = {}
    m_wb = {}
    m_x1te = {}
    m_oadd = {}

    first_ffn_dma = [True]
    for _q in ("sync", "gpsimd"):
        for _s in ("tw0", "tw1", "tw2", "tw3"):
            S.wait(_q, _s, S.cnt.get(_s, 0))
    S.wait("sync", "sp", m_pe_a3)
    S.wait("gpsimd", "sp", m_pe_a3)

    def w12_load(e, ts, ht):
        buf = ht % 4
        S.slot_wait("sync", ("w12s", buf))
        srcb = d["w12b"][e, :, 2 * ht:2 * ht + 2, :].rearrange(
            "(ct p) m h -> p ct (m h)", p=128)
        dstb = w12s[:, buf].rearrange("p ct m h -> p ct (m h)")
        m_w12[(e, ts, ht)] = S.dma("sync", dstb, srcb, f"rl{buf}")

    def w3_load(e, ts, ht):
        buf = ht % 2
        S.slot_wait("gpsimd", ("w3s", buf))
        m_w3[(e, ts, ht)] = S.dma(
            "gpsimd", w3s[:, buf, :],
            d["w3b"][e, ht * 128:(ht + 1) * 128, :], f"u{buf}")

    def prep_x1te(e, ts):
        par = ts % 2
        b = 4 + par
        if e == 0 and ts == 0:
            S.wait("gpsimd", "sv", m_gT)
        S.slot_wait("gpsimd", ("gRow", par))
        m_grow = S.dma("gpsimd", gRow[0:1, par, :],
                       gTrow[e:e + 1, ts * 4:(ts + 1) * 4, :].rearrange(
                           "r b q -> r (b q)"), f"gr{par}")
        S.wait("tensor", f"gr{par}", m_grow)
        S.bank_wait("tensor", b)
        mgbm = S.op("tensor", "matmul", pf[b][:, :], lhsT=ones_bf[0:1, :],
                    rhs=gRow[0:1, par, :], start=True, stop=True, inc="sp")
        S.slot_set(("gRow", par), "sp", mgbm)
        S.wait("vector", "sp", mgbm)
        S.slot_wait("vector", ("gBt", par))
        mgb = S.op("vector", "tensor_copy", gBt[:, par, :], pf[b][:, :],
                   inc="sv")
        S.bank_free(b, "sv", mgb)
        S.wait("vector", "sv", mgb)
        S.slot_wait("vector", ("x1te", par))
        mte = None
        for ct in range(NCT):
            mte = S.op("vector", "tensor_mul", x1te[:, par, ct, :],
                       A1[:, ct, ts * 512:(ts + 1) * 512], gBt[:, par, :],
                       inc="sv" if ct == NCT - 1 else None)
        S.slot_set(("gBt", par), "sv", mte)
        m_x1te[(e, ts)] = mte

    for e in range(NE):
        for ts in range(4):
            if ts == 0:
                prep_x1te(e, 0)
            w3_load(e, ts, 0)
            w3_load(e, ts, 1)
            mhg = {}
            for ht in range(NHT):
                buf = ht % 4
                w12_load(e, ts, ht)
                S.wait("tensor", f"rl{buf}", m_w12[(e, ts, ht)])
                if ht == 0:
                    S.wait("tensor", "sv", m_x1te[(e, ts)])
                b1 = ht % 4
                b2 = 4 + ht % 4
                S.bank_wait("tensor", b1)
                mm1 = None
                for ct in range(NCT):
                    mm1 = S.op("tensor", "matmul", pf[b1][:, :],
                               lhsT=w12s[:, buf, ct, 0, :],
                               rhs=A1[:, ct, ts * 512:(ts + 1) * 512],
                               start=(ct == 0), stop=(ct == NCT - 1),
                               inc="sp" if ct == NCT - 1 else None)
                S.bank_wait("tensor", b2)
                mm2 = None
                for ct in range(NCT):
                    mm2 = S.op("tensor", "matmul", pf[b2][:, :],
                               lhsT=w12s[:, buf, ct, 1, :],
                               rhs=x1te[:, ts % 2, ct, :],
                               start=(ct == 0), stop=(ct == NCT - 1),
                               inc="sp" if ct == NCT - 1 else None)
                S.slot_set(("w12s", buf), "sp", mm2)
                if ht == NHT - 1:
                    m_h2_done = mm2
                S.wait("scalar", "sp", mm1)
                S.slot_wait("scalar", ("s1b", ht % 2))
                ms1 = S.op("scalar", "activation", s1b[:, ht % 2, :],
                           pf[b1][:, :], AF.Silu, inc="ss")
                S.bank_free(b1, "ss", ms1)
                S.wait("vector", "ss", ms1)
                S.wait("vector", "sp", mm2)
                mhg[ht] = S.op("vector", "tensor_mul", hiddenG[:, ht, :],
                               s1b[:, ht % 2, :], pf[b2][:, :], inc="sv")
                S.slot_set(("s1b", ht % 2), "sv", mhg[ht])
                S.bank_free(b2, "sv", mhg[ht])
            S.slot_set(("x1te", ts % 2), "sp", m_h2_done)
            if ts < 3:
                prep_x1te(e, ts + 1)
            elif e < NE - 1:
                prep_x1te(e + 1, 0)
            mlast = {}
            for ht in range(NHT):
                buf = ht % 2
                if ht >= 2:
                    pass
                S.wait("tensor", f"u{buf}", m_w3[(e, ts, ht)])
                if ht == 0:
                    S.wait("tensor", "sv", mhg[NHT - 1])
                    for b in range(8):
                        S.bank_wait("tensor", b)
                for tbq in range(4):
                    for dhh in range(2):
                        b = tbq * 2 + dhh
                        mm = S.op("tensor", "matmul", pf[b][:, :],
                                  lhsT=hiddenG[:, ht, tbq * 128:(tbq + 1) * 128],
                                  rhs=w3s[:, buf, dhh * 512:(dhh + 1) * 512],
                                  start=(ht == 0), stop=(ht == NHT - 1),
                                  skip_group_check=True,
                                  inc="sp" if (tbq == 3 and dhh == 1) else None)
                S.slot_set(("w3s", buf), "sp", mm)
                if ht + 2 < NHT:
                    w3_load(e, ts, ht + 2)
            m_w3_all = S.cnt["sp"]
            mev = None
            for tbq in range(4):
                for dhh in range(2):
                    b = tbq * 2 + dhh
                    S.wait("vector", "sp", m_w3_all)
                    mev = S.op("vector", "tensor_add",
                               oac[:, ts * 4 + tbq, dhh * 512:(dhh + 1) * 512],
                               pf[b][:, :],
                               oac[:, ts * 4 + tbq, dhh * 512:(dhh + 1) * 512],
                               inc="sv")
                    S.bank_free(b, "sv", mev)
            m_oadd[(e, ts)] = mev
    m_ffn_done = S.cnt["sv"]

    for tb in range(NTB):
        if tb == 0:
            S.wait("vector", "sv", m_ffn_done)
        S.slot_wait("vector", ("outb",))
        S.op("vector", "reduce_sum", lnm[:], oac[:, tb, :], axis=AX, negate=True)
        S.edge("vector", "sv")
        S.op("vector", "tensor_scalar_mul", lnm[:], lnm[:], 1.0 / 1024.0)
        S.edge("vector", "sv")
        S.op("vector", "tensor_scalar_add", cbuf[:], oac[:, tb, :], lnm[:, 0:1])
        S.edge("vector", "sv")
        S.op("vector", "scalar_tensor_tensor", rbuf[:], in0=cbuf[:], scalar=1.0,
             in1=cbuf[:], op0=ALU.mult, op1=ALU.mult, accum_out=lnv[:, 0:1])
        S.edge("vector", "sv")
        mvv = S.op("vector", "tensor_scalar", lnv[:], lnv[:], 1.0 / 1024.0, EPS,
                   op0=ALU.mult, op1=ALU.add, inc="sv")
        S.wait("scalar", "sv", mvv)
        S.op("scalar", "activation", lnv[:], lnv[:], AF.Ln)
        S.edge("scalar", "ss")
        S.op("scalar", "activation", lnr[:], lnv[:], AF.Exp, scale=-0.5)
        mrstd = S.edge("scalar", "ss")
        S.wait("vector", "ss", mrstd)
        mo = S.op("vector", "tensor_scalar_mul", outb[:, 0, :], cbuf[:],
                  lnr[:, 0:1], inc="sv")
        S.wait("sync", "sv", mo)
        mdma = S.dma("sync", d["out"][tb * 128:(tb + 1) * 128, :],
                     outb[:, 0, :], f"so{tb % 2}")
        S.slot_set(("outb",), f"so{tb % 2}", mdma)
    S.wait("sync", "so0", S.cnt["so0"])
    S.wait("sync", "so1", S.cnt["so1"])

    with nc.Block() as block:
        def _mk(eng_name):
            def runner(engine):
                for meth, a, k, incv in S.prog[eng_name]:
                    if meth == "wait_ge":
                        engine.wait_ge(sems[a[0]], a[1])
                        continue
                    inst = getattr(engine, meth)(*a, **k)
                    if incv is not None:
                        inst.then_inc(sems[incv[0]], incv[1])
            return runner
        block.sync(_mk("sync"))
        block.gpsimd(_mk("gpsimd"))
        block.tensor(_mk("tensor"))
        block.vector(_mk("vector"))
        block.scalar(_mk("scalar"))

    es_f.close()
    es_oac.close()
    es_all.close()
    return nc


_CACHE = {}


def kernel(**inputs):
    def A(name):
        return np.ascontiguousarray(np.asarray(inputs[name], dtype=np.float32))

    x = A("L0_attn_x")
    wq = A("L0_attn_Wq_down")
    wk = A("L0_attn_Wk_down")
    wv = A("L0_attn_Wv_down")
    wc = A("L0_attn_Wc")
    wupqc = A("L0_attn_Wq_up_c")
    wupqe = A("L0_attn_Wq_up_e")
    wupkc = A("L0_attn_Wk_up_c")
    wupke = A("L0_attn_Wk_up_e")
    wupv = A("L0_attn_Wv_up")
    wg = A("L0_ffn_Wg")
    w1 = A("L0_ffn_W1")
    w2 = A("L0_ffn_W2")
    w3 = A("L0_ffn_W3")

    wgT = np.ascontiguousarray(wg.T)
    w1t = np.zeros((NE, C, HIDP), np.float32)
    w1t[:, :, :2730] = w1.transpose(0, 2, 1)
    w2t = np.zeros((NE, C, HIDP), np.float32)
    w2t[:, :, :2730] = w2.transpose(0, 2, 1)
    w12i = np.zeros((NE, C, 2 * NHT, 128), np.float32)
    w12i[:, :, 0::2, :] = w1t.reshape(NE, C, NHT, 128)
    w12i[:, :, 1::2, :] = w2t.reshape(NE, C, NHT, 128)
    w12i = np.ascontiguousarray(w12i)
    w3p = np.zeros((NE, HIDP, C), np.float32)
    w3p[:, :2730, :] = w3.transpose(0, 2, 1)
    w3p = np.ascontiguousarray(w3p)
    cos, sin, pm = _host_consts()

    if "nc" not in _CACHE:
        _CACHE["nc"] = build()
    nc = _CACHE["nc"]

    shared = dict(wq=wq, wk=wk, wv=wv, wc=wc, wupqc=wupqc, wupqe=wupqe,
                  wupkc=wupkc, wupke=wupke, wupv=wupv, wgT=wgT,
                  w12i=w12i, w3p=w3p, cos=cos, sin=sin, pmat=pm)
    in_maps = []
    for b in range(8):
        m = dict(shared)
        m["x"] = np.ascontiguousarray(x[b])
        m["xT"] = np.ascontiguousarray(x[b].T)
        in_maps.append(m)

    res = run_bass_kernel_spmd(nc, in_maps, core_ids=list(range(8)))
    return np.stack([r["out"] for r in res.results], axis=0)


# version 15
# speedup vs baseline: 1.2235x; 1.0099x over previous
import numpy as np
import ml_dtypes
import concourse.bass as bass
import concourse.mybir as mybir
from concourse.bass_utils import run_bass_kernel_spmd
from contextlib import ExitStack

F32 = mybir.dt.float32
BF16 = mybir.dt.bfloat16
AX = mybir.AxisListType.X
ALU = mybir.AluOpType
AF = mybir.ActivationFunctionType

T = 2048
C = 1024
NCT = 8
NH = 16
NTB = 16
NTS = 4
NE = 8
NHT = 22
HIDP = 2816
EPS = 1e-5
SCALE = 0.125

ENG = ("sync", "gpsimd", "tensor", "vector", "scalar")


class Sched:
    def __init__(self):
        self.prog = {e: [] for e in ENG}
        self.cnt = {}
        self.bank = {}
        self.slot = {}

    def op(self, eng, meth, *a, inc=None, incn=1, **k):
        m = None
        if inc is not None:
            self.cnt[inc] = self.cnt.get(inc, 0) + incn
            m = self.cnt[inc]
        self.prog[eng].append((meth, a, k, (inc, incn) if inc else None))
        return m

    def dma(self, eng, out, in_, inc, **k):
        return self.op(eng, "dma_start", inc=inc, incn=16, out=out, in_=in_, **k)

    def wait(self, eng, sem, val):
        if val:
            self.prog[eng].append(("wait_ge", (sem, val), {}, None))

    def edge(self, eng, sem):
        last = self.prog[eng][-1]
        if last[0] == "wait_ge":
            return self.cnt.get(sem, 0)
        if last[3] is not None:
            s2 = last[3][0]
            m = self.cnt[s2]
            self.prog[eng].append(("wait_ge", (s2, m), {}, None))
            return m
        self.cnt[sem] = self.cnt.get(sem, 0) + 1
        m = self.cnt[sem]
        self.prog[eng][-1] = (last[0], last[1], last[2], (sem, 1))
        self.prog[eng].append(("wait_ge", (sem, m), {}, None))
        return m

    def bank_wait(self, eng, b):
        if b in self.bank:
            self.wait(eng, *self.bank[b])

    def bank_free(self, b, sem, val):
        self.bank[b] = (sem, val)

    def slot_wait(self, eng, key):
        if key in self.slot:
            self.wait(eng, *self.slot[key])

    def slot_set(self, key, sem, val):
        self.slot[key] = (sem, val)


def _host_consts():
    theta = 10000.0
    inv = 1.0 / theta ** (np.arange(0, 32, 2, dtype=np.float64) / 32.0)
    tt = np.arange(T, dtype=np.float64)[:, None] * inv[None, :]
    emb = np.concatenate([tt, tt], axis=1)
    cos = np.zeros((128, T), dtype=np.float32)
    sin = np.zeros((128, T), dtype=np.float32)
    for base in (0, 64):
        cos[base:base + 32, :] = 1.0
        cos[base + 32:base + 64, :] = np.cos(emb).T.astype(np.float32)
        sin[base + 32:base + 64, :] = np.sin(emb).T.astype(np.float32)
    pm = np.zeros((128, 128), dtype=np.float32)
    for base in (32, 96):
        for i in range(16):
            pm[base + 2 * i + 1, base + 2 * i] = -1.0
            pm[base + 2 * i, base + 2 * i + 1] = 1.0
    return cos, sin, pm


def build():
    nc = bass.Bass("TRN2", target_bir_lowering=False)
    S = Sched()

    d = {}
    d["x"] = nc.dram_tensor("x", [T, C], F32, kind="ExternalInput").ap()
    d["xT"] = nc.dram_tensor("xT", [C, T], F32, kind="ExternalInput").ap()
    for n in ("wq", "wk", "wv", "wc"):
        d[n] = nc.dram_tensor(n, [C, C], F32, kind="ExternalInput").ap()
    for n in ("wupqc", "wupqe", "wupkc", "wupke"):
        d[n] = nc.dram_tensor(n, [64, 32], F32, kind="ExternalInput").ap()
    d["wupv"] = nc.dram_tensor("wupv", [64, 64], F32, kind="ExternalInput").ap()
    d["wgT"] = nc.dram_tensor("wgT", [C, NE], F32, kind="ExternalInput").ap()
    d["w12i"] = nc.dram_tensor("w12i", [NE, C, 2 * NHT, 128], F32,
                               kind="ExternalInput").ap()
    d["w3p"] = nc.dram_tensor("w3p", [NE, HIDP, C], F32, kind="ExternalInput").ap()
    d["cos"] = nc.dram_tensor("cos", [128, T], F32, kind="ExternalInput").ap()
    d["sin"] = nc.dram_tensor("sin", [128, T], F32, kind="ExternalInput").ap()
    d["pmat"] = nc.dram_tensor("pmat", [128, 128], F32, kind="ExternalInput").ap()
    d["out"] = nc.dram_tensor("out", [T, C], F32, kind="ExternalOutput").ap()
    d["w12b"] = nc.dram_tensor("w12b", [NE, C, 2 * NHT, 128], BF16,
                               kind="Internal").ap()
    d["w3b"] = nc.dram_tensor("w3b", [NE, HIDP, C], BF16, kind="Internal").ap()

    es_all = ExitStack()
    sems = {}
    pS = es_all.enter_context(nc.psum_tensor("pS", [128, 2048], F32))
    pxs = [es_all.enter_context(nc.psum_tensor(f"px{i}", [128, 512], F32))
           for i in range(4)]

    class PSlice:
        def __init__(self, h, off):
            self.h, self.off = h, off

        def __getitem__(self, idx):
            ps, fs = idx
            a = fs.start or 0
            b = fs.stop if fs.stop is not None else 512
            return self.h[ps, self.off + a:self.off + b]

    pf = [PSlice(pS, i * 512) for i in range(4)] + \
         [PSlice(p, 0) for p in pxs]

    A1 = es_all.enter_context(nc.sbuf_tensor("uA1", [128, NCT, T], BF16))
    ident_bf = es_all.enter_context(nc.sbuf_tensor("ident_bf", [128, 128], BF16))
    ident_f = es_all.enter_context(nc.sbuf_tensor("ident_f", [128, 128], F32))
    pmat = es_all.enter_context(nc.sbuf_tensor("pmat_s", [128, 128], F32))
    bq = es_all.enter_context(nc.sbuf_tensor("bq", [128, 128], BF16))
    bk = es_all.enter_context(nc.sbuf_tensor("bk", [128, 128], BF16))
    ones_bf = es_all.enter_context(nc.sbuf_tensor("ones_bf", [1, 128], BF16))
    wvup = es_all.enter_context(nc.sbuf_tensor("wvup", [128, 64], BF16))
    ones1 = es_all.enter_context(nc.sbuf_tensor("ones1", [128, 128], F32))
    rbuf = es_all.enter_context(nc.sbuf_tensor("rbuf", [128, 1024], F32))
    cbuf = es_all.enter_context(nc.sbuf_tensor("cbuf", [128, 1024], F32))
    x1bf = es_all.enter_context(nc.sbuf_tensor("x1bf", [128, 1024], BF16))
    g_all = es_all.enter_context(nc.sbuf_tensor("g_all", [128, NTB, 8], F32))
    sl = es_all.enter_context(nc.sbuf_tensor("sl", [128, 8], F32))
    gm1 = es_all.enter_context(nc.sbuf_tensor("gm1", [128, 1], F32))
    gm1n = es_all.enter_context(nc.sbuf_tensor("gm1n", [128, 1], F32))
    geq = es_all.enter_context(nc.sbuf_tensor("geq", [128, 8], F32))
    gm2 = es_all.enter_context(nc.sbuf_tensor("gm2", [128, 1], F32))
    gwex = es_all.enter_context(nc.sbuf_tensor("gwex", [128, 8], F32))
    gkeep = es_all.enter_context(nc.sbuf_tensor("gkeep", [128, 8], F32))
    gz = es_all.enter_context(nc.sbuf_tensor("gz", [128, 1], F32))
    lnm = es_all.enter_context(nc.sbuf_tensor("lnm", [128, 1], F32))
    lnv = es_all.enter_context(nc.sbuf_tensor("lnv", [128, 1], F32))
    lnr = es_all.enter_context(nc.sbuf_tensor("lnr", [128, 1], F32))
    cshift = es_all.enter_context(nc.sbuf_tensor("cshift", [128, 1], F32))

    for sname in ("sd", "sg", "sm", "sc", "sp", "sv", "ss", "sq",
                  "so0", "so1", "sw0", "sw1", "sx0", "sx1",
                  "sl0", "sl1", "wb0", "wb1", "u0", "u1", "ub0", "ub1",
                  "rl0", "rl1", "rl2", "rl3", "u2", "u3", "gr0", "gr1", "yd0", "yd1",
                  "tl0", "tl1", "tw0", "tw1", "tl2", "tl3", "tw2", "tw3"):
        sems[sname] = es_all.enter_context(nc.semaphore(sname))

    S.op("gpsimd", "memset", ident_bf[:], 0.0)
    S.edge("gpsimd", "sq")
    S.op("gpsimd", "affine_select", out=ident_bf[:], in_=ident_bf[:],
         compare_op=ALU.not_equal, fill=1.0, base=0, pattern=[[-1, 128]],
         channel_multiplier=1)
    S.op("gpsimd", "memset", ident_f[:], 0.0)
    S.edge("gpsimd", "sq")
    S.op("gpsimd", "affine_select", out=ident_f[:], in_=ident_f[:],
         compare_op=ALU.not_equal, fill=1.0, base=0, pattern=[[-1, 128]],
         channel_multiplier=1)
    S.op("gpsimd", "memset", ones1[:], 1.0)
    S.op("gpsimd", "memset", ones_bf[:], 1.0)
    S.op("gpsimd", "memset", cshift[:], -20.0, inc="sq")
    m_consts = S.cnt["sq"]
    S.op("gpsimd", "memset", bq[:], 0.0)
    S.op("gpsimd", "memset", bk[:], 0.0, inc="sq")
    m_bqbk = S.cnt["sq"]
    S.wait("gpsimd", "sq", m_bqbk)
    S.dma("gpsimd", pmat[:], d["pmat"], "sm")
    for (bmat, c_d, e_d) in ((bq, d["wupqc"], d["wupqe"]),
                             (bk, d["wupkc"], d["wupke"])):
        for pb in (0, 64):
            S.dma("gpsimd", bmat[pb:pb + 64, pb + 0:pb + 32], c_d, "sm")
            S.dma("gpsimd", bmat[pb:pb + 64, pb + 32:pb + 64], e_d, "sm")
    S.dma("gpsimd", wvup[0:64, :], d["wupv"], "sm")
    S.dma("gpsimd", wvup[64:128, :], d["wupv"], "sm")
    m_small_w = S.cnt["sm"]

    es_qkv = ExitStack()
    QT = es_qkv.enter_context(nc.sbuf_tensor("QT", [128, NCT, T], BF16))
    KT = es_qkv.enter_context(nc.sbuf_tensor("KT", [128, NCT, T], BF16))
    es_v = ExitStack()
    V = es_v.enter_context(nc.sbuf_tensor("V", [128, NTB, 16 * 65], BF16, side="right"))
    es_lat = ExitStack()
    A5 = es_lat.enter_context(nc.sbuf_tensor("uA5", [128, NCT, T], BF16, side="right"))
    es_wtb = ExitStack()
    wtb = es_wtb.enter_context(nc.sbuf_tensor("wtb", [128, 4, 1536], BF16,
                                              side="right"))
    es_1 = ExitStack()
    costab = es_1.enter_context(nc.sbuf_tensor("costab", [128, T], F32))
    sintab = es_1.enter_context(nc.sbuf_tensor("sintab", [128, T], F32))
    wstage = es_1.enter_context(nc.sbuf_tensor("wstage", [128, 2, NCT, 128], BF16))

    S.dma("sync", costab[:], d["cos"], "sd")
    S.dma("sync", sintab[:], d["sin"], "sd")
    m_tabs = S.cnt["sd"]

    def _flat(ap, per_p):
        return ap.rearrange("a b c -> (a b c)").rearrange("(p q) -> p q", p=128)

    w12f = d["w12i"].rearrange("e c m h -> (e c) (m h)").rearrange(
        "(p q) r -> p (q r)", p=128)
    w12bf = d["w12b"].rearrange("e c m h -> (e c) (m h)").rearrange(
        "(p q) r -> p (q r)", p=128)
    w3f = d["w3p"].rearrange("e h c -> (e h) c").rearrange(
        "(p q) r -> p (q r)", p=128)
    w3bf = d["w3b"].rearrange("e h c -> (e h) c").rearrange(
        "(p q) r -> p (q r)", p=128)
    tc_jobs = []
    for (fsrc, fdst, total) in ((w12f, w12bf, 64 * 5632), (w3f, w3bf, 176 * 1024)):
        off = 0
        while off < total:
            wdt = min(1536, total - off)
            tc_jobs.append((fsrc[:, off:off + wdt], fdst[:, off:off + wdt], wdt))
            off += wdt
    tc_state = {"i": 0}

    def emit_transcode(n):
        for _ in range(n):
            i = tc_state["i"]
            if i >= len(tc_jobs):
                return
            tc_state["i"] = i + 1
            slot = i % 4
            srcf, dstb, width = tc_jobs[i]
            S.slot_wait("gpsimd", ("wtb_st", slot))
            ml = S.dma("gpsimd", wtb[:, slot, 0:width], srcf, f"tl{slot}")
            S.wait("sync", f"tl{slot}", ml)
            ms = S.dma("sync", dstb, wtb[:, slot, 0:width], f"tw{slot}")
            S.slot_set(("wtb_st", slot), f"tw{slot}", ms)

    S.dma("gpsimd", A1[:], d["xT"].rearrange("(c p) t -> p c t", p=128), "sg")
    m_xt = S.cnt["sg"]

    S.wait("tensor", "sm", m_small_w)
    S.wait("vector", "sd", m_tabs)
    S.wait("vector", "sq", m_consts)

    m_lat_last = {}

    def down_proj(pi, wd, gi0):
        m_wload = {}

        def load(fp):
            gi = gi0 + fp
            S.slot_wait("gpsimd", ("wstage", gi % 2))
            m_wload[fp] = S.dma(
                "gpsimd", wstage[:, gi % 2, :, :],
                wd[:, fp * 128:(fp + 1) * 128].rearrange("(ct p) f -> p ct f",
                                                         p=128), f"sw{gi % 2}")
        load(0)
        load(1)
        last_ev = None
        for fp in range(NCT):
            emit_transcode(1)
            gi = gi0 + fp
            S.wait("tensor", f"sw{gi % 2}", m_wload[fp])
            if pi == 0 and fp == 0:
                S.wait("tensor", "sg", m_xt)
            for tc in range(4):
                b = tc % 2
                S.bank_wait("tensor", b)
                mm = None
                for ct in range(NCT):
                    mm = S.op("tensor", "matmul", pf[b][:, :],
                              lhsT=wstage[:, gi % 2, ct, :],
                              rhs=A1[:, ct, tc * 512:(tc + 1) * 512],
                              start=(ct == 0), stop=(ct == NCT - 1),
                              inc="sp" if ct == NCT - 1 else None)
                if tc == 3:
                    S.slot_set(("wstage", gi % 2), "sp", mm)
                S.wait("vector", "sp", mm)
                last_ev = S.op("vector", "tensor_copy",
                               A5[:, fp, tc * 512:(tc + 1) * 512], pf[b][:, :],
                               inc="sv")
                S.bank_free(b, "sv", last_ev)
            if fp + 2 < NCT:
                load(fp + 2)
        m_lat_last[pi] = last_ev

    def up_proj_rope(pi, bmat, scale):
        S.wait("tensor", "sv", m_lat_last[pi])
        dst = QT if pi == 0 else KT
        qtmp = cbuf[:, 0:512]
        t1b = rbuf[:, 0:512]
        t2b = rbuf[:, 512:1024]
        mdone = None
        for pt in range(NCT):
            for tc in range(4):
                i = pt * 4 + tc
                ub = 2 + (i % 2)
                rb = 4 + (i % 2)
                S.bank_wait("tensor", ub)
                mU = S.op("tensor", "matmul", pf[ub][:, :], lhsT=bmat[:],
                          rhs=A5[:, pt, tc * 512:(tc + 1) * 512],
                          start=True, stop=True, inc="sp")
                S.wait("vector", "sp", mU)
                mq = S.op("vector", "tensor_scalar_mul", qtmp[:], pf[ub][:, :],
                          scale, inc="sv")
                S.bank_free(ub, "sv", mq)
                S.wait("tensor", "sv", mq)
                S.bank_wait("tensor", rb)
                mR = S.op("tensor", "matmul", pf[rb][:, :], lhsT=pmat[:],
                          rhs=qtmp[:], start=True, stop=True, inc="sp")
                S.wait("vector", "sv", mq)
                S.op("vector", "tensor_mul", t1b[:], qtmp[:],
                     costab[:, tc * 512:(tc + 1) * 512])
                S.wait("vector", "sp", mR)
                S.op("vector", "tensor_mul", t2b[:], pf[rb][:, :],
                     sintab[:, tc * 512:(tc + 1) * 512])
                S.edge("vector", "sv")
                mdone = S.op("vector", "tensor_add",
                             dst[:, pt, tc * 512:(tc + 1) * 512], t1b[:], t2b[:],
                             inc="sv")
                S.bank_free(rb, "sv", mdone)
        return mdone

    down_proj(0, d["wq"], 0)
    up_proj_rope(0, bq, SCALE)
    S.wait("vector", "sp", S.cnt["sp"])
    down_proj(1, d["wk"], 8)
    mk_done = up_proj_rope(1, bk, 1.0)
    S.wait("vector", "sp", S.cnt["sp"])
    down_proj(2, d["wv"], 16)

    emit_transcode(6)
    S.op("vector", "memset",
         V[:].rearrange("p t (h c) -> p t h c", c=65)[:, :, :, 64:65], 1.0,
         inc="sv")
    S.wait("tensor", "sv", m_lat_last[2])
    for h in range(NH):
        off = 64 * (h % 2)
        for tb in range(NTB):
            i = h * NTB + tb
            b = 6 + (i % 2)
            S.bank_wait("tensor", b)
            mm = S.op("tensor", "matmul", pf[b][:, 0:64],
                      lhsT=A5[off:off + 64, h // 2, tb * 128:(tb + 1) * 128],
                      rhs=wvup[off:off + 64, :], start=True, stop=True,
                      inc="sp")
            S.wait("vector", "sp", mm)
            mev = S.op("vector", "tensor_copy",
                       V[:, tb, h * 65:h * 65 + 64], pf[b][:, 0:64], inc="sv")
            S.bank_free(b, "sv", mev)
    m_v_done = S.cnt["sv"]
    m_pe_a1 = S.cnt["sp"]

    es_1.close()
    es_2 = ExitStack()
    attTA = es_2.enter_context(nc.sbuf_tensor("attTA", [128, 2, 1024], BF16))
    attTB = es_2.enter_context(nc.sbuf_tensor("attTB", [128, 2, 1024], BF16))
    umask = es_2.enter_context(nc.sbuf_tensor("umask", [128, 4, 512], BF16))
    rzs = es_2.enter_context(nc.sbuf_tensor("rzs", [128, 1, 512], F32))
    bsb = es_2.enter_context(nc.sbuf_tensor("bsb", [128, 1, 512], F32))
    ytmp = es_2.enter_context(nc.sbuf_tensor("ytmp", [128, 2, 512], BF16))

    S.op("gpsimd", "memset", umask[:], 0.0, inc="sq")
    S.wait("gpsimd", "sq", S.cnt["sq"])
    for i in range(4):
        S.op("gpsimd", "affine_select", out=umask[:, i, :], in_=umask[:, i, :],
             compare_op=ALU.is_ge, fill=-30000.0, base=-128 * i,
             pattern=[[1, 512]], channel_multiplier=-1,
             inc="sq" if i == 3 else None)
    m_umask = S.cnt["sq"]
    S.wait("tensor", "sq", m_umask)

    def emit_y(h, qS, g, nkc, m_exp):
        hp, hi = h // 2, h % 2
        attx = attT_bufs[hi]
        gpar = g % 2
        yb = 4 + (h * 4 + qS) % 2
        my = None
        S.wait("tensor", "ss", m_exp[g])
        if g == 0:
            S.bank_wait("tensor", yb)
        for i in range(4):
            kc = g * 4 + i
            my = S.op("tensor", "matmul", pf[yb][0:65, :],
                      lhsT=V[:, kc, 65 * h:65 * h + 65],
                      rhs=attx[:, gpar, i * 512:(i + 1) * 512],
                      start=(kc == 0), stop=(kc == nkc - 1),
                      skip_group_check=True,
                      inc="sp" if i == 3 else None)
        S.slot_set(("attT", gpar), "sp", my)
        return my

    attT_bufs = (attTA, attTB)
    S.wait("tensor", "sv", m_v_done)
    m_ydma = 0
    for hp in range(NH // 2):
        for qS in range(4):
            emit_transcode(7)
            nkc = (qS + 1) * 4
            m_exp = {}
            m_mm = {}
            m_ylast = {}

            def y_pair(g2, hi, nkc=nkc):
                h = 2 * hp + hi
                attx = attT_bufs[hi]
                gpar = g2 % 2
                yb = 4 + hi
                S.wait("tensor", "ss", m_exp[(g2, hi)])
                if g2 == 0:
                    S.bank_wait("tensor", yb)
                my = None
                for i in range(2):
                    kc = g2 * 2 + i
                    my = S.op("tensor", "matmul", pf[yb][0:65, :],
                              lhsT=V[:, kc, 65 * h:65 * h + 65],
                              rhs=attx[:, gpar, i * 512:(i + 1) * 512],
                              start=(kc == 0), stop=(kc == nkc - 1),
                              skip_group_check=True,
                              inc="sp" if i == 1 else None)
                S.slot_set(("attT", hi, gpar), "sp", my)
                m_ylast[hi] = my

            for g2 in range(nkc // 2):
                for i in range(2):
                    kc = g2 * 2 + i
                    diag = kc - 4 * qS
                    if i == 0:
                        S.bank_wait("tensor", 0)
                        S.op("tensor", "matmul", pf[0][:, :],
                             lhsT=ident_bf[:], rhs=umask[:, 0, :],
                             start=True, stop=True)
                    for hi in range(2):
                        off = 64 * hi
                        b = hi * 2 + i
                        if not (i == 0 and hi == 0):
                            S.bank_wait("tensor", b)
                        S.op("tensor", "matmul", pf[b][:, :],
                             lhsT=KT[off:off + 64, hp,
                                     kc * 128:(kc + 1) * 128],
                             rhs=QT[off:off + 64, hp,
                                    qS * 512:(qS + 1) * 512],
                             start=True, stop=(diag < 0))
                        if diag >= 0:
                            S.op("tensor", "matmul", pf[b][:, :],
                                 lhsT=ident_bf[:], rhs=umask[:, diag, :],
                                 start=False, stop=True)
                        last = S.prog["tensor"][-1]
                        S.cnt["sp"] = S.cnt.get("sp", 0) + 1
                        m_mm[hi] = S.cnt["sp"]
                        S.prog["tensor"][-1] = (last[0], last[1], last[2],
                                                ("sp", 1))
                if g2 > 0:
                    y_pair(g2 - 1, 0)
                    y_pair(g2 - 1, 1)
                gpar = g2 % 2
                for hi in range(2):
                    attx = attT_bufs[hi]
                    S.slot_wait("scalar", ("attT", hi, gpar))
                    if hp == 0 and qS == 0 and g2 == 0 and hi == 0:
                        S.wait("scalar", "sq", m_consts)
                    S.wait("scalar", "sp", m_mm[hi])
                    mex = S.op("scalar", "activation", attx[:, gpar, 0:1024],
                               pS[:, hi * 1024:hi * 1024 + 1024], AF.Exp,
                               bias=cshift[:, 0:1], inc="ss")
                    m_exp[(g2, hi)] = mex
                    S.bank_free(hi * 2, "ss", mex)
                    S.bank_free(hi * 2 + 1, "ss", mex)
            y_pair(nkc // 2 - 1, 0)
            y_pair(nkc // 2 - 1, 1)
            for hi in range(2):
                h = 2 * hp + hi
                yb = 4 + hi
                bb = 6 + hi
                S.wait("scalar", "sp", m_ylast[hi])
                S.slot_wait("scalar", ("rzs",))
                S.op("scalar", "activation", rzs[64:65, 0, :],
                     pf[yb][64:65, :], AF.Ln)
                S.edge("scalar", "ss")
                mrz = S.op("scalar", "activation", rzs[64:65, 0, :],
                           rzs[64:65, 0, :], AF.Exp, scale=-1.0, inc="ss")
                S.wait("tensor", "ss", mrz)
                S.bank_wait("tensor", bb)
                mbb = S.op("tensor", "matmul", pf[bb][0:64, :],
                           lhsT=ones1[64:65, 0:64], rhs=rzs[64:65, 0, :],
                           start=True, stop=True, inc="sp")
                S.slot_set(("rzs",), "sp", mbb)
                S.wait("vector", "sp", mbb)
                S.slot_wait("vector", ("bsb",))
                mbe = S.op("vector", "tensor_copy", bsb[0:64, 0, :],
                           pf[bb][0:64, :], inc="sv")
                S.bank_free(bb, "sv", mbe)
                S.wait("vector", "sv", mbe)
                if hi == 0:
                    mtt = S.op("vector", "tensor_mul",
                               A5[0:64, hp, qS * 512:(qS + 1) * 512],
                               pf[yb][0:64, :], bsb[0:64, 0, :], inc="sv")
                else:
                    S.slot_wait("vector", ("ytmp", qS % 2))
                    mtt = S.op("vector", "tensor_mul", ytmp[0:64, qS % 2, :],
                               pf[yb][0:64, :], bsb[0:64, 0, :], inc="sv")
                    S.wait("sync", "sv", mtt)
                    m_ydma = S.dma("sync", A5[64:128, hp,
                                              qS * 512:(qS + 1) * 512],
                                   ytmp[0:64, qS % 2, :], f"yd{qS % 2}")
                    S.slot_set(("ytmp", qS % 2), f"yd{qS % 2}", m_ydma)
                S.slot_set(("bsb",), "sv", mtt)
                S.bank_free(yb, "sv", mtt)
    emit_transcode(len(tc_jobs))
    m_yT_done = S.cnt["sv"]
    m_yT_dma = (S.cnt.get("yd0", 0), S.cnt.get("yd1", 0))

    es_2.close()
    es_wtb.close()
    es_qkv.close()
    es_oac = ExitStack()
    oac = es_oac.enter_context(nc.sbuf_tensor("oac", [128, NTB, 1024], F32))
    wcb = es_oac.enter_context(nc.sbuf_tensor("wcb", [128, NCT, C], BF16))
    xres = es_oac.enter_context(nc.sbuf_tensor("xres", [128, 2, 1024], F32))
    wgb = es_oac.enter_context(nc.sbuf_tensor("wgb", [128, NCT, 8], BF16))

    m_pe_a2 = S.cnt["sp"]
    S.wait("gpsimd", "sp", m_pe_a2)
    S.wait("gpsimd", "ss", S.cnt["ss"])
    S.wait("gpsimd", "sv", m_yT_done)
    m_wcb = S.dma("gpsimd", wcb[:],
                  d["wc"].rearrange("(ct p) f -> p ct f", p=128), "sc")
    m_wgb = S.dma("gpsimd", wgb[:],
                  d["wgT"].rearrange("(ct p) e -> p ct e", p=128), "sm")

    m_ln = {}
    m_x1T = {}
    S.wait("tensor", "sc", m_wcb)
    S.wait("tensor", "sv", m_yT_done)
    S.wait("tensor", "yd0", m_yT_dma[0])
    S.wait("tensor", "yd1", m_yT_dma[1])
    S.wait("sync", "sp", m_pe_a2)
    S.wait("sync", "ss", S.cnt["ss"])
    S.wait("sync", "sv", m_yT_done)
    S.wait("sync", "yd0", m_yT_dma[0])
    S.wait("sync", "yd1", m_yT_dma[1])
    for _s in ("tw0", "tw1", "tw2", "tw3"):
        S.wait("sync", _s, S.cnt.get(_s, 0))
    for tb in range(NTB):
        S.slot_wait("sync", ("xres", tb % 2))
        m_xr = S.dma("sync", xres[:, tb % 2, :],
                     d["x"][tb * 128:(tb + 1) * 128, :], f"sx{tb % 2}")
        mWc = {}
        for fh in range(2):
            b = 2 * (tb % 2) + fh
            S.bank_wait("tensor", b)
            mm = None
            for ct in range(NCT):
                mm = S.op("tensor", "matmul", pf[b][:, :],
                          lhsT=A5[:, ct, tb * 128:(tb + 1) * 128],
                          rhs=wcb[:, ct, fh * 512:(fh + 1) * 512],
                          start=(ct == 0), stop=(ct == NCT - 1),
                          inc="sp" if ct == NCT - 1 else None)
            mWc[fh] = mm
        S.wait("vector", "sp", mWc[1])
        S.wait("vector", f"sx{tb % 2}", m_xr)
        S.op("vector", "tensor_add", rbuf[:, 0:512],
             pf[2 * (tb % 2)][:, :], xres[:, tb % 2, 0:512])
        mpr = S.op("vector", "tensor_add", rbuf[:, 512:1024],
                   pf[2 * (tb % 2) + 1][:, :], xres[:, tb % 2, 512:1024],
                   inc="sv")
        S.bank_free(2 * (tb % 2), "sv", mpr)
        S.bank_free(2 * (tb % 2) + 1, "sv", mpr)
        S.slot_set(("xres", tb % 2), "sv", mpr)
        S.wait("vector", "sv", mpr)
        S.op("vector", "reduce_sum", lnm[:], rbuf[:], axis=AX, negate=True)
        S.edge("vector", "sv")
        S.op("vector", "tensor_scalar_mul", lnm[:], lnm[:], 1.0 / 1024.0)
        S.edge("vector", "sv")
        S.op("vector", "tensor_scalar_add", cbuf[:], rbuf[:], lnm[:, 0:1])
        S.edge("vector", "sv")
        S.op("vector", "scalar_tensor_tensor", rbuf[:], in0=cbuf[:], scalar=1.0,
             in1=cbuf[:], op0=ALU.mult, op1=ALU.mult, accum_out=lnv[:, 0:1])
        S.edge("vector", "sv")
        mvv = S.op("vector", "tensor_scalar", lnv[:], lnv[:], 1.0 / 1024.0, EPS,
                   op0=ALU.mult, op1=ALU.add, inc="sv")
        S.wait("scalar", "sv", mvv)
        S.op("scalar", "activation", lnv[:], lnv[:], AF.Ln)
        S.edge("scalar", "ss")
        S.op("scalar", "activation", lnr[:], lnv[:], AF.Exp, scale=-0.5)
        mrstd = S.edge("scalar", "ss")
        S.wait("vector", "ss", mrstd)
        S.op("vector", "tensor_scalar_mul", oac[:, tb, :], cbuf[:], lnr[:, 0:1])
        mx1b = S.op("vector", "tensor_scalar_mul", x1bf[:], cbuf[:],
                    lnr[:, 0:1], inc="sv")
        m_ln[tb] = mx1b
        S.wait("tensor", "sv", mx1b)
        for half in range(2):
            b = 6 + half
            S.bank_wait("tensor", b)
            mt = None
            for ci in range(4):
                ct = half * 4 + ci
                mt = S.op("tensor", "matmul", pf[b][:, ci * 128:(ci + 1) * 128],
                          lhsT=x1bf[:, ct * 128:(ct + 1) * 128],
                          rhs=ident_bf[:], start=True, stop=True,
                          inc="sp" if ci == 3 else None)
            S.wait("vector", "sp", mt)
            mev = S.op("vector", "tensor_copy",
                       A1[:, half * 4:half * 4 + 4, tb * 128:(tb + 1) * 128],
                       pf[b][:, :].rearrange("p (c q) -> p c q", c=4),
                       inc="sv")
            S.bank_free(b, "sv", mev)
        m_x1T[tb] = mev

    m_g = {}
    S.wait("tensor", "sm", m_wgb)
    for tb in range(NTB):
        S.wait("tensor", "sv", m_x1T[tb])
        S.bank_wait("tensor", 5)
        mm = None
        for ct in range(NCT):
            mm = S.op("tensor", "matmul", pf[5][:, 0:8],
                      lhsT=A1[:, ct, tb * 128:(tb + 1) * 128],
                      rhs=wgb[:, ct, :], start=(ct == 0), stop=(ct == NCT - 1),
                      inc="sp" if ct == NCT - 1 else None)
        S.wait("vector", "sp", mm)
        if tb > 0:
            S.wait("vector", "ss", m_sl_free)
        mev = S.op("vector", "tensor_copy", sl[:], pf[5][:, 0:8], inc="sv")
        S.bank_free(5, "sv", mev)
        S.wait("vector", "sv", mev)
        S.op("vector", "reduce_max", gm1[:], sl[:], axis=AX)
        S.edge("vector", "sv")
        S.op("vector", "tensor_scalar_mul", gm1n[:], gm1[:], -1.0)
        S.op("vector", "tensor_scalar", geq[:], sl[:], gm1[:, 0:1], None,
             op0=ALU.is_ge)
        S.edge("vector", "sv")
        S.op("vector", "scalar_tensor_tensor", geq[:], in0=geq[:], scalar=-1e30,
             in1=sl[:], op0=ALU.mult, op1=ALU.add)
        S.edge("vector", "sv")
        S.op("vector", "reduce_max", gm2[:], geq[:], axis=AX)
        mgv = S.edge("vector", "sv")
        S.wait("scalar", "sv", mgv)
        msx = S.op("scalar", "activation", gwex[:], sl[:], AF.Exp,
                   bias=gm1n[:, 0:1], inc="ss")
        m_sl_free = msx
        S.op("vector", "tensor_scalar", gkeep[:], sl[:], gm2[:, 0:1], None,
             op0=ALU.is_ge)
        S.edge("vector", "sv")
        S.wait("vector", "ss", msx)
        S.op("vector", "tensor_mul", gkeep[:], gkeep[:], gwex[:])
        S.edge("vector", "sv")
        S.op("vector", "reduce_sum", gz[:], gkeep[:], axis=AX)
        S.edge("vector", "sv")
        S.op("vector", "reciprocal", gz[:], gz[:])
        S.edge("vector", "sv")
        m_g[tb] = S.op("vector", "tensor_scalar_mul", g_all[:, tb, :],
                       gkeep[:], gz[:, 0:1], inc="sv")

    es_lat.close()
    es_v.close()
    es_f = ExitStack()
    gTrow = es_f.enter_context(nc.sbuf_tensor("gTrow", [8, NTB, 128], BF16, side="right"))
    gRow = es_f.enter_context(nc.sbuf_tensor("gRow", [1, 2, 512], BF16, side="right"))
    hiddenG = es_f.enter_context(nc.sbuf_tensor("hiddenG", [128, NHT, 512], BF16, side="right"))
    w12s = es_f.enter_context(nc.sbuf_tensor("w12s", [128, 4, NCT, 2, 128], BF16, side="right"))
    w3s = es_f.enter_context(nc.sbuf_tensor("w3s", [128, 2, 1024], BF16, side="right"))
    x1te = es_f.enter_context(nc.sbuf_tensor("x1te", [128, 2, NCT, 512], BF16))
    gBt = es_f.enter_context(nc.sbuf_tensor("gBt", [128, 2, 512], BF16, side="right"))
    s1b = es_f.enter_context(nc.sbuf_tensor("s1b", [128, 2, 512], BF16, side="right"))
    outb = es_f.enter_context(nc.sbuf_tensor("outb", [128, 1, 1024], F32))

    for tb in range(NTB):
        S.wait("tensor", "sv", m_g[tb])
        S.bank_wait("tensor", 4)
        mt = S.op("tensor", "matmul", pf[4][0:8, 0:128],
                  lhsT=g_all[:, tb, :], rhs=ident_f[:], start=True, stop=True,
                  inc="sp")
        S.wait("vector", "sp", mt)
        mev = S.op("vector", "tensor_copy", gTrow[0:8, tb, :],
                   pf[4][0:8, 0:128], inc="sv")
        S.bank_free(4, "sv", mev)
    m_gT = S.cnt["sv"]
    m_pe_a3 = S.cnt["sp"]

    m_w12 = {}
    m_w3 = {}
    m_wb = {}
    m_x1te = {}
    m_oadd = {}

    first_ffn_dma = [True]
    for _q in ("sync", "gpsimd"):
        for _s in ("tw0", "tw1", "tw2", "tw3"):
            S.wait(_q, _s, S.cnt.get(_s, 0))
    S.wait("sync", "sp", m_pe_a3)
    S.wait("gpsimd", "sp", m_pe_a3)

    def w12_load(e, ts, ht):
        buf = ht % 4
        S.slot_wait("sync", ("w12s", buf))
        srcb = d["w12b"][e, :, 2 * ht:2 * ht + 2, :].rearrange(
            "(ct p) m h -> p ct (m h)", p=128)
        dstb = w12s[:, buf].rearrange("p ct m h -> p ct (m h)")
        m_w12[(e, ts, ht)] = S.dma("sync", dstb, srcb, f"rl{buf}")

    def w3_load(e, ts, ht):
        buf = ht % 2
        S.slot_wait("gpsimd", ("w3s", buf))
        m_w3[(e, ts, ht)] = S.dma(
            "gpsimd", w3s[:, buf, :],
            d["w3b"][e, ht * 128:(ht + 1) * 128, :], f"u{buf}")

    def prep_x1te(e, ts):
        par = ts % 2
        b = 4 + par
        if e == 0 and ts == 0:
            S.wait("gpsimd", "sv", m_gT)
        S.slot_wait("gpsimd", ("gRow", par))
        m_grow = S.dma("gpsimd", gRow[0:1, par, :],
                       gTrow[e:e + 1, ts * 4:(ts + 1) * 4, :].rearrange(
                           "r b q -> r (b q)"), f"gr{par}")
        S.wait("tensor", f"gr{par}", m_grow)
        S.bank_wait("tensor", b)
        mgbm = S.op("tensor", "matmul", pf[b][:, :], lhsT=ones_bf[0:1, :],
                    rhs=gRow[0:1, par, :], start=True, stop=True, inc="sp")
        S.slot_set(("gRow", par), "sp", mgbm)
        S.wait("vector", "sp", mgbm)
        S.slot_wait("vector", ("gBt", par))
        mgb = S.op("vector", "tensor_copy", gBt[:, par, :], pf[b][:, :],
                   inc="sv")
        S.bank_free(b, "sv", mgb)
        S.wait("vector", "sv", mgb)
        S.slot_wait("vector", ("x1te", par))
        mte = None
        for ct in range(NCT):
            mte = S.op("vector", "tensor_mul", x1te[:, par, ct, :],
                       A1[:, ct, ts * 512:(ts + 1) * 512], gBt[:, par, :],
                       inc="sv" if ct == NCT - 1 else None)
        S.slot_set(("gBt", par), "sv", mte)
        m_x1te[(e, ts)] = mte

    for e in range(NE):
        for ts in range(4):
            if ts == 0:
                prep_x1te(e, 0)
            w3_load(e, ts, 0)
            w3_load(e, ts, 1)
            mhg = {}
            for ht in range(NHT):
                buf = ht % 4
                w12_load(e, ts, ht)
                S.wait("tensor", f"rl{buf}", m_w12[(e, ts, ht)])
                if ht == 0:
                    S.wait("tensor", "sv", m_x1te[(e, ts)])
                b1 = buf
                b2 = 2 + buf
                S.bank_wait("tensor", b1)
                mm1 = None
                for ct in range(NCT):
                    mm1 = S.op("tensor", "matmul", pf[b1][:, :],
                               lhsT=w12s[:, buf, ct, 0, :],
                               rhs=A1[:, ct, ts * 512:(ts + 1) * 512],
                               start=(ct == 0), stop=(ct == NCT - 1),
                               inc="sp" if ct == NCT - 1 else None)
                S.bank_wait("tensor", b2)
                mm2 = None
                for ct in range(NCT):
                    mm2 = S.op("tensor", "matmul", pf[b2][:, :],
                               lhsT=w12s[:, buf, ct, 1, :],
                               rhs=x1te[:, ts % 2, ct, :],
                               start=(ct == 0), stop=(ct == NCT - 1),
                               inc="sp" if ct == NCT - 1 else None)
                S.slot_set(("w12s", buf), "sp", mm2)
                if ht == NHT - 1:
                    m_h2_done = mm2
                S.wait("scalar", "sp", mm1)
                S.slot_wait("scalar", ("s1b", ht % 2))
                ms1 = S.op("scalar", "activation", s1b[:, ht % 2, :],
                           pf[b1][:, :], AF.Silu, inc="ss")
                S.bank_free(b1, "ss", ms1)
                S.wait("vector", "ss", ms1)
                S.wait("vector", "sp", mm2)
                mhg[ht] = S.op("vector", "tensor_mul", hiddenG[:, ht, :],
                               s1b[:, ht % 2, :], pf[b2][:, :], inc="sv")
                S.slot_set(("s1b", ht % 2), "sv", mhg[ht])
                S.bank_free(b2, "sv", mhg[ht])
            S.slot_set(("x1te", ts % 2), "sp", m_h2_done)
            if ts < 3:
                prep_x1te(e, ts + 1)
            elif e < NE - 1:
                prep_x1te(e + 1, 0)
            mlast = {}
            for ht in range(NHT):
                buf = ht % 2
                if ht >= 2:
                    pass
                S.wait("tensor", f"u{buf}", m_w3[(e, ts, ht)])
                if ht == 0:
                    S.wait("tensor", "sv", mhg[NHT - 1])
                    for b in range(8):
                        S.bank_wait("tensor", b)
                for tbq in range(4):
                    for dhh in range(2):
                        b = tbq * 2 + dhh
                        mm = S.op("tensor", "matmul", pf[b][:, :],
                                  lhsT=hiddenG[:, ht, tbq * 128:(tbq + 1) * 128],
                                  rhs=w3s[:, buf, dhh * 512:(dhh + 1) * 512],
                                  start=(ht == 0), stop=(ht == NHT - 1),
                                  skip_group_check=True,
                                  inc="sp" if (tbq == 3 and dhh == 1) else None)
                S.slot_set(("w3s", buf), "sp", mm)
                if ht + 2 < NHT:
                    w3_load(e, ts, ht + 2)
            m_w3_all = S.cnt["sp"]
            mev = None
            for tbq in range(4):
                for dhh in range(2):
                    b = tbq * 2 + dhh
                    S.wait("vector", "sp", m_w3_all)
                    mev = S.op("vector", "tensor_add",
                               oac[:, ts * 4 + tbq, dhh * 512:(dhh + 1) * 512],
                               pf[b][:, :],
                               oac[:, ts * 4 + tbq, dhh * 512:(dhh + 1) * 512],
                               inc="sv")
                    S.bank_free(b, "sv", mev)
            m_oadd[(e, ts)] = mev
    m_ffn_done = S.cnt["sv"]

    for tb in range(NTB):
        if tb == 0:
            S.wait("vector", "sv", m_ffn_done)
        S.slot_wait("vector", ("outb",))
        S.op("vector", "reduce_sum", lnm[:], oac[:, tb, :], axis=AX, negate=True)
        S.edge("vector", "sv")
        S.op("vector", "tensor_scalar_mul", lnm[:], lnm[:], 1.0 / 1024.0)
        S.edge("vector", "sv")
        S.op("vector", "tensor_scalar_add", cbuf[:], oac[:, tb, :], lnm[:, 0:1])
        S.edge("vector", "sv")
        S.op("vector", "scalar_tensor_tensor", rbuf[:], in0=cbuf[:], scalar=1.0,
             in1=cbuf[:], op0=ALU.mult, op1=ALU.mult, accum_out=lnv[:, 0:1])
        S.edge("vector", "sv")
        mvv = S.op("vector", "tensor_scalar", lnv[:], lnv[:], 1.0 / 1024.0, EPS,
                   op0=ALU.mult, op1=ALU.add, inc="sv")
        S.wait("scalar", "sv", mvv)
        S.op("scalar", "activation", lnv[:], lnv[:], AF.Ln)
        S.edge("scalar", "ss")
        S.op("scalar", "activation", lnr[:], lnv[:], AF.Exp, scale=-0.5)
        mrstd = S.edge("scalar", "ss")
        S.wait("vector", "ss", mrstd)
        mo = S.op("vector", "tensor_scalar_mul", outb[:, 0, :], cbuf[:],
                  lnr[:, 0:1], inc="sv")
        S.wait("sync", "sv", mo)
        mdma = S.dma("sync", d["out"][tb * 128:(tb + 1) * 128, :],
                     outb[:, 0, :], f"so{tb % 2}")
        S.slot_set(("outb",), f"so{tb % 2}", mdma)
    S.wait("sync", "so0", S.cnt["so0"])
    S.wait("sync", "so1", S.cnt["so1"])

    with nc.Block() as block:
        def _mk(eng_name):
            def runner(engine):
                for meth, a, k, incv in S.prog[eng_name]:
                    if meth == "wait_ge":
                        engine.wait_ge(sems[a[0]], a[1])
                        continue
                    inst = getattr(engine, meth)(*a, **k)
                    if incv is not None:
                        inst.then_inc(sems[incv[0]], incv[1])
            return runner
        block.sync(_mk("sync"))
        block.gpsimd(_mk("gpsimd"))
        block.tensor(_mk("tensor"))
        block.vector(_mk("vector"))
        block.scalar(_mk("scalar"))

    es_f.close()
    es_oac.close()
    es_all.close()
    return nc


_CACHE = {}


def kernel(**inputs):
    def A(name):
        return np.ascontiguousarray(np.asarray(inputs[name], dtype=np.float32))

    x = A("L0_attn_x")
    wq = A("L0_attn_Wq_down")
    wk = A("L0_attn_Wk_down")
    wv = A("L0_attn_Wv_down")
    wc = A("L0_attn_Wc")
    wupqc = A("L0_attn_Wq_up_c")
    wupqe = A("L0_attn_Wq_up_e")
    wupkc = A("L0_attn_Wk_up_c")
    wupke = A("L0_attn_Wk_up_e")
    wupv = A("L0_attn_Wv_up")
    wg = A("L0_ffn_Wg")
    w1 = A("L0_ffn_W1")
    w2 = A("L0_ffn_W2")
    w3 = A("L0_ffn_W3")

    wgT = np.ascontiguousarray(wg.T)
    w1t = np.zeros((NE, C, HIDP), np.float32)
    w1t[:, :, :2730] = w1.transpose(0, 2, 1)
    w2t = np.zeros((NE, C, HIDP), np.float32)
    w2t[:, :, :2730] = w2.transpose(0, 2, 1)
    w12i = np.zeros((NE, C, 2 * NHT, 128), np.float32)
    w12i[:, :, 0::2, :] = w1t.reshape(NE, C, NHT, 128)
    w12i[:, :, 1::2, :] = w2t.reshape(NE, C, NHT, 128)
    w12i = np.ascontiguousarray(w12i)
    w3p = np.zeros((NE, HIDP, C), np.float32)
    w3p[:, :2730, :] = w3.transpose(0, 2, 1)
    w3p = np.ascontiguousarray(w3p)
    cos, sin, pm = _host_consts()

    if "nc" not in _CACHE:
        _CACHE["nc"] = build()
    nc = _CACHE["nc"]

    shared = dict(wq=wq, wk=wk, wv=wv, wc=wc, wupqc=wupqc, wupqe=wupqe,
                  wupkc=wupkc, wupke=wupke, wupv=wupv, wgT=wgT,
                  w12i=w12i, w3p=w3p, cos=cos, sin=sin, pmat=pm)
    in_maps = []
    for b in range(8):
        m = dict(shared)
        m["x"] = np.ascontiguousarray(x[b])
        m["xT"] = np.ascontiguousarray(x[b].T)
        in_maps.append(m)

    res = run_bass_kernel_spmd(nc, in_maps, core_ids=list(range(8)))
    return np.stack([r["out"] for r in res.results], axis=0)


# version 17
# speedup vs baseline: 1.2340x; 1.0085x over previous
import numpy as np
import ml_dtypes
import concourse.bass as bass
import concourse.mybir as mybir
from concourse.bass_utils import run_bass_kernel_spmd
from contextlib import ExitStack

F32 = mybir.dt.float32
BF16 = mybir.dt.bfloat16
AX = mybir.AxisListType.X
ALU = mybir.AluOpType
AF = mybir.ActivationFunctionType

T = 2048
C = 1024
NCT = 8
NH = 16
NTB = 16
NTS = 4
NE = 8
NHT = 22
HIDP = 2816
EPS = 1e-5
SCALE = 0.125

ENG = ("sync", "gpsimd", "tensor", "vector", "scalar")


class Sched:
    def __init__(self):
        self.prog = {e: [] for e in ENG}
        self.cnt = {}
        self.bank = {}
        self.slot = {}

    def op(self, eng, meth, *a, inc=None, incn=1, **k):
        m = None
        if inc is not None:
            self.cnt[inc] = self.cnt.get(inc, 0) + incn
            m = self.cnt[inc]
        self.prog[eng].append((meth, a, k, (inc, incn) if inc else None))
        return m

    def dma(self, eng, out, in_, inc, **k):
        return self.op(eng, "dma_start", inc=inc, incn=16, out=out, in_=in_, **k)

    def wait(self, eng, sem, val):
        if val:
            self.prog[eng].append(("wait_ge", (sem, val), {}, None))

    def edge(self, eng, sem):
        last = self.prog[eng][-1]
        if last[0] == "wait_ge":
            return self.cnt.get(sem, 0)
        if last[3] is not None:
            s2 = last[3][0]
            m = self.cnt[s2]
            self.prog[eng].append(("wait_ge", (s2, m), {}, None))
            return m
        self.cnt[sem] = self.cnt.get(sem, 0) + 1
        m = self.cnt[sem]
        self.prog[eng][-1] = (last[0], last[1], last[2], (sem, 1))
        self.prog[eng].append(("wait_ge", (sem, m), {}, None))
        return m

    def bank_wait(self, eng, b):
        if b in self.bank:
            self.wait(eng, *self.bank[b])

    def bank_free(self, b, sem, val):
        self.bank[b] = (sem, val)

    def slot_wait(self, eng, key):
        if key in self.slot:
            self.wait(eng, *self.slot[key])

    def slot_set(self, key, sem, val):
        self.slot[key] = (sem, val)


def _host_consts():
    theta = 10000.0
    inv = 1.0 / theta ** (np.arange(0, 32, 2, dtype=np.float64) / 32.0)
    tt = np.arange(T, dtype=np.float64)[:, None] * inv[None, :]
    emb = np.concatenate([tt, tt], axis=1)
    cos = np.zeros((128, T), dtype=np.float32)
    sin = np.zeros((128, T), dtype=np.float32)
    for base in (0, 64):
        cos[base:base + 32, :] = 1.0
        cos[base + 32:base + 64, :] = np.cos(emb).T.astype(np.float32)
        sin[base + 32:base + 64, :] = np.sin(emb).T.astype(np.float32)
    pm = np.zeros((128, 128), dtype=np.float32)
    for base in (32, 96):
        for i in range(16):
            pm[base + 2 * i + 1, base + 2 * i] = -1.0
            pm[base + 2 * i, base + 2 * i + 1] = 1.0
    return cos, sin, pm


def build():
    nc = bass.Bass("TRN2", target_bir_lowering=False)
    S = Sched()

    d = {}
    d["x"] = nc.dram_tensor("x", [T, C], F32, kind="ExternalInput").ap()
    d["xT"] = nc.dram_tensor("xT", [C, T], F32, kind="ExternalInput").ap()
    for n in ("wq", "wk", "wv", "wc"):
        d[n] = nc.dram_tensor(n, [C, C], F32, kind="ExternalInput").ap()
    for n in ("wupqc", "wupqe", "wupkc", "wupke"):
        d[n] = nc.dram_tensor(n, [64, 32], F32, kind="ExternalInput").ap()
    d["wupv"] = nc.dram_tensor("wupv", [64, 64], F32, kind="ExternalInput").ap()
    d["wgT"] = nc.dram_tensor("wgT", [C, NE], F32, kind="ExternalInput").ap()
    d["w12i"] = nc.dram_tensor("w12i", [NE, C, 2 * NHT, 128], F32,
                               kind="ExternalInput").ap()
    d["w3p"] = nc.dram_tensor("w3p", [NE, HIDP, C], F32, kind="ExternalInput").ap()
    d["cos"] = nc.dram_tensor("cos", [128, T], F32, kind="ExternalInput").ap()
    d["sin"] = nc.dram_tensor("sin", [128, T], F32, kind="ExternalInput").ap()
    d["pmat"] = nc.dram_tensor("pmat", [128, 128], F32, kind="ExternalInput").ap()
    d["out"] = nc.dram_tensor("out", [T, C], F32, kind="ExternalOutput").ap()
    d["w12b"] = nc.dram_tensor("w12b", [NE, C, 2 * NHT, 128], BF16,
                               kind="Internal").ap()
    d["w3b"] = nc.dram_tensor("w3b", [NE, HIDP, C], BF16, kind="Internal").ap()

    es_all = ExitStack()
    sems = {}
    pS = es_all.enter_context(nc.psum_tensor("pS", [128, 2048], F32))
    pxs = [es_all.enter_context(nc.psum_tensor(f"px{i}", [128, 512], F32))
           for i in range(4)]

    class PSlice:
        def __init__(self, h, off):
            self.h, self.off = h, off

        def __getitem__(self, idx):
            ps, fs = idx
            a = fs.start or 0
            b = fs.stop if fs.stop is not None else 512
            return self.h[ps, self.off + a:self.off + b]

    pf = [PSlice(pS, i * 512) for i in range(4)] + \
         [PSlice(p, 0) for p in pxs]

    A1 = es_all.enter_context(nc.sbuf_tensor("uA1", [128, NCT, T], BF16))
    ident_bf = es_all.enter_context(nc.sbuf_tensor("ident_bf", [128, 128], BF16))
    ident_f = es_all.enter_context(nc.sbuf_tensor("ident_f", [128, 128], F32))
    pmat = es_all.enter_context(nc.sbuf_tensor("pmat_s", [128, 128], F32))
    bq = es_all.enter_context(nc.sbuf_tensor("bq", [128, 128], BF16))
    bk = es_all.enter_context(nc.sbuf_tensor("bk", [128, 128], BF16))
    ones_bf = es_all.enter_context(nc.sbuf_tensor("ones_bf", [1, 128], BF16))
    wvup = es_all.enter_context(nc.sbuf_tensor("wvup", [128, 64], BF16))
    ones1 = es_all.enter_context(nc.sbuf_tensor("ones1", [128, 128], F32))
    rbuf = es_all.enter_context(nc.sbuf_tensor("rbuf", [128, 1024], F32))
    cbuf = es_all.enter_context(nc.sbuf_tensor("cbuf", [128, 1024], F32))
    x1bf = es_all.enter_context(nc.sbuf_tensor("x1bf", [128, 1024], BF16))
    g_all = es_all.enter_context(nc.sbuf_tensor("g_all", [128, NTB, 8], F32))
    sl = es_all.enter_context(nc.sbuf_tensor("sl", [128, 8], F32))
    gm1 = es_all.enter_context(nc.sbuf_tensor("gm1", [128, 1], F32))
    gm1n = es_all.enter_context(nc.sbuf_tensor("gm1n", [128, 1], F32))
    geq = es_all.enter_context(nc.sbuf_tensor("geq", [128, 8], F32))
    gm2 = es_all.enter_context(nc.sbuf_tensor("gm2", [128, 1], F32))
    gwex = es_all.enter_context(nc.sbuf_tensor("gwex", [128, 8], F32))
    gkeep = es_all.enter_context(nc.sbuf_tensor("gkeep", [128, 8], F32))
    gz = es_all.enter_context(nc.sbuf_tensor("gz", [128, 1], F32))
    lnm = es_all.enter_context(nc.sbuf_tensor("lnm", [128, 1], F32))
    lnv = es_all.enter_context(nc.sbuf_tensor("lnv", [128, 1], F32))
    lnr = es_all.enter_context(nc.sbuf_tensor("lnr", [128, 1], F32))
    cshift = es_all.enter_context(nc.sbuf_tensor("cshift", [128, 1], F32))
    epsb = es_all.enter_context(nc.sbuf_tensor("epsb", [128, 1], F32))

    for sname in ("sd", "sg", "sm", "sc", "sp", "sv", "ss", "sq",
                  "so0", "so1", "sw0", "sw1", "sx0", "sx1",
                  "sl0", "sl1", "wb0", "wb1", "u0", "u1", "ub0", "ub1",
                  "rl0", "rl1", "rl2", "rl3", "u2", "u3", "gr0", "gr1", "yd0", "yd1",
                  "tl0", "tl1", "tw0", "tw1", "tl2", "tl3", "tw2", "tw3"):
        sems[sname] = es_all.enter_context(nc.semaphore(sname))

    S.op("gpsimd", "memset", ident_bf[:], 0.0)
    S.edge("gpsimd", "sq")
    S.op("gpsimd", "affine_select", out=ident_bf[:], in_=ident_bf[:],
         compare_op=ALU.not_equal, fill=1.0, base=0, pattern=[[-1, 128]],
         channel_multiplier=1)
    S.op("gpsimd", "memset", ident_f[:], 0.0)
    S.edge("gpsimd", "sq")
    S.op("gpsimd", "affine_select", out=ident_f[:], in_=ident_f[:],
         compare_op=ALU.not_equal, fill=1.0, base=0, pattern=[[-1, 128]],
         channel_multiplier=1)
    S.op("gpsimd", "memset", ones1[:], 1.0)
    S.op("gpsimd", "memset", ones_bf[:], 1.0)
    S.op("gpsimd", "memset", cshift[:], -20.0)
    S.op("gpsimd", "memset", epsb[:], EPS, inc="sq")
    m_consts = S.cnt["sq"]
    S.op("gpsimd", "memset", bq[:], 0.0)
    S.op("gpsimd", "memset", bk[:], 0.0, inc="sq")
    m_bqbk = S.cnt["sq"]
    S.wait("gpsimd", "sq", m_bqbk)
    S.dma("gpsimd", pmat[:], d["pmat"], "sm")
    for (bmat, c_d, e_d) in ((bq, d["wupqc"], d["wupqe"]),
                             (bk, d["wupkc"], d["wupke"])):
        for pb in (0, 64):
            S.dma("gpsimd", bmat[pb:pb + 64, pb + 0:pb + 32], c_d, "sm")
            S.dma("gpsimd", bmat[pb:pb + 64, pb + 32:pb + 64], e_d, "sm")
    S.dma("gpsimd", wvup[0:64, :], d["wupv"], "sm")
    S.dma("gpsimd", wvup[64:128, :], d["wupv"], "sm")
    m_small_w = S.cnt["sm"]

    es_qkv = ExitStack()
    QT = es_qkv.enter_context(nc.sbuf_tensor("QT", [128, NCT, T], BF16))
    KT = es_qkv.enter_context(nc.sbuf_tensor("KT", [128, NCT, T], BF16))
    es_v = ExitStack()
    V = es_v.enter_context(nc.sbuf_tensor("V", [128, NTB, 16 * 65], BF16, side="right"))
    es_lat = ExitStack()
    A5 = es_lat.enter_context(nc.sbuf_tensor("uA5", [128, NCT, T], BF16, side="right"))
    es_wtb = ExitStack()
    wtb = es_wtb.enter_context(nc.sbuf_tensor("wtb", [128, 4, 1536], BF16,
                                              side="right"))
    es_1 = ExitStack()
    costab = es_1.enter_context(nc.sbuf_tensor("costab", [128, T], F32))
    sintab = es_1.enter_context(nc.sbuf_tensor("sintab", [128, T], F32))
    wstage = es_1.enter_context(nc.sbuf_tensor("wstage", [128, 2, NCT, 128], BF16))

    S.dma("sync", costab[:], d["cos"], "sd")
    S.dma("sync", sintab[:], d["sin"], "sd")
    m_tabs = S.cnt["sd"]

    def _flat(ap, per_p):
        return ap.rearrange("a b c -> (a b c)").rearrange("(p q) -> p q", p=128)

    w12f = d["w12i"].rearrange("e c m h -> (e c) (m h)").rearrange(
        "(p q) r -> p (q r)", p=128)
    w12bf = d["w12b"].rearrange("e c m h -> (e c) (m h)").rearrange(
        "(p q) r -> p (q r)", p=128)
    w3f = d["w3p"].rearrange("e h c -> (e h) c").rearrange(
        "(p q) r -> p (q r)", p=128)
    w3bf = d["w3b"].rearrange("e h c -> (e h) c").rearrange(
        "(p q) r -> p (q r)", p=128)
    tc_jobs = []
    for (fsrc, fdst, total) in ((w12f, w12bf, 64 * 5632), (w3f, w3bf, 176 * 1024)):
        off = 0
        while off < total:
            wdt = min(1536, total - off)
            tc_jobs.append((fsrc[:, off:off + wdt], fdst[:, off:off + wdt], wdt))
            off += wdt
    tc_state = {"i": 0}

    def emit_transcode(n):
        for _ in range(n):
            i = tc_state["i"]
            if i >= len(tc_jobs):
                return
            tc_state["i"] = i + 1
            slot = i % 4
            srcf, dstb, width = tc_jobs[i]
            S.slot_wait("gpsimd", ("wtb_st", slot))
            ml = S.dma("gpsimd", wtb[:, slot, 0:width], srcf, f"tl{slot}")
            S.wait("sync", f"tl{slot}", ml)
            ms = S.dma("sync", dstb, wtb[:, slot, 0:width], f"tw{slot}")
            S.slot_set(("wtb_st", slot), f"tw{slot}", ms)

    S.dma("gpsimd", A1[:], d["xT"].rearrange("(c p) t -> p c t", p=128), "sg")
    m_xt = S.cnt["sg"]

    S.wait("tensor", "sm", m_small_w)
    S.wait("vector", "sd", m_tabs)
    S.wait("vector", "sq", m_consts)

    m_lat_last = {}

    def down_proj(pi, wd, gi0):
        m_wload = {}

        def load(fp):
            gi = gi0 + fp
            S.slot_wait("gpsimd", ("wstage", gi % 2))
            m_wload[fp] = S.dma(
                "gpsimd", wstage[:, gi % 2, :, :],
                wd[:, fp * 128:(fp + 1) * 128].rearrange("(ct p) f -> p ct f",
                                                         p=128), f"sw{gi % 2}")
        load(0)
        load(1)
        last_ev = None
        for fp in range(NCT):
            emit_transcode(1)
            gi = gi0 + fp
            S.wait("tensor", f"sw{gi % 2}", m_wload[fp])
            if pi == 0 and fp == 0:
                S.wait("tensor", "sg", m_xt)
            for tc in range(4):
                b = tc % 2
                S.bank_wait("tensor", b)
                mm = None
                for ct in range(NCT):
                    mm = S.op("tensor", "matmul", pf[b][:, :],
                              lhsT=wstage[:, gi % 2, ct, :],
                              rhs=A1[:, ct, tc * 512:(tc + 1) * 512],
                              start=(ct == 0), stop=(ct == NCT - 1),
                              inc="sp" if ct == NCT - 1 else None)
                if tc == 3:
                    S.slot_set(("wstage", gi % 2), "sp", mm)
                S.wait("vector", "sp", mm)
                last_ev = S.op("vector", "tensor_copy",
                               A5[:, fp, tc * 512:(tc + 1) * 512], pf[b][:, :],
                               inc="sv")
                S.bank_free(b, "sv", last_ev)
            if fp + 2 < NCT:
                load(fp + 2)
        m_lat_last[pi] = last_ev

    def up_proj_rope(pi, bmat, scale):
        S.wait("tensor", "sv", m_lat_last[pi])
        dst = QT if pi == 0 else KT
        qtmp = cbuf[:, 0:512]
        t1b = rbuf[:, 0:512]
        t2b = rbuf[:, 512:1024]
        mdone = None
        for pt in range(NCT):
            for tc in range(4):
                i = pt * 4 + tc
                ub = 2 + (i % 2)
                rb = 4 + (i % 2)
                S.bank_wait("tensor", ub)
                mU = S.op("tensor", "matmul", pf[ub][:, :], lhsT=bmat[:],
                          rhs=A5[:, pt, tc * 512:(tc + 1) * 512],
                          start=True, stop=True, inc="sp")
                S.wait("vector", "sp", mU)
                mq = S.op("vector", "tensor_scalar_mul", qtmp[:], pf[ub][:, :],
                          scale, inc="sv")
                S.bank_free(ub, "sv", mq)
                S.wait("tensor", "sv", mq)
                S.bank_wait("tensor", rb)
                mR = S.op("tensor", "matmul", pf[rb][:, :], lhsT=pmat[:],
                          rhs=qtmp[:], start=True, stop=True, inc="sp")
                S.wait("vector", "sv", mq)
                S.op("vector", "tensor_mul", t1b[:], qtmp[:],
                     costab[:, tc * 512:(tc + 1) * 512])
                S.wait("vector", "sp", mR)
                S.op("vector", "tensor_mul", t2b[:], pf[rb][:, :],
                     sintab[:, tc * 512:(tc + 1) * 512])
                S.edge("vector", "sv")
                mdone = S.op("vector", "tensor_add",
                             dst[:, pt, tc * 512:(tc + 1) * 512], t1b[:], t2b[:],
                             inc="sv")
                S.bank_free(rb, "sv", mdone)
        return mdone

    down_proj(0, d["wq"], 0)
    up_proj_rope(0, bq, SCALE)
    S.wait("vector", "sp", S.cnt["sp"])
    down_proj(1, d["wk"], 8)
    mk_done = up_proj_rope(1, bk, 1.0)
    S.wait("vector", "sp", S.cnt["sp"])
    down_proj(2, d["wv"], 16)

    emit_transcode(6)
    S.op("vector", "memset",
         V[:].rearrange("p t (h c) -> p t h c", c=65)[:, :, :, 64:65], 1.0,
         inc="sv")
    S.wait("tensor", "sv", m_lat_last[2])
    for h in range(NH):
        off = 64 * (h % 2)
        for tb in range(NTB):
            i = h * NTB + tb
            b = 6 + (i % 2)
            S.bank_wait("tensor", b)
            mm = S.op("tensor", "matmul", pf[b][:, 0:64],
                      lhsT=A5[off:off + 64, h // 2, tb * 128:(tb + 1) * 128],
                      rhs=wvup[off:off + 64, :], start=True, stop=True,
                      inc="sp")
            S.wait("vector", "sp", mm)
            mev = S.op("vector", "tensor_copy",
                       V[:, tb, h * 65:h * 65 + 64], pf[b][:, 0:64], inc="sv")
            S.bank_free(b, "sv", mev)
    m_v_done = S.cnt["sv"]
    m_pe_a1 = S.cnt["sp"]

    es_1.close()
    es_2 = ExitStack()
    attTA = es_2.enter_context(nc.sbuf_tensor("attTA", [128, 2, 1024], BF16))
    attTB = es_2.enter_context(nc.sbuf_tensor("attTB", [128, 2, 1024], BF16))
    umask = es_2.enter_context(nc.sbuf_tensor("umask", [128, 4, 512], BF16))
    rzs = es_2.enter_context(nc.sbuf_tensor("rzs", [128, 1, 512], F32))
    bsb = es_2.enter_context(nc.sbuf_tensor("bsb", [128, 1, 512], F32))
    ytmp = es_2.enter_context(nc.sbuf_tensor("ytmp", [128, 2, 512], BF16))

    S.op("gpsimd", "memset", umask[:], 0.0, inc="sq")
    S.wait("gpsimd", "sq", S.cnt["sq"])
    for i in range(4):
        S.op("gpsimd", "affine_select", out=umask[:, i, :], in_=umask[:, i, :],
             compare_op=ALU.is_ge, fill=-30000.0, base=-128 * i,
             pattern=[[1, 512]], channel_multiplier=-1,
             inc="sq" if i == 3 else None)
    m_umask = S.cnt["sq"]
    S.wait("tensor", "sq", m_umask)

    def emit_y(h, qS, g, nkc, m_exp):
        hp, hi = h // 2, h % 2
        attx = attT_bufs[hi]
        gpar = g % 2
        yb = 4 + (h * 4 + qS) % 2
        my = None
        S.wait("tensor", "ss", m_exp[g])
        if g == 0:
            S.bank_wait("tensor", yb)
        for i in range(4):
            kc = g * 4 + i
            my = S.op("tensor", "matmul", pf[yb][0:65, :],
                      lhsT=V[:, kc, 65 * h:65 * h + 65],
                      rhs=attx[:, gpar, i * 512:(i + 1) * 512],
                      start=(kc == 0), stop=(kc == nkc - 1),
                      skip_group_check=True,
                      inc="sp" if i == 3 else None)
        S.slot_set(("attT", gpar), "sp", my)
        return my

    attT_bufs = (attTA, attTB)
    S.wait("tensor", "sv", m_v_done)
    m_ydma = 0
    for hp in range(NH // 2):
        for qS in range(4):
            emit_transcode(7)
            nkc = (qS + 1) * 4
            m_exp = {}
            m_mm = {}
            m_ylast = {}

            def y_pair(g2, hi, nkc=nkc):
                h = 2 * hp + hi
                attx = attT_bufs[hi]
                gpar = g2 % 2
                yb = 4 + hi
                S.wait("tensor", "ss", m_exp[(g2, hi)])
                if g2 == 0:
                    S.bank_wait("tensor", yb)
                my = None
                for i in range(2):
                    kc = g2 * 2 + i
                    my = S.op("tensor", "matmul", pf[yb][0:65, :],
                              lhsT=V[:, kc, 65 * h:65 * h + 65],
                              rhs=attx[:, gpar, i * 512:(i + 1) * 512],
                              start=(kc == 0), stop=(kc == nkc - 1),
                              skip_group_check=True,
                              inc="sp" if i == 1 else None)
                S.slot_set(("attT", hi, gpar), "sp", my)
                m_ylast[hi] = my

            for g2 in range(nkc // 2):
                for i in range(2):
                    kc = g2 * 2 + i
                    diag = kc - 4 * qS
                    if i == 0:
                        S.bank_wait("tensor", 0)
                        S.op("tensor", "matmul", pf[0][:, :],
                             lhsT=ident_bf[:], rhs=umask[:, 0, :],
                             start=True, stop=True)
                    for hi in range(2):
                        off = 64 * hi
                        b = hi * 2 + i
                        if not (i == 0 and hi == 0):
                            S.bank_wait("tensor", b)
                        S.op("tensor", "matmul", pf[b][:, :],
                             lhsT=KT[off:off + 64, hp,
                                     kc * 128:(kc + 1) * 128],
                             rhs=QT[off:off + 64, hp,
                                    qS * 512:(qS + 1) * 512],
                             start=True, stop=(diag < 0))
                        if diag >= 0:
                            S.op("tensor", "matmul", pf[b][:, :],
                                 lhsT=ident_bf[:], rhs=umask[:, diag, :],
                                 start=False, stop=True)
                        last = S.prog["tensor"][-1]
                        S.cnt["sp"] = S.cnt.get("sp", 0) + 1
                        m_mm[hi] = S.cnt["sp"]
                        S.prog["tensor"][-1] = (last[0], last[1], last[2],
                                                ("sp", 1))
                if g2 > 0:
                    y_pair(g2 - 1, 0)
                    y_pair(g2 - 1, 1)
                gpar = g2 % 2
                for hi in range(2):
                    attx = attT_bufs[hi]
                    S.slot_wait("scalar", ("attT", hi, gpar))
                    if hp == 0 and qS == 0 and g2 == 0 and hi == 0:
                        S.wait("scalar", "sq", m_consts)
                    S.wait("scalar", "sp", m_mm[hi])
                    mex = S.op("scalar", "activation", attx[:, gpar, 0:1024],
                               pS[:, hi * 1024:hi * 1024 + 1024], AF.Exp,
                               bias=cshift[:, 0:1], inc="ss")
                    m_exp[(g2, hi)] = mex
                    S.bank_free(hi * 2, "ss", mex)
                    S.bank_free(hi * 2 + 1, "ss", mex)
            y_pair(nkc // 2 - 1, 0)
            y_pair(nkc // 2 - 1, 1)
            for hi in range(2):
                h = 2 * hp + hi
                yb = 4 + hi
                bb = 6 + hi
                S.wait("scalar", "sp", m_ylast[hi])
                S.slot_wait("scalar", ("rzs",))
                S.op("scalar", "activation", rzs[64:65, 0, :],
                     pf[yb][64:65, :], AF.Ln)
                S.edge("scalar", "ss")
                mrz = S.op("scalar", "activation", rzs[64:65, 0, :],
                           rzs[64:65, 0, :], AF.Exp, scale=-1.0, inc="ss")
                S.wait("tensor", "ss", mrz)
                S.bank_wait("tensor", bb)
                mbb = S.op("tensor", "matmul", pf[bb][0:64, :],
                           lhsT=ones1[64:65, 0:64], rhs=rzs[64:65, 0, :],
                           start=True, stop=True, inc="sp")
                S.slot_set(("rzs",), "sp", mbb)
                S.wait("vector", "sp", mbb)
                S.slot_wait("vector", ("bsb",))
                mbe = S.op("vector", "tensor_copy", bsb[0:64, 0, :],
                           pf[bb][0:64, :], inc="sv")
                S.bank_free(bb, "sv", mbe)
                S.wait("vector", "sv", mbe)
                if hi == 0:
                    mtt = S.op("vector", "tensor_mul",
                               A5[0:64, hp, qS * 512:(qS + 1) * 512],
                               pf[yb][0:64, :], bsb[0:64, 0, :], inc="sv")
                else:
                    S.slot_wait("vector", ("ytmp", qS % 2))
                    mtt = S.op("vector", "tensor_mul", ytmp[0:64, qS % 2, :],
                               pf[yb][0:64, :], bsb[0:64, 0, :], inc="sv")
                    S.wait("sync", "sv", mtt)
                    m_ydma = S.dma("sync", A5[64:128, hp,
                                              qS * 512:(qS + 1) * 512],
                                   ytmp[0:64, qS % 2, :], f"yd{qS % 2}")
                    S.slot_set(("ytmp", qS % 2), f"yd{qS % 2}", m_ydma)
                S.slot_set(("bsb",), "sv", mtt)
                S.bank_free(yb, "sv", mtt)
    emit_transcode(len(tc_jobs))
    m_yT_done = S.cnt["sv"]
    m_yT_dma = (S.cnt.get("yd0", 0), S.cnt.get("yd1", 0))

    es_2.close()
    es_wtb.close()
    es_qkv.close()
    es_oac = ExitStack()
    oac = es_oac.enter_context(nc.sbuf_tensor("oac", [128, NTB, 1024], F32))
    wcb = es_oac.enter_context(nc.sbuf_tensor("wcb", [128, NCT, C], BF16))
    xres = es_oac.enter_context(nc.sbuf_tensor("xres", [128, 2, 1024], F32))
    wgb = es_oac.enter_context(nc.sbuf_tensor("wgb", [128, NCT, 8], BF16))

    m_pe_a2 = S.cnt["sp"]
    S.wait("gpsimd", "sp", m_pe_a2)
    S.wait("gpsimd", "ss", S.cnt["ss"])
    S.wait("gpsimd", "sv", m_yT_done)
    m_wcb = S.dma("gpsimd", wcb[:],
                  d["wc"].rearrange("(ct p) f -> p ct f", p=128), "sc")
    m_wgb = S.dma("gpsimd", wgb[:],
                  d["wgT"].rearrange("(ct p) e -> p ct e", p=128), "sm")

    m_ln = {}
    m_x1T = {}
    S.wait("tensor", "sc", m_wcb)
    S.wait("tensor", "sv", m_yT_done)
    S.wait("tensor", "yd0", m_yT_dma[0])
    S.wait("tensor", "yd1", m_yT_dma[1])
    S.wait("sync", "sp", m_pe_a2)
    S.wait("sync", "ss", S.cnt["ss"])
    S.wait("sync", "sv", m_yT_done)
    S.wait("sync", "yd0", m_yT_dma[0])
    S.wait("sync", "yd1", m_yT_dma[1])
    for _s in ("tw0", "tw1", "tw2", "tw3"):
        S.wait("sync", _s, S.cnt.get(_s, 0))
    for tb in range(NTB):
        S.slot_wait("sync", ("xres", tb % 2))
        m_xr = S.dma("sync", xres[:, tb % 2, :],
                     d["x"][tb * 128:(tb + 1) * 128, :], f"sx{tb % 2}")
        mWc = {}
        for fh in range(2):
            b = 2 * (tb % 2) + fh
            S.bank_wait("tensor", b)
            mm = None
            for ct in range(NCT):
                mm = S.op("tensor", "matmul", pf[b][:, :],
                          lhsT=A5[:, ct, tb * 128:(tb + 1) * 128],
                          rhs=wcb[:, ct, fh * 512:(fh + 1) * 512],
                          start=(ct == 0), stop=(ct == NCT - 1),
                          inc="sp" if ct == NCT - 1 else None)
            mWc[fh] = mm
        S.wait("vector", "sp", mWc[1])
        S.wait("vector", f"sx{tb % 2}", m_xr)
        S.op("vector", "tensor_add", rbuf[:, 0:512],
             pf[2 * (tb % 2)][:, :], xres[:, tb % 2, 0:512])
        mpr = S.op("vector", "tensor_add", rbuf[:, 512:1024],
                   pf[2 * (tb % 2) + 1][:, :], xres[:, tb % 2, 512:1024],
                   inc="sv")
        S.bank_free(2 * (tb % 2), "sv", mpr)
        S.bank_free(2 * (tb % 2) + 1, "sv", mpr)
        S.slot_set(("xres", tb % 2), "sv", mpr)
        S.wait("vector", "sv", mpr)
        S.op("vector", "reduce_sum", lnm[:], rbuf[:], axis=AX, negate=True)
        mrd = S.edge("vector", "sv")
        S.wait("scalar", "sv", mrd)
        S.op("scalar", "mul", lnm[:], lnm[:], 1.0 / 1024.0)
        S.edge("scalar", "ss")
        mc = S.op("scalar", "add", cbuf[:], rbuf[:], lnm[:, 0:1], inc="ss")
        S.wait("vector", "ss", mc)
        S.op("vector", "scalar_tensor_tensor", rbuf[:], in0=cbuf[:], scalar=1.0,
             in1=cbuf[:], op0=ALU.mult, op1=ALU.mult, accum_out=lnv[:, 0:1])
        mvv = S.edge("vector", "sv")
        S.wait("scalar", "sv", mvv)
        S.op("scalar", "activation", lnv[:], lnv[:], AF.Ln,
             scale=1.0 / 1024.0, bias=epsb[:, 0:1])
        S.edge("scalar", "ss")
        S.op("scalar", "activation", lnr[:], lnv[:], AF.Exp, scale=-0.5)
        S.edge("scalar", "ss")
        S.op("scalar", "mul", oac[:, tb, :], cbuf[:], lnr[:, 0:1])
        S.edge("scalar", "ss")
        mx1b = S.op("scalar", "mul", x1bf[:], cbuf[:], lnr[:, 0:1], inc="ss")
        m_ln[tb] = mx1b
        S.wait("tensor", "ss", mx1b)
        for half in range(2):
            b = 6 + half
            S.bank_wait("tensor", b)
            mt = None
            for ci in range(4):
                ct = half * 4 + ci
                mt = S.op("tensor", "matmul", pf[b][:, ci * 128:(ci + 1) * 128],
                          lhsT=x1bf[:, ct * 128:(ct + 1) * 128],
                          rhs=ident_bf[:], start=True, stop=True,
                          inc="sp" if ci == 3 else None)
            S.wait("vector", "sp", mt)
            mev = S.op("vector", "tensor_copy",
                       A1[:, half * 4:half * 4 + 4, tb * 128:(tb + 1) * 128],
                       pf[b][:, :].rearrange("p (c q) -> p c q", c=4),
                       inc="sv")
            S.bank_free(b, "sv", mev)
        m_x1T[tb] = mev

    m_g = {}
    S.wait("tensor", "sm", m_wgb)
    for tb in range(NTB):
        S.wait("tensor", "sv", m_x1T[tb])
        S.bank_wait("tensor", 5)
        mm = None
        for ct in range(NCT):
            mm = S.op("tensor", "matmul", pf[5][:, 0:8],
                      lhsT=A1[:, ct, tb * 128:(tb + 1) * 128],
                      rhs=wgb[:, ct, :], start=(ct == 0), stop=(ct == NCT - 1),
                      inc="sp" if ct == NCT - 1 else None)
        S.wait("vector", "sp", mm)
        if tb > 0:
            S.wait("vector", "ss", m_sl_free)
        mev = S.op("vector", "tensor_copy", sl[:], pf[5][:, 0:8], inc="sv")
        S.bank_free(5, "sv", mev)
        S.wait("vector", "sv", mev)
        S.op("vector", "reduce_max", gm1[:], sl[:], axis=AX)
        S.edge("vector", "sv")
        S.op("vector", "tensor_scalar_mul", gm1n[:], gm1[:], -1.0)
        S.op("vector", "tensor_scalar", geq[:], sl[:], gm1[:, 0:1], None,
             op0=ALU.is_ge)
        S.edge("vector", "sv")
        S.op("vector", "scalar_tensor_tensor", geq[:], in0=geq[:], scalar=-1e30,
             in1=sl[:], op0=ALU.mult, op1=ALU.add)
        S.edge("vector", "sv")
        S.op("vector", "reduce_max", gm2[:], geq[:], axis=AX)
        mgv = S.edge("vector", "sv")
        S.wait("scalar", "sv", mgv)
        msx = S.op("scalar", "activation", gwex[:], sl[:], AF.Exp,
                   bias=gm1n[:, 0:1], inc="ss")
        m_sl_free = msx
        S.op("vector", "tensor_scalar", gkeep[:], sl[:], gm2[:, 0:1], None,
             op0=ALU.is_ge)
        S.edge("vector", "sv")
        S.wait("vector", "ss", msx)
        S.op("vector", "tensor_mul", gkeep[:], gkeep[:], gwex[:])
        S.edge("vector", "sv")
        S.op("vector", "reduce_sum", gz[:], gkeep[:], axis=AX)
        S.edge("vector", "sv")
        S.op("vector", "reciprocal", gz[:], gz[:])
        S.edge("vector", "sv")
        m_g[tb] = S.op("vector", "tensor_scalar_mul", g_all[:, tb, :],
                       gkeep[:], gz[:, 0:1], inc="sv")

    es_lat.close()
    es_v.close()
    es_f = ExitStack()
    gTrow = es_f.enter_context(nc.sbuf_tensor("gTrow", [8, NTB, 128], BF16, side="right"))
    gRow = es_f.enter_context(nc.sbuf_tensor("gRow", [1, 2, 512], BF16, side="right"))
    hiddenG = es_f.enter_context(nc.sbuf_tensor("hiddenG", [128, NHT, 512], BF16, side="right"))
    w12s = es_f.enter_context(nc.sbuf_tensor("w12s", [128, 4, NCT, 2, 128], BF16, side="right"))
    w3s = es_f.enter_context(nc.sbuf_tensor("w3s", [128, 2, 1024], BF16, side="right"))
    x1te = es_f.enter_context(nc.sbuf_tensor("x1te", [128, 2, NCT, 512], BF16))
    gBt = es_f.enter_context(nc.sbuf_tensor("gBt", [128, 2, 512], BF16, side="right"))
    s1b = es_f.enter_context(nc.sbuf_tensor("s1b", [128, 2, 512], BF16, side="right"))
    outb = es_f.enter_context(nc.sbuf_tensor("outb", [128, 1, 1024], F32))

    for tb in range(NTB):
        S.wait("tensor", "sv", m_g[tb])
        S.bank_wait("tensor", 4)
        mt = S.op("tensor", "matmul", pf[4][0:8, 0:128],
                  lhsT=g_all[:, tb, :], rhs=ident_f[:], start=True, stop=True,
                  inc="sp")
        S.wait("vector", "sp", mt)
        mev = S.op("vector", "tensor_copy", gTrow[0:8, tb, :],
                   pf[4][0:8, 0:128], inc="sv")
        S.bank_free(4, "sv", mev)
    m_gT = S.cnt["sv"]
    m_pe_a3 = S.cnt["sp"]

    m_w12 = {}
    m_w3 = {}
    m_wb = {}
    m_x1te = {}
    m_oadd = {}

    first_ffn_dma = [True]
    for _q in ("sync", "gpsimd"):
        for _s in ("tw0", "tw1", "tw2", "tw3"):
            S.wait(_q, _s, S.cnt.get(_s, 0))
    S.wait("sync", "sp", m_pe_a3)
    S.wait("gpsimd", "sp", m_pe_a3)

    def w12_load(e, ts, ht):
        buf = ht % 4
        S.slot_wait("sync", ("w12s", buf))
        srcb = d["w12b"][e, :, 2 * ht:2 * ht + 2, :].rearrange(
            "(ct p) m h -> p ct (m h)", p=128)
        dstb = w12s[:, buf].rearrange("p ct m h -> p ct (m h)")
        m_w12[(e, ts, ht)] = S.dma("sync", dstb, srcb, f"rl{buf}")

    def w3_load(e, ts, ht):
        buf = ht % 2
        S.slot_wait("gpsimd", ("w3s", buf))
        m_w3[(e, ts, ht)] = S.dma(
            "gpsimd", w3s[:, buf, :],
            d["w3b"][e, ht * 128:(ht + 1) * 128, :], f"u{buf}")

    def prep_x1te(e, ts):
        par = ts % 2
        b = 4 + par
        if e == 0 and ts == 0:
            S.wait("gpsimd", "sv", m_gT)
        S.slot_wait("gpsimd", ("gRow", par))
        m_grow = S.dma("gpsimd", gRow[0:1, par, :],
                       gTrow[e:e + 1, ts * 4:(ts + 1) * 4, :].rearrange(
                           "r b q -> r (b q)"), f"gr{par}")
        S.wait("tensor", f"gr{par}", m_grow)
        S.bank_wait("tensor", b)
        mgbm = S.op("tensor", "matmul", pf[b][:, :], lhsT=ones_bf[0:1, :],
                    rhs=gRow[0:1, par, :], start=True, stop=True, inc="sp")
        S.slot_set(("gRow", par), "sp", mgbm)
        S.wait("vector", "sp", mgbm)
        S.slot_wait("vector", ("gBt", par))
        mgb = S.op("vector", "tensor_copy", gBt[:, par, :], pf[b][:, :],
                   inc="sv")
        S.bank_free(b, "sv", mgb)
        S.wait("vector", "sv", mgb)
        S.slot_wait("vector", ("x1te", par))
        mte = None
        for ct in range(NCT):
            mte = S.op("vector", "tensor_mul", x1te[:, par, ct, :],
                       A1[:, ct, ts * 512:(ts + 1) * 512], gBt[:, par, :],
                       inc="sv" if ct == NCT - 1 else None)
        S.slot_set(("gBt", par), "sv", mte)
        m_x1te[(e, ts)] = mte

    for e in range(NE):
        for ts in range(4):
            if ts == 0:
                prep_x1te(e, 0)
            w3_load(e, ts, 0)
            w3_load(e, ts, 1)
            mhg = {}
            for ht in range(NHT):
                buf = ht % 4
                w12_load(e, ts, ht)
                S.wait("tensor", f"rl{buf}", m_w12[(e, ts, ht)])
                if ht == 0:
                    S.wait("tensor", "sv", m_x1te[(e, ts)])
                b1 = buf
                b2 = 2 + buf
                S.bank_wait("tensor", b1)
                mm1 = None
                for ct in range(NCT):
                    mm1 = S.op("tensor", "matmul", pf[b1][:, :],
                               lhsT=w12s[:, buf, ct, 0, :],
                               rhs=A1[:, ct, ts * 512:(ts + 1) * 512],
                               start=(ct == 0), stop=(ct == NCT - 1),
                               inc="sp" if ct == NCT - 1 else None)
                S.bank_wait("tensor", b2)
                mm2 = None
                for ct in range(NCT):
                    mm2 = S.op("tensor", "matmul", pf[b2][:, :],
                               lhsT=w12s[:, buf, ct, 1, :],
                               rhs=x1te[:, ts % 2, ct, :],
                               start=(ct == 0), stop=(ct == NCT - 1),
                               inc="sp" if ct == NCT - 1 else None)
                S.slot_set(("w12s", buf), "sp", mm2)
                if ht == NHT - 1:
                    m_h2_done = mm2
                S.wait("scalar", "sp", mm1)
                S.slot_wait("scalar", ("s1b", ht % 2))
                ms1 = S.op("scalar", "activation", s1b[:, ht % 2, :],
                           pf[b1][:, :], AF.Silu, inc="ss")
                S.bank_free(b1, "ss", ms1)
                S.wait("vector", "ss", ms1)
                S.wait("vector", "sp", mm2)
                mhg[ht] = S.op("vector", "tensor_mul", hiddenG[:, ht, :],
                               s1b[:, ht % 2, :], pf[b2][:, :], inc="sv")
                S.slot_set(("s1b", ht % 2), "sv", mhg[ht])
                S.bank_free(b2, "sv", mhg[ht])
            S.slot_set(("x1te", ts % 2), "sp", m_h2_done)
            if ts < 3:
                prep_x1te(e, ts + 1)
            elif e < NE - 1:
                prep_x1te(e + 1, 0)
            mlast = {}
            for ht in range(NHT):
                buf = ht % 2
                if ht >= 2:
                    pass
                S.wait("tensor", f"u{buf}", m_w3[(e, ts, ht)])
                if ht == 0:
                    S.wait("tensor", "sv", mhg[NHT - 1])
                    for b in range(8):
                        S.bank_wait("tensor", b)
                for tbq in range(4):
                    for dhh in range(2):
                        b = tbq * 2 + dhh
                        mm = S.op("tensor", "matmul", pf[b][:, :],
                                  lhsT=hiddenG[:, ht, tbq * 128:(tbq + 1) * 128],
                                  rhs=w3s[:, buf, dhh * 512:(dhh + 1) * 512],
                                  start=(ht == 0), stop=(ht == NHT - 1),
                                  skip_group_check=True,
                                  inc="sp" if (ht == NHT - 1 or
                                               (tbq == 3 and dhh == 1))
                                  else None)
                S.slot_set(("w3s", buf), "sp", mm)
                if ht + 2 < NHT:
                    w3_load(e, ts, ht + 2)
            m_w3_all = S.cnt["sp"]
            mev = None
            for tbq in range(4):
                for dhh in range(2):
                    b = tbq * 2 + dhh
                    S.wait("vector", "sp", m_w3_all - (3 - tbq) * 2 - (1 - dhh))
                    mev = S.op("vector", "tensor_add",
                               oac[:, ts * 4 + tbq, dhh * 512:(dhh + 1) * 512],
                               pf[b][:, :],
                               oac[:, ts * 4 + tbq, dhh * 512:(dhh + 1) * 512],
                               inc="sv")
                    S.bank_free(b, "sv", mev)
            m_oadd[(e, ts)] = mev
    m_ffn_done = S.cnt["sv"]

    for tb in range(NTB):
        if tb == 0:
            S.wait("vector", "sv", m_ffn_done)
        S.slot_wait("scalar", ("outb",))
        S.op("vector", "reduce_sum", lnm[:], oac[:, tb, :], axis=AX, negate=True)
        mrd = S.edge("vector", "sv")
        S.wait("scalar", "sv", mrd)
        S.op("scalar", "mul", lnm[:], lnm[:], 1.0 / 1024.0)
        S.edge("scalar", "ss")
        mc = S.op("scalar", "add", cbuf[:], oac[:, tb, :], lnm[:, 0:1],
                  inc="ss")
        S.wait("vector", "ss", mc)
        S.op("vector", "scalar_tensor_tensor", rbuf[:], in0=cbuf[:], scalar=1.0,
             in1=cbuf[:], op0=ALU.mult, op1=ALU.mult, accum_out=lnv[:, 0:1])
        mvv = S.edge("vector", "sv")
        S.wait("scalar", "sv", mvv)
        S.op("scalar", "activation", lnv[:], lnv[:], AF.Ln,
             scale=1.0 / 1024.0, bias=epsb[:, 0:1])
        S.edge("scalar", "ss")
        S.op("scalar", "activation", lnr[:], lnv[:], AF.Exp, scale=-0.5)
        S.edge("scalar", "ss")
        mo = S.op("scalar", "mul", outb[:, 0, :], cbuf[:], lnr[:, 0:1],
                  inc="ss")
        S.wait("sync", "ss", mo)
        mdma = S.dma("sync", d["out"][tb * 128:(tb + 1) * 128, :],
                     outb[:, 0, :], f"so{tb % 2}")
        S.slot_set(("outb",), f"so{tb % 2}", mdma)
    S.wait("sync", "so0", S.cnt["so0"])
    S.wait("sync", "so1", S.cnt["so1"])

    with nc.Block() as block:
        def _mk(eng_name):
            def runner(engine):
                for meth, a, k, incv in S.prog[eng_name]:
                    if meth == "wait_ge":
                        engine.wait_ge(sems[a[0]], a[1])
                        continue
                    inst = getattr(engine, meth)(*a, **k)
                    if incv is not None:
                        inst.then_inc(sems[incv[0]], incv[1])
            return runner
        block.sync(_mk("sync"))
        block.gpsimd(_mk("gpsimd"))
        block.tensor(_mk("tensor"))
        block.vector(_mk("vector"))
        block.scalar(_mk("scalar"))

    es_f.close()
    es_oac.close()
    es_all.close()
    return nc


_CACHE = {}


def kernel(**inputs):
    def A(name):
        return np.ascontiguousarray(np.asarray(inputs[name], dtype=np.float32))

    x = A("L0_attn_x")
    wq = A("L0_attn_Wq_down")
    wk = A("L0_attn_Wk_down")
    wv = A("L0_attn_Wv_down")
    wc = A("L0_attn_Wc")
    wupqc = A("L0_attn_Wq_up_c")
    wupqe = A("L0_attn_Wq_up_e")
    wupkc = A("L0_attn_Wk_up_c")
    wupke = A("L0_attn_Wk_up_e")
    wupv = A("L0_attn_Wv_up")
    wg = A("L0_ffn_Wg")
    w1 = A("L0_ffn_W1")
    w2 = A("L0_ffn_W2")
    w3 = A("L0_ffn_W3")

    wgT = np.ascontiguousarray(wg.T)
    w1t = np.zeros((NE, C, HIDP), np.float32)
    w1t[:, :, :2730] = w1.transpose(0, 2, 1)
    w2t = np.zeros((NE, C, HIDP), np.float32)
    w2t[:, :, :2730] = w2.transpose(0, 2, 1)
    w12i = np.zeros((NE, C, 2 * NHT, 128), np.float32)
    w12i[:, :, 0::2, :] = w1t.reshape(NE, C, NHT, 128)
    w12i[:, :, 1::2, :] = w2t.reshape(NE, C, NHT, 128)
    w12i = np.ascontiguousarray(w12i)
    w3p = np.zeros((NE, HIDP, C), np.float32)
    w3p[:, :2730, :] = w3.transpose(0, 2, 1)
    w3p = np.ascontiguousarray(w3p)
    cos, sin, pm = _host_consts()

    if "nc" not in _CACHE:
        _CACHE["nc"] = build()
    nc = _CACHE["nc"]

    shared = dict(wq=wq, wk=wk, wv=wv, wc=wc, wupqc=wupqc, wupqe=wupqe,
                  wupkc=wupkc, wupke=wupke, wupv=wupv, wgT=wgT,
                  w12i=w12i, w3p=w3p, cos=cos, sin=sin, pmat=pm)
    in_maps = []
    for b in range(8):
        m = dict(shared)
        m["x"] = np.ascontiguousarray(x[b])
        m["xT"] = np.ascontiguousarray(x[b].T)
        in_maps.append(m)

    res = run_bass_kernel_spmd(nc, in_maps, core_ids=list(range(8)))
    return np.stack([r["out"] for r in res.results], axis=0)


# version 18 (final)
# speedup vs baseline: 1.2523x; 1.0149x over previous
import numpy as np
import ml_dtypes
import concourse.bass as bass
import concourse.mybir as mybir
from concourse.bass_utils import run_bass_kernel_spmd
from contextlib import ExitStack

F32 = mybir.dt.float32
BF16 = mybir.dt.bfloat16
AX = mybir.AxisListType.X
ALU = mybir.AluOpType
AF = mybir.ActivationFunctionType

T = 2048
C = 1024
NCT = 8
NH = 16
NTB = 16
NTS = 4
NE = 8
NHT = 22
HIDP = 2816
EPS = 1e-5
SCALE = 0.125

ENG = ("sync", "gpsimd", "tensor", "vector", "scalar")


class Sched:
    def __init__(self):
        self.prog = {e: [] for e in ENG}
        self.cnt = {}
        self.bank = {}
        self.slot = {}

    def op(self, eng, meth, *a, inc=None, incn=1, **k):
        m = None
        if inc is not None:
            self.cnt[inc] = self.cnt.get(inc, 0) + incn
            m = self.cnt[inc]
        self.prog[eng].append((meth, a, k, (inc, incn) if inc else None))
        return m

    def dma(self, eng, out, in_, inc, **k):
        return self.op(eng, "dma_start", inc=inc, incn=16, out=out, in_=in_, **k)

    def wait(self, eng, sem, val):
        if val:
            self.prog[eng].append(("wait_ge", (sem, val), {}, None))

    def edge(self, eng, sem):
        last = self.prog[eng][-1]
        if last[0] == "wait_ge":
            return self.cnt.get(sem, 0)
        if last[3] is not None:
            s2 = last[3][0]
            m = self.cnt[s2]
            self.prog[eng].append(("wait_ge", (s2, m), {}, None))
            return m
        self.cnt[sem] = self.cnt.get(sem, 0) + 1
        m = self.cnt[sem]
        self.prog[eng][-1] = (last[0], last[1], last[2], (sem, 1))
        self.prog[eng].append(("wait_ge", (sem, m), {}, None))
        return m

    def bank_wait(self, eng, b):
        if b in self.bank:
            self.wait(eng, *self.bank[b])

    def bank_free(self, b, sem, val):
        self.bank[b] = (sem, val)

    def slot_wait(self, eng, key):
        if key in self.slot:
            self.wait(eng, *self.slot[key])

    def slot_set(self, key, sem, val):
        self.slot[key] = (sem, val)


def _host_consts():
    theta = 10000.0
    inv = 1.0 / theta ** (np.arange(0, 32, 2, dtype=np.float64) / 32.0)
    tt = np.arange(T, dtype=np.float64)[:, None] * inv[None, :]
    emb = np.concatenate([tt, tt], axis=1)
    cos = np.zeros((128, T), dtype=np.float32)
    sin = np.zeros((128, T), dtype=np.float32)
    for base in (0, 64):
        cos[base:base + 32, :] = 1.0
        cos[base + 32:base + 64, :] = np.cos(emb).T.astype(np.float32)
        sin[base + 32:base + 64, :] = np.sin(emb).T.astype(np.float32)
    pm = np.zeros((128, 128), dtype=np.float32)
    for base in (32, 96):
        for i in range(16):
            pm[base + 2 * i + 1, base + 2 * i] = -1.0
            pm[base + 2 * i, base + 2 * i + 1] = 1.0
    return cos, sin, pm


def build():
    nc = bass.Bass("TRN2", target_bir_lowering=False)
    S = Sched()

    d = {}
    d["x"] = nc.dram_tensor("x", [T, C], F32, kind="ExternalInput").ap()
    d["xT"] = nc.dram_tensor("xT", [C, T], F32, kind="ExternalInput").ap()
    for n in ("wq", "wk", "wv", "wc"):
        d[n] = nc.dram_tensor(n, [C, C], F32, kind="ExternalInput").ap()
    for n in ("wupqc", "wupqe", "wupkc", "wupke"):
        d[n] = nc.dram_tensor(n, [64, 32], F32, kind="ExternalInput").ap()
    d["wupv"] = nc.dram_tensor("wupv", [64, 64], F32, kind="ExternalInput").ap()
    d["wgT"] = nc.dram_tensor("wgT", [C, NE], F32, kind="ExternalInput").ap()
    d["w12i"] = nc.dram_tensor("w12i", [NE, C, 2 * NHT, 128], F32,
                               kind="ExternalInput").ap()
    d["w3p"] = nc.dram_tensor("w3p", [NE, HIDP, C], F32, kind="ExternalInput").ap()
    d["cos"] = nc.dram_tensor("cos", [128, T], F32, kind="ExternalInput").ap()
    d["sin"] = nc.dram_tensor("sin", [128, T], F32, kind="ExternalInput").ap()
    d["pmat"] = nc.dram_tensor("pmat", [128, 128], F32, kind="ExternalInput").ap()
    d["out"] = nc.dram_tensor("out", [T, C], F32, kind="ExternalOutput").ap()
    d["w12b"] = nc.dram_tensor("w12b", [NE, C, 2 * NHT, 128], BF16,
                               kind="Internal").ap()
    d["w3b"] = nc.dram_tensor("w3b", [NE, HIDP, C], BF16, kind="Internal").ap()

    es_all = ExitStack()
    sems = {}
    pS = es_all.enter_context(nc.psum_tensor("pS", [128, 2048], F32))
    pxs = [es_all.enter_context(nc.psum_tensor(f"px{i}", [128, 512], F32))
           for i in range(4)]

    class PSlice:
        def __init__(self, h, off):
            self.h, self.off = h, off

        def __getitem__(self, idx):
            ps, fs = idx
            a = fs.start or 0
            b = fs.stop if fs.stop is not None else 512
            return self.h[ps, self.off + a:self.off + b]

    pf = [PSlice(pS, i * 512) for i in range(4)] + \
         [PSlice(p, 0) for p in pxs]

    A1 = es_all.enter_context(nc.sbuf_tensor("uA1", [128, NCT, T], BF16))
    ident_bf = es_all.enter_context(nc.sbuf_tensor("ident_bf", [128, 128], BF16))
    ident_f = es_all.enter_context(nc.sbuf_tensor("ident_f", [128, 128], F32))
    pmat = es_all.enter_context(nc.sbuf_tensor("pmat_s", [128, 128], F32))
    bq = es_all.enter_context(nc.sbuf_tensor("bq", [128, 128], BF16))
    bk = es_all.enter_context(nc.sbuf_tensor("bk", [128, 128], BF16))
    ones_bf = es_all.enter_context(nc.sbuf_tensor("ones_bf", [1, 128], BF16))
    wvup = es_all.enter_context(nc.sbuf_tensor("wvup", [128, 64], BF16))
    ones1 = es_all.enter_context(nc.sbuf_tensor("ones1", [128, 128], F32))
    rbuf = es_all.enter_context(nc.sbuf_tensor("rbuf", [128, 1024], F32))
    cbuf = es_all.enter_context(nc.sbuf_tensor("cbuf", [128, 1024], F32))
    x1bf = es_all.enter_context(nc.sbuf_tensor("x1bf", [128, 1024], BF16))
    g_all = es_all.enter_context(nc.sbuf_tensor("g_all", [128, NTB, 8], F32))
    sl = es_all.enter_context(nc.sbuf_tensor("sl", [128, 8], F32))
    gm1 = es_all.enter_context(nc.sbuf_tensor("gm1", [128, 1], F32))
    gm1n = es_all.enter_context(nc.sbuf_tensor("gm1n", [128, 1], F32))
    geq = es_all.enter_context(nc.sbuf_tensor("geq", [128, 8], F32))
    gm2 = es_all.enter_context(nc.sbuf_tensor("gm2", [128, 1], F32))
    gwex = es_all.enter_context(nc.sbuf_tensor("gwex", [128, 8], F32))
    gkeep = es_all.enter_context(nc.sbuf_tensor("gkeep", [128, 8], F32))
    gz = es_all.enter_context(nc.sbuf_tensor("gz", [128, 1], F32))
    lnm = es_all.enter_context(nc.sbuf_tensor("lnm", [128, 1], F32))
    lnv = es_all.enter_context(nc.sbuf_tensor("lnv", [128, 1], F32))
    lnr = es_all.enter_context(nc.sbuf_tensor("lnr", [128, 1], F32))
    cshift = es_all.enter_context(nc.sbuf_tensor("cshift", [128, 1], F32))

    for sname in ("sd", "sg", "sm", "sc", "sp", "sv", "ss", "sq",
                  "so0", "so1", "sw0", "sw1", "sx0", "sx1",
                  "sl0", "sl1", "wb0", "wb1", "u0", "u1", "ub0", "ub1",
                  "rl0", "rl1", "rl2", "rl3", "u2", "u3", "gr0", "gr1", "yd0", "yd1",
                  "tl0", "tl1", "tw0", "tw1", "tl2", "tl3", "tw2", "tw3"):
        sems[sname] = es_all.enter_context(nc.semaphore(sname))

    S.op("gpsimd", "memset", ident_bf[:], 0.0)
    S.edge("gpsimd", "sq")
    S.op("gpsimd", "affine_select", out=ident_bf[:], in_=ident_bf[:],
         compare_op=ALU.not_equal, fill=1.0, base=0, pattern=[[-1, 128]],
         channel_multiplier=1)
    S.op("gpsimd", "memset", ident_f[:], 0.0)
    S.edge("gpsimd", "sq")
    S.op("gpsimd", "affine_select", out=ident_f[:], in_=ident_f[:],
         compare_op=ALU.not_equal, fill=1.0, base=0, pattern=[[-1, 128]],
         channel_multiplier=1)
    S.op("gpsimd", "memset", ones1[:], 1.0)
    S.op("gpsimd", "memset", ones_bf[:], 1.0)
    S.op("gpsimd", "memset", cshift[:], -20.0, inc="sq")
    m_consts = S.cnt["sq"]
    S.op("gpsimd", "memset", bq[:], 0.0)
    S.op("gpsimd", "memset", bk[:], 0.0, inc="sq")
    m_bqbk = S.cnt["sq"]
    S.wait("gpsimd", "sq", m_bqbk)
    S.dma("gpsimd", pmat[:], d["pmat"], "sm")
    for (bmat, c_d, e_d) in ((bq, d["wupqc"], d["wupqe"]),
                             (bk, d["wupkc"], d["wupke"])):
        for pb in (0, 64):
            S.dma("gpsimd", bmat[pb:pb + 64, pb + 0:pb + 32], c_d, "sm")
            S.dma("gpsimd", bmat[pb:pb + 64, pb + 32:pb + 64], e_d, "sm")
    S.dma("gpsimd", wvup[0:64, :], d["wupv"], "sm")
    S.dma("gpsimd", wvup[64:128, :], d["wupv"], "sm")
    m_small_w = S.cnt["sm"]

    es_qkv = ExitStack()
    QT = es_qkv.enter_context(nc.sbuf_tensor("QT", [128, NCT, T], BF16))
    KT = es_qkv.enter_context(nc.sbuf_tensor("KT", [128, NCT, T], BF16))
    es_v = ExitStack()
    V = es_v.enter_context(nc.sbuf_tensor("V", [128, NTB, 16 * 65], BF16, side="right"))
    es_lat = ExitStack()
    A5 = es_lat.enter_context(nc.sbuf_tensor("uA5", [128, NCT, T], BF16, side="right"))
    es_wtb = ExitStack()
    wtb = es_wtb.enter_context(nc.sbuf_tensor("wtb", [128, 4, 1536], BF16,
                                              side="right"))
    es_1 = ExitStack()
    costab = es_1.enter_context(nc.sbuf_tensor("costab", [128, T], F32))
    sintab = es_1.enter_context(nc.sbuf_tensor("sintab", [128, T], F32))
    wstage = es_1.enter_context(nc.sbuf_tensor("wstage", [128, 2, NCT, 128], BF16))

    S.dma("sync", costab[:], d["cos"], "sd")
    S.dma("sync", sintab[:], d["sin"], "sd")
    m_tabs = S.cnt["sd"]

    def _flat(ap, per_p):
        return ap.rearrange("a b c -> (a b c)").rearrange("(p q) -> p q", p=128)

    w12f = d["w12i"].rearrange("e c m h -> (e c) (m h)").rearrange(
        "(p q) r -> p (q r)", p=128)
    w12bf = d["w12b"].rearrange("e c m h -> (e c) (m h)").rearrange(
        "(p q) r -> p (q r)", p=128)
    w3f = d["w3p"].rearrange("e h c -> (e h) c").rearrange(
        "(p q) r -> p (q r)", p=128)
    w3bf = d["w3b"].rearrange("e h c -> (e h) c").rearrange(
        "(p q) r -> p (q r)", p=128)
    tc_jobs = []
    for (fsrc, fdst, total) in ((w12f, w12bf, 64 * 5632), (w3f, w3bf, 176 * 1024)):
        off = 0
        while off < total:
            wdt = min(1536, total - off)
            tc_jobs.append((fsrc[:, off:off + wdt], fdst[:, off:off + wdt], wdt))
            off += wdt
    tc_state = {"i": 0}

    def emit_transcode(n):
        for _ in range(n):
            i = tc_state["i"]
            if i >= len(tc_jobs):
                return
            tc_state["i"] = i + 1
            slot = i % 4
            srcf, dstb, width = tc_jobs[i]
            S.slot_wait("gpsimd", ("wtb_st", slot))
            ml = S.dma("gpsimd", wtb[:, slot, 0:width], srcf, f"tl{slot}")
            S.wait("sync", f"tl{slot}", ml)
            ms = S.dma("sync", dstb, wtb[:, slot, 0:width], f"tw{slot}")
            S.slot_set(("wtb_st", slot), f"tw{slot}", ms)

    S.dma("gpsimd", A1[:], d["xT"].rearrange("(c p) t -> p c t", p=128), "sg")
    m_xt = S.cnt["sg"]

    S.wait("tensor", "sm", m_small_w)
    S.wait("vector", "sd", m_tabs)
    S.wait("vector", "sq", m_consts)

    m_lat_last = {}

    def down_fp(pi, wd, gi0, fp, m_wload):
        emit_transcode(1)
        gi = gi0 + fp
        S.wait("tensor", f"sw{gi % 2}", m_wload[fp])
        if pi == 0 and fp == 0:
            S.wait("tensor", "sg", m_xt)
        last_ev = None
        for tc in range(4):
            b = tc % 2
            S.bank_wait("tensor", b)
            mm = None
            for ct in range(NCT):
                mm = S.op("tensor", "matmul", pf[b][:, :],
                          lhsT=wstage[:, gi % 2, ct, :],
                          rhs=A1[:, ct, tc * 512:(tc + 1) * 512],
                          start=(ct == 0), stop=(ct == NCT - 1),
                          inc="sp" if ct == NCT - 1 else None)
            if tc == 3:
                S.slot_set(("wstage", gi % 2), "sp", mm)
            S.wait("vector", "sp", mm)
            last_ev = S.op("vector", "tensor_copy",
                           A5[:, fp, tc * 512:(tc + 1) * 512], pf[b][:, :],
                           inc="sv")
            S.bank_free(b, "sv", last_ev)
        m_lat_last[pi] = last_ev

    def wload(wd, gi0, fp, m_wload):
        gi = gi0 + fp
        S.slot_wait("gpsimd", ("wstage", gi % 2))
        m_wload[fp] = S.dma(
            "gpsimd", wstage[:, gi % 2, :, :],
            wd[:, fp * 128:(fp + 1) * 128].rearrange("(ct p) f -> p ct f",
                                                     p=128), f"sw{gi % 2}")

    def up_tile(pi, bmat, scale, pt):
        dst = QT if pi == 0 else KT
        qtmp = cbuf[:, 0:512]
        t1b = rbuf[:, 0:512]
        t2b = rbuf[:, 512:1024]
        mdone = None
        for tc in range(4):
            i = pt * 4 + tc
            ub = 2 + (i % 2)
            rb = 4 + (i % 2)
            S.bank_wait("tensor", ub)
            mU = S.op("tensor", "matmul", pf[ub][:, :], lhsT=bmat[:],
                      rhs=A5[:, pt, tc * 512:(tc + 1) * 512],
                      start=True, stop=True, inc="sp")
            S.wait("vector", "sp", mU)
            mq = S.op("vector", "tensor_scalar_mul", qtmp[:], pf[ub][:, :],
                      scale, inc="sv")
            S.bank_free(ub, "sv", mq)
            S.wait("tensor", "sv", mq)
            S.bank_wait("tensor", rb)
            mR = S.op("tensor", "matmul", pf[rb][:, :], lhsT=pmat[:],
                      rhs=qtmp[:], start=True, stop=True, inc="sp")
            S.wait("vector", "sv", mq)
            S.op("vector", "tensor_mul", t1b[:], qtmp[:],
                 costab[:, tc * 512:(tc + 1) * 512])
            S.wait("vector", "sp", mR)
            S.op("vector", "tensor_mul", t2b[:], pf[rb][:, :],
                 sintab[:, tc * 512:(tc + 1) * 512])
            S.edge("vector", "sv")
            mdone = S.op("vector", "tensor_add",
                         dst[:, pt, tc * 512:(tc + 1) * 512], t1b[:], t2b[:],
                         inc="sv")
            S.bank_free(rb, "sv", mdone)
        return mdone

    m_wl_q, m_wl_k, m_wl_v = {}, {}, {}
    wload(d["wq"], 0, 0, m_wl_q)
    wload(d["wq"], 0, 1, m_wl_q)
    for fp in range(NCT):
        down_fp(0, d["wq"], 0, fp, m_wl_q)
        if fp + 2 < NCT:
            wload(d["wq"], 0, fp + 2, m_wl_q)
    wload(d["wk"], 8, 0, m_wl_k)
    wload(d["wk"], 8, 1, m_wl_k)
    for pt in range(NCT):
        up_tile(0, bq, SCALE, pt)
        down_fp(1, d["wk"], 8, pt, m_wl_k)
        if pt + 2 < NCT:
            wload(d["wk"], 8, pt + 2, m_wl_k)
    wload(d["wv"], 16, 0, m_wl_v)
    wload(d["wv"], 16, 1, m_wl_v)
    for pt in range(NCT):
        up_tile(1, bk, 1.0, pt)
        down_fp(2, d["wv"], 16, pt, m_wl_v)
        if pt + 2 < NCT:
            wload(d["wv"], 16, pt + 2, m_wl_v)
    m_rope_all = S.cnt["sv"]

    S.op("vector", "memset",
         V[:].rearrange("p t (h c) -> p t h c", c=65)[:, :, :, 64:65], 1.0,
         inc="sv")
    S.wait("tensor", "sv", m_lat_last[2])
    for h in range(NH):
        off = 64 * (h % 2)
        for tb in range(NTB):
            i = h * NTB + tb
            b = 6 + (i % 2)
            S.bank_wait("tensor", b)
            mm = S.op("tensor", "matmul", pf[b][:, 0:64],
                      lhsT=A5[off:off + 64, h // 2, tb * 128:(tb + 1) * 128],
                      rhs=wvup[off:off + 64, :], start=True, stop=True,
                      inc="sp")
            S.wait("vector", "sp", mm)
            mev = S.op("vector", "tensor_copy",
                       V[:, tb, h * 65:h * 65 + 64], pf[b][:, 0:64], inc="sv")
            S.bank_free(b, "sv", mev)
    m_v_done = S.cnt["sv"]
    m_pe_a1 = S.cnt["sp"]

    es_1.close()
    es_2 = ExitStack()
    attTA = es_2.enter_context(nc.sbuf_tensor("attTA", [128, 2, 1024], BF16))
    attTB = es_2.enter_context(nc.sbuf_tensor("attTB", [128, 2, 1024], BF16))
    umask = es_2.enter_context(nc.sbuf_tensor("umask", [128, 4, 512], BF16))
    rzs = es_2.enter_context(nc.sbuf_tensor("rzs", [128, 1, 512], F32))
    bsb = es_2.enter_context(nc.sbuf_tensor("bsb", [128, 1, 512], F32))
    ytmp = es_2.enter_context(nc.sbuf_tensor("ytmp", [128, 2, 512], BF16))

    S.wait("gpsimd", "sv", m_rope_all)
    S.op("gpsimd", "memset", umask[:], 0.0, inc="sq")
    S.wait("gpsimd", "sq", S.cnt["sq"])
    for i in range(4):
        S.op("gpsimd", "affine_select", out=umask[:, i, :], in_=umask[:, i, :],
             compare_op=ALU.is_ge, fill=-30000.0, base=-128 * i,
             pattern=[[1, 512]], channel_multiplier=-1,
             inc="sq" if i == 3 else None)
    m_umask = S.cnt["sq"]
    S.wait("tensor", "sq", m_umask)

    def emit_y(h, qS, g, nkc, m_exp):
        hp, hi = h // 2, h % 2
        attx = attT_bufs[hi]
        gpar = g % 2
        yb = 4 + (h * 4 + qS) % 2
        my = None
        S.wait("tensor", "ss", m_exp[g])
        if g == 0:
            S.bank_wait("tensor", yb)
        for i in range(4):
            kc = g * 4 + i
            my = S.op("tensor", "matmul", pf[yb][0:65, :],
                      lhsT=V[:, kc, 65 * h:65 * h + 65],
                      rhs=attx[:, gpar, i * 512:(i + 1) * 512],
                      start=(kc == 0), stop=(kc == nkc - 1),
                      skip_group_check=True,
                      inc="sp" if i == 3 else None)
        S.slot_set(("attT", gpar), "sp", my)
        return my

    attT_bufs = (attTA, attTB)
    S.wait("tensor", "sv", m_v_done)
    m_ydma = 0
    for hp in range(NH // 2):
        for qS in range(4):
            emit_transcode(7)
            nkc = (qS + 1) * 4
            m_exp = {}
            m_mm = {}
            m_ylast = {}

            def y_pair(g2, hi, nkc=nkc):
                h = 2 * hp + hi
                attx = attT_bufs[hi]
                gpar = g2 % 2
                yb = 4 + hi
                S.wait("tensor", "ss", m_exp[(g2, hi)])
                if g2 == 0:
                    S.bank_wait("tensor", yb)
                my = None
                for i in range(2):
                    kc = g2 * 2 + i
                    my = S.op("tensor", "matmul", pf[yb][0:65, :],
                              lhsT=V[:, kc, 65 * h:65 * h + 65],
                              rhs=attx[:, gpar, i * 512:(i + 1) * 512],
                              start=(kc == 0), stop=(kc == nkc - 1),
                              skip_group_check=True,
                              inc="sp" if i == 1 else None)
                S.slot_set(("attT", hi, gpar), "sp", my)
                m_ylast[hi] = my

            for g2 in range(nkc // 2):
                for i in range(2):
                    kc = g2 * 2 + i
                    diag = kc - 4 * qS
                    if i == 0:
                        S.bank_wait("tensor", 0)
                        S.op("tensor", "matmul", pf[0][:, :],
                             lhsT=ident_bf[:], rhs=umask[:, 0, :],
                             start=True, stop=True)
                    for hi in range(2):
                        off = 64 * hi
                        b = hi * 2 + i
                        if not (i == 0 and hi == 0):
                            S.bank_wait("tensor", b)
                        S.op("tensor", "matmul", pf[b][:, :],
                             lhsT=KT[off:off + 64, hp,
                                     kc * 128:(kc + 1) * 128],
                             rhs=QT[off:off + 64, hp,
                                    qS * 512:(qS + 1) * 512],
                             start=True, stop=(diag < 0))
                        if diag >= 0:
                            S.op("tensor", "matmul", pf[b][:, :],
                                 lhsT=ident_bf[:], rhs=umask[:, diag, :],
                                 start=False, stop=True)
                        last = S.prog["tensor"][-1]
                        S.cnt["sp"] = S.cnt.get("sp", 0) + 1
                        m_mm[hi] = S.cnt["sp"]
                        S.prog["tensor"][-1] = (last[0], last[1], last[2],
                                                ("sp", 1))
                if g2 > 0:
                    y_pair(g2 - 1, 0)
                    y_pair(g2 - 1, 1)
                gpar = g2 % 2
                for hi in range(2):
                    attx = attT_bufs[hi]
                    S.slot_wait("scalar", ("attT", hi, gpar))
                    if hp == 0 and qS == 0 and g2 == 0 and hi == 0:
                        S.wait("scalar", "sq", m_consts)
                    S.wait("scalar", "sp", m_mm[hi])
                    mex = S.op("scalar", "activation", attx[:, gpar, 0:1024],
                               pS[:, hi * 1024:hi * 1024 + 1024], AF.Exp,
                               bias=cshift[:, 0:1], inc="ss")
                    m_exp[(g2, hi)] = mex
                    S.bank_free(hi * 2, "ss", mex)
                    S.bank_free(hi * 2 + 1, "ss", mex)
            y_pair(nkc // 2 - 1, 0)
            y_pair(nkc // 2 - 1, 1)
            for hi in range(2):
                h = 2 * hp + hi
                yb = 4 + hi
                bb = 6 + hi
                S.wait("scalar", "sp", m_ylast[hi])
                S.slot_wait("scalar", ("rzs",))
                S.op("scalar", "activation", rzs[64:65, 0, :],
                     pf[yb][64:65, :], AF.Ln)
                S.edge("scalar", "ss")
                mrz = S.op("scalar", "activation", rzs[64:65, 0, :],
                           rzs[64:65, 0, :], AF.Exp, scale=-1.0, inc="ss")
                S.wait("tensor", "ss", mrz)
                S.bank_wait("tensor", bb)
                mbb = S.op("tensor", "matmul", pf[bb][0:64, :],
                           lhsT=ones1[64:65, 0:64], rhs=rzs[64:65, 0, :],
                           start=True, stop=True, inc="sp")
                S.slot_set(("rzs",), "sp", mbb)
                S.wait("vector", "sp", mbb)
                S.slot_wait("vector", ("bsb",))
                mbe = S.op("vector", "tensor_copy", bsb[0:64, 0, :],
                           pf[bb][0:64, :], inc="sv")
                S.bank_free(bb, "sv", mbe)
                S.wait("vector", "sv", mbe)
                if hi == 0:
                    mtt = S.op("vector", "tensor_mul",
                               A5[0:64, hp, qS * 512:(qS + 1) * 512],
                               pf[yb][0:64, :], bsb[0:64, 0, :], inc="sv")
                else:
                    S.slot_wait("vector", ("ytmp", qS % 2))
                    mtt = S.op("vector", "tensor_mul", ytmp[0:64, qS % 2, :],
                               pf[yb][0:64, :], bsb[0:64, 0, :], inc="sv")
                    S.wait("sync", "sv", mtt)
                    m_ydma = S.dma("sync", A5[64:128, hp,
                                              qS * 512:(qS + 1) * 512],
                                   ytmp[0:64, qS % 2, :], f"yd{qS % 2}")
                    S.slot_set(("ytmp", qS % 2), f"yd{qS % 2}", m_ydma)
                S.slot_set(("bsb",), "sv", mtt)
                S.bank_free(yb, "sv", mtt)
    emit_transcode(len(tc_jobs))
    m_yT_done = S.cnt["sv"]
    m_yT_dma = (S.cnt.get("yd0", 0), S.cnt.get("yd1", 0))

    es_2.close()
    es_wtb.close()
    es_qkv.close()
    es_oac = ExitStack()
    oac = es_oac.enter_context(nc.sbuf_tensor("oac", [128, NTB, 1024], F32))
    wcb = es_oac.enter_context(nc.sbuf_tensor("wcb", [128, NCT, C], BF16))
    xres = es_oac.enter_context(nc.sbuf_tensor("xres", [128, 2, 1024], F32))
    wgb = es_oac.enter_context(nc.sbuf_tensor("wgb", [128, NCT, 8], BF16))

    m_pe_a2 = S.cnt["sp"]
    S.wait("gpsimd", "sp", m_pe_a2)
    S.wait("gpsimd", "ss", S.cnt["ss"])
    S.wait("gpsimd", "sv", m_yT_done)
    m_wcb = S.dma("gpsimd", wcb[:],
                  d["wc"].rearrange("(ct p) f -> p ct f", p=128), "sc")
    m_wgb = S.dma("gpsimd", wgb[:],
                  d["wgT"].rearrange("(ct p) e -> p ct e", p=128), "sm")

    m_ln = {}
    m_x1T = {}
    S.wait("tensor", "sc", m_wcb)
    S.wait("tensor", "sv", m_yT_done)
    S.wait("tensor", "yd0", m_yT_dma[0])
    S.wait("tensor", "yd1", m_yT_dma[1])
    S.wait("sync", "sp", m_pe_a2)
    S.wait("sync", "ss", S.cnt["ss"])
    S.wait("sync", "sv", m_yT_done)
    S.wait("sync", "yd0", m_yT_dma[0])
    S.wait("sync", "yd1", m_yT_dma[1])
    for _s in ("tw0", "tw1", "tw2", "tw3"):
        S.wait("sync", _s, S.cnt.get(_s, 0))
    for tb in range(NTB):
        S.slot_wait("sync", ("xres", tb % 2))
        m_xr = S.dma("sync", xres[:, tb % 2, :],
                     d["x"][tb * 128:(tb + 1) * 128, :], f"sx{tb % 2}")
        mWc = {}
        for fh in range(2):
            b = 2 * (tb % 2) + fh
            S.bank_wait("tensor", b)
            mm = None
            for ct in range(NCT):
                mm = S.op("tensor", "matmul", pf[b][:, :],
                          lhsT=A5[:, ct, tb * 128:(tb + 1) * 128],
                          rhs=wcb[:, ct, fh * 512:(fh + 1) * 512],
                          start=(ct == 0), stop=(ct == NCT - 1),
                          inc="sp" if ct == NCT - 1 else None)
            mWc[fh] = mm
        S.wait("vector", "sp", mWc[1])
        S.wait("vector", f"sx{tb % 2}", m_xr)
        S.op("vector", "tensor_add", rbuf[:, 0:512],
             pf[2 * (tb % 2)][:, :], xres[:, tb % 2, 0:512])
        mpr = S.op("vector", "tensor_add", rbuf[:, 512:1024],
                   pf[2 * (tb % 2) + 1][:, :], xres[:, tb % 2, 512:1024],
                   inc="sv")
        S.bank_free(2 * (tb % 2), "sv", mpr)
        S.bank_free(2 * (tb % 2) + 1, "sv", mpr)
        S.slot_set(("xres", tb % 2), "sv", mpr)
        S.wait("vector", "sv", mpr)
        S.op("vector", "reduce_sum", lnm[:], rbuf[:], axis=AX, negate=True)
        S.edge("vector", "sv")
        S.op("vector", "tensor_scalar_mul", lnm[:], lnm[:], 1.0 / 1024.0)
        S.edge("vector", "sv")
        S.op("vector", "tensor_scalar_add", cbuf[:], rbuf[:], lnm[:, 0:1])
        S.edge("vector", "sv")
        S.op("vector", "scalar_tensor_tensor", rbuf[:], in0=cbuf[:], scalar=1.0,
             in1=cbuf[:], op0=ALU.mult, op1=ALU.mult, accum_out=lnv[:, 0:1])
        S.edge("vector", "sv")
        mvv = S.op("vector", "tensor_scalar", lnv[:], lnv[:], 1.0 / 1024.0, EPS,
                   op0=ALU.mult, op1=ALU.add, inc="sv")
        S.wait("scalar", "sv", mvv)
        S.op("scalar", "activation", lnv[:], lnv[:], AF.Ln)
        S.edge("scalar", "ss")
        S.op("scalar", "activation", lnr[:], lnv[:], AF.Exp, scale=-0.5)
        mrstd = S.edge("scalar", "ss")
        S.wait("vector", "ss", mrstd)
        S.op("vector", "tensor_scalar_mul", oac[:, tb, :], cbuf[:], lnr[:, 0:1])
        mx1b = S.op("vector", "tensor_scalar_mul", x1bf[:], cbuf[:],
                    lnr[:, 0:1], inc="sv")
        m_ln[tb] = mx1b
        S.wait("tensor", "sv", mx1b)
        for half in range(2):
            b = 6 + half
            S.bank_wait("tensor", b)
            mt = None
            for ci in range(4):
                ct = half * 4 + ci
                mt = S.op("tensor", "matmul", pf[b][:, ci * 128:(ci + 1) * 128],
                          lhsT=x1bf[:, ct * 128:(ct + 1) * 128],
                          rhs=ident_bf[:], start=True, stop=True,
                          inc="sp" if ci == 3 else None)
            S.wait("vector", "sp", mt)
            mev = S.op("vector", "tensor_copy",
                       A1[:, half * 4:half * 4 + 4, tb * 128:(tb + 1) * 128],
                       pf[b][:, :].rearrange("p (c q) -> p c q", c=4),
                       inc="sv")
            S.bank_free(b, "sv", mev)
        m_x1T[tb] = mev

    m_g = {}
    S.wait("tensor", "sm", m_wgb)
    for tb in range(NTB):
        S.wait("tensor", "sv", m_x1T[tb])
        S.bank_wait("tensor", 5)
        mm = None
        for ct in range(NCT):
            mm = S.op("tensor", "matmul", pf[5][:, 0:8],
                      lhsT=A1[:, ct, tb * 128:(tb + 1) * 128],
                      rhs=wgb[:, ct, :], start=(ct == 0), stop=(ct == NCT - 1),
                      inc="sp" if ct == NCT - 1 else None)
        S.wait("vector", "sp", mm)
        if tb > 0:
            S.wait("vector", "ss", m_sl_free)
        mev = S.op("vector", "tensor_copy", sl[:], pf[5][:, 0:8], inc="sv")
        S.bank_free(5, "sv", mev)
        S.wait("vector", "sv", mev)
        S.op("vector", "reduce_max", gm1[:], sl[:], axis=AX)
        S.edge("vector", "sv")
        S.op("vector", "tensor_scalar_mul", gm1n[:], gm1[:], -1.0)
        S.op("vector", "tensor_scalar", geq[:], sl[:], gm1[:, 0:1], None,
             op0=ALU.is_ge)
        S.edge("vector", "sv")
        S.op("vector", "scalar_tensor_tensor", geq[:], in0=geq[:], scalar=-1e30,
             in1=sl[:], op0=ALU.mult, op1=ALU.add)
        S.edge("vector", "sv")
        S.op("vector", "reduce_max", gm2[:], geq[:], axis=AX)
        mgv = S.edge("vector", "sv")
        S.wait("scalar", "sv", mgv)
        msx = S.op("scalar", "activation", gwex[:], sl[:], AF.Exp,
                   bias=gm1n[:, 0:1], inc="ss")
        m_sl_free = msx
        S.op("vector", "tensor_scalar", gkeep[:], sl[:], gm2[:, 0:1], None,
             op0=ALU.is_ge)
        S.edge("vector", "sv")
        S.wait("vector", "ss", msx)
        S.op("vector", "tensor_mul", gkeep[:], gkeep[:], gwex[:])
        S.edge("vector", "sv")
        S.op("vector", "reduce_sum", gz[:], gkeep[:], axis=AX)
        S.edge("vector", "sv")
        S.op("vector", "reciprocal", gz[:], gz[:])
        S.edge("vector", "sv")
        m_g[tb] = S.op("vector", "tensor_scalar_mul", g_all[:, tb, :],
                       gkeep[:], gz[:, 0:1], inc="sv")

    es_lat.close()
    es_v.close()
    es_f = ExitStack()
    gTrow = es_f.enter_context(nc.sbuf_tensor("gTrow", [8, NTB, 128], BF16, side="right"))
    gRow = es_f.enter_context(nc.sbuf_tensor("gRow", [1, 2, 512], BF16, side="right"))
    hiddenG = es_f.enter_context(nc.sbuf_tensor("hiddenG", [128, NHT, 512], BF16, side="right"))
    w12s = es_f.enter_context(nc.sbuf_tensor("w12s", [128, 4, NCT, 2, 128], BF16, side="right"))
    w3s = es_f.enter_context(nc.sbuf_tensor("w3s", [128, 2, 1024], BF16, side="right"))
    x1te = es_f.enter_context(nc.sbuf_tensor("x1te", [128, 2, NCT, 512], BF16))
    gBt = es_f.enter_context(nc.sbuf_tensor("gBt", [128, 2, 512], BF16, side="right"))
    s1b = es_f.enter_context(nc.sbuf_tensor("s1b", [128, 2, 512], BF16, side="right"))
    outb = es_f.enter_context(nc.sbuf_tensor("outb", [128, 1, 1024], F32))

    for tb in range(NTB):
        S.wait("tensor", "sv", m_g[tb])
        S.bank_wait("tensor", 4)
        mt = S.op("tensor", "matmul", pf[4][0:8, 0:128],
                  lhsT=g_all[:, tb, :], rhs=ident_f[:], start=True, stop=True,
                  inc="sp")
        S.wait("vector", "sp", mt)
        mev = S.op("vector", "tensor_copy", gTrow[0:8, tb, :],
                   pf[4][0:8, 0:128], inc="sv")
        S.bank_free(4, "sv", mev)
    m_gT = S.cnt["sv"]
    m_pe_a3 = S.cnt["sp"]

    m_w12 = {}
    m_w3 = {}
    m_wb = {}
    m_x1te = {}
    m_oadd = {}

    first_ffn_dma = [True]
    for _q in ("sync", "gpsimd"):
        for _s in ("tw0", "tw1", "tw2", "tw3"):
            S.wait(_q, _s, S.cnt.get(_s, 0))
    S.wait("sync", "sp", m_pe_a3)
    S.wait("gpsimd", "sp", m_pe_a3)

    def w12_load(e, ts, ht):
        buf = ht % 4
        S.slot_wait("sync", ("w12s", buf))
        srcb = d["w12b"][e, :, 2 * ht:2 * ht + 2, :].rearrange(
            "(ct p) m h -> p ct (m h)", p=128)
        dstb = w12s[:, buf].rearrange("p ct m h -> p ct (m h)")
        m_w12[(e, ts, ht)] = S.dma("sync", dstb, srcb, f"rl{buf}")

    def w3_load(e, ts, ht):
        buf = ht % 2
        S.slot_wait("gpsimd", ("w3s", buf))
        m_w3[(e, ts, ht)] = S.dma(
            "gpsimd", w3s[:, buf, :],
            d["w3b"][e, ht * 128:(ht + 1) * 128, :], f"u{buf}")

    def prep_x1te(e, ts):
        par = ts % 2
        b = 4 + par
        if e == 0 and ts == 0:
            S.wait("gpsimd", "sv", m_gT)
        S.slot_wait("gpsimd", ("gRow", par))
        m_grow = S.dma("gpsimd", gRow[0:1, par, :],
                       gTrow[e:e + 1, ts * 4:(ts + 1) * 4, :].rearrange(
                           "r b q -> r (b q)"), f"gr{par}")
        S.wait("tensor", f"gr{par}", m_grow)
        S.bank_wait("tensor", b)
        mgbm = S.op("tensor", "matmul", pf[b][:, :], lhsT=ones_bf[0:1, :],
                    rhs=gRow[0:1, par, :], start=True, stop=True, inc="sp")
        S.slot_set(("gRow", par), "sp", mgbm)
        S.wait("vector", "sp", mgbm)
        S.slot_wait("vector", ("gBt", par))
        mgb = S.op("vector", "tensor_copy", gBt[:, par, :], pf[b][:, :],
                   inc="sv")
        S.bank_free(b, "sv", mgb)
        S.wait("vector", "sv", mgb)
        S.slot_wait("vector", ("x1te", par))
        mte = None
        for ct in range(NCT):
            mte = S.op("vector", "tensor_mul", x1te[:, par, ct, :],
                       A1[:, ct, ts * 512:(ts + 1) * 512], gBt[:, par, :],
                       inc="sv" if ct == NCT - 1 else None)
        S.slot_set(("gBt", par), "sv", mte)
        m_x1te[(e, ts)] = mte

    for e in range(NE):
        for ts in range(4):
            if ts == 0:
                prep_x1te(e, 0)
            w3_load(e, ts, 0)
            w3_load(e, ts, 1)
            mhg = {}
            for ht in range(NHT):
                buf = ht % 4
                w12_load(e, ts, ht)
                S.wait("tensor", f"rl{buf}", m_w12[(e, ts, ht)])
                if ht == 0:
                    S.wait("tensor", "sv", m_x1te[(e, ts)])
                b1 = buf
                b2 = 2 + buf
                S.bank_wait("tensor", b1)
                mm1 = None
                for ct in range(NCT):
                    mm1 = S.op("tensor", "matmul", pf[b1][:, :],
                               lhsT=w12s[:, buf, ct, 0, :],
                               rhs=A1[:, ct, ts * 512:(ts + 1) * 512],
                               start=(ct == 0), stop=(ct == NCT - 1),
                               inc="sp" if ct == NCT - 1 else None)
                S.bank_wait("tensor", b2)
                mm2 = None
                for ct in range(NCT):
                    mm2 = S.op("tensor", "matmul", pf[b2][:, :],
                               lhsT=w12s[:, buf, ct, 1, :],
                               rhs=x1te[:, ts % 2, ct, :],
                               start=(ct == 0), stop=(ct == NCT - 1),
                               inc="sp" if ct == NCT - 1 else None)
                S.slot_set(("w12s", buf), "sp", mm2)
                if ht == NHT - 1:
                    m_h2_done = mm2
                S.wait("scalar", "sp", mm1)
                S.slot_wait("scalar", ("s1b", ht % 2))
                ms1 = S.op("scalar", "activation", s1b[:, ht % 2, :],
                           pf[b1][:, :], AF.Silu, inc="ss")
                S.bank_free(b1, "ss", ms1)
                S.wait("vector", "ss", ms1)
                S.wait("vector", "sp", mm2)
                mhg[ht] = S.op("vector", "tensor_mul", hiddenG[:, ht, :],
                               s1b[:, ht % 2, :], pf[b2][:, :], inc="sv")
                S.slot_set(("s1b", ht % 2), "sv", mhg[ht])
                S.bank_free(b2, "sv", mhg[ht])
            S.slot_set(("x1te", ts % 2), "sp", m_h2_done)
            if ts < 3:
                prep_x1te(e, ts + 1)
            elif e < NE - 1:
                prep_x1te(e + 1, 0)
            mlast = {}
            for ht in range(NHT):
                buf = ht % 2
                if ht >= 2:
                    pass
                S.wait("tensor", f"u{buf}", m_w3[(e, ts, ht)])
                if ht == 0:
                    S.wait("tensor", "sv", mhg[NHT - 1])
                    for b in range(8):
                        S.bank_wait("tensor", b)
                for tbq in range(4):
                    for dhh in range(2):
                        b = tbq * 2 + dhh
                        mm = S.op("tensor", "matmul", pf[b][:, :],
                                  lhsT=hiddenG[:, ht, tbq * 128:(tbq + 1) * 128],
                                  rhs=w3s[:, buf, dhh * 512:(dhh + 1) * 512],
                                  start=(ht == 0), stop=(ht == NHT - 1),
                                  skip_group_check=True,
                                  inc="sp" if (tbq == 3 and dhh == 1) else None)
                S.slot_set(("w3s", buf), "sp", mm)
                if ht + 2 < NHT:
                    w3_load(e, ts, ht + 2)
            m_w3_all = S.cnt["sp"]
            mev = None
            for tbq in range(4):
                for dhh in range(2):
                    b = tbq * 2 + dhh
                    S.wait("vector", "sp", m_w3_all)
                    mev = S.op("vector", "tensor_add",
                               oac[:, ts * 4 + tbq, dhh * 512:(dhh + 1) * 512],
                               pf[b][:, :],
                               oac[:, ts * 4 + tbq, dhh * 512:(dhh + 1) * 512],
                               inc="sv")
                    S.bank_free(b, "sv", mev)
            m_oadd[(e, ts)] = mev
    m_ffn_done = S.cnt["sv"]

    for tb in range(NTB):
        if tb == 0:
            S.wait("vector", "sv", m_ffn_done)
        S.slot_wait("vector", ("outb",))
        S.op("vector", "reduce_sum", lnm[:], oac[:, tb, :], axis=AX, negate=True)
        S.edge("vector", "sv")
        S.op("vector", "tensor_scalar_mul", lnm[:], lnm[:], 1.0 / 1024.0)
        S.edge("vector", "sv")
        S.op("vector", "tensor_scalar_add", cbuf[:], oac[:, tb, :], lnm[:, 0:1])
        S.edge("vector", "sv")
        S.op("vector", "scalar_tensor_tensor", rbuf[:], in0=cbuf[:], scalar=1.0,
             in1=cbuf[:], op0=ALU.mult, op1=ALU.mult, accum_out=lnv[:, 0:1])
        S.edge("vector", "sv")
        mvv = S.op("vector", "tensor_scalar", lnv[:], lnv[:], 1.0 / 1024.0, EPS,
                   op0=ALU.mult, op1=ALU.add, inc="sv")
        S.wait("scalar", "sv", mvv)
        S.op("scalar", "activation", lnv[:], lnv[:], AF.Ln)
        S.edge("scalar", "ss")
        S.op("scalar", "activation", lnr[:], lnv[:], AF.Exp, scale=-0.5)
        mrstd = S.edge("scalar", "ss")
        S.wait("vector", "ss", mrstd)
        mo = S.op("vector", "tensor_scalar_mul", outb[:, 0, :], cbuf[:],
                  lnr[:, 0:1], inc="sv")
        S.wait("sync", "sv", mo)
        mdma = S.dma("sync", d["out"][tb * 128:(tb + 1) * 128, :],
                     outb[:, 0, :], f"so{tb % 2}")
        S.slot_set(("outb",), f"so{tb % 2}", mdma)
    S.wait("sync", "so0", S.cnt["so0"])
    S.wait("sync", "so1", S.cnt["so1"])

    with nc.Block() as block:
        def _mk(eng_name):
            def runner(engine):
                for meth, a, k, incv in S.prog[eng_name]:
                    if meth == "wait_ge":
                        engine.wait_ge(sems[a[0]], a[1])
                        continue
                    inst = getattr(engine, meth)(*a, **k)
                    if incv is not None:
                        inst.then_inc(sems[incv[0]], incv[1])
            return runner
        block.sync(_mk("sync"))
        block.gpsimd(_mk("gpsimd"))
        block.tensor(_mk("tensor"))
        block.vector(_mk("vector"))
        block.scalar(_mk("scalar"))

    es_f.close()
    es_oac.close()
    es_all.close()
    return nc


_CACHE = {}


def kernel(**inputs):
    def A(name):
        return np.ascontiguousarray(np.asarray(inputs[name], dtype=np.float32))

    x = A("L0_attn_x")
    wq = A("L0_attn_Wq_down")
    wk = A("L0_attn_Wk_down")
    wv = A("L0_attn_Wv_down")
    wc = A("L0_attn_Wc")
    wupqc = A("L0_attn_Wq_up_c")
    wupqe = A("L0_attn_Wq_up_e")
    wupkc = A("L0_attn_Wk_up_c")
    wupke = A("L0_attn_Wk_up_e")
    wupv = A("L0_attn_Wv_up")
    wg = A("L0_ffn_Wg")
    w1 = A("L0_ffn_W1")
    w2 = A("L0_ffn_W2")
    w3 = A("L0_ffn_W3")

    wgT = np.ascontiguousarray(wg.T)
    w1t = np.zeros((NE, C, HIDP), np.float32)
    w1t[:, :, :2730] = w1.transpose(0, 2, 1)
    w2t = np.zeros((NE, C, HIDP), np.float32)
    w2t[:, :, :2730] = w2.transpose(0, 2, 1)
    w12i = np.zeros((NE, C, 2 * NHT, 128), np.float32)
    w12i[:, :, 0::2, :] = w1t.reshape(NE, C, NHT, 128)
    w12i[:, :, 1::2, :] = w2t.reshape(NE, C, NHT, 128)
    w12i = np.ascontiguousarray(w12i)
    w3p = np.zeros((NE, HIDP, C), np.float32)
    w3p[:, :2730, :] = w3.transpose(0, 2, 1)
    w3p = np.ascontiguousarray(w3p)
    cos, sin, pm = _host_consts()

    if "nc" not in _CACHE:
        _CACHE["nc"] = build()
    nc = _CACHE["nc"]

    shared = dict(wq=wq, wk=wk, wv=wv, wc=wc, wupqc=wupqc, wupqe=wupqe,
                  wupkc=wupkc, wupke=wupke, wupv=wupv, wgT=wgT,
                  w12i=w12i, w3p=w3p, cos=cos, sin=sin, pmat=pm)
    in_maps = []
    for b in range(8):
        m = dict(shared)
        m["x"] = np.ascontiguousarray(x[b])
        m["xT"] = np.ascontiguousarray(x[b].T)
        in_maps.append(m)

    res = run_bass_kernel_spmd(nc, in_maps, core_ids=list(range(8)))
    return np.stack([r["out"] for r in res.results], axis=0)
